# version 09
# baseline (speedup 1.0000x reference)
import numpy as np
import ml_dtypes
import concourse.bass as bass
import concourse.mybir as mybir
from concourse.bass_utils import run_bass_kernel_spmd
from concourse.library_config import mlp as _MLP_LIB

F32 = mybir.dt.float32
BF16 = mybir.dt.bfloat16
I16 = mybir.dt.int16
AF = mybir.ActivationFunctionType
ALU = mybir.AluOpType
AX = mybir.AxisListType.X
BF = ml_dtypes.bfloat16

NCORES = 8
TLOC = 2048
TGLB = NCORES * TLOC
D = 1024
H = 8
CH = 64
NCH = TLOC // CH
NTT = TLOC // 128
E = 64
TOPK = 8
CAP = 512
EO = 8
NSLOT = EO * CAP
F = 768
SF = 768
RTW = 64
EPS_LN = 1e-5

_CACHE = {}
DBG = False
_MARKS = {}


def _host_inputs(core, inp):
    xf = inp["L0_attn_x"].reshape(TGLB, D)
    m = {
        "x_loc": np.ascontiguousarray(xf[core * TLOC:(core + 1) * TLOC]),
        "wq": inp["L0_attn_Wq"], "wk": inp["L0_attn_Wk"], "wv": inp["L0_attn_Wv"],
        "wg": inp["L0_attn_Wg"], "wo": inp["L0_attn_Wo"],
        "rwt": np.ascontiguousarray(inp["L0_ffn_router_w"].T),
        "wsgu": inp["L0_ffn_ws_gate_up"], "wsd": inp["L0_ffn_ws_down"],
        "weg": np.ascontiguousarray(inp["L0_ffn_w_gate"][core * EO:(core + 1) * EO]),
        "weu": np.ascontiguousarray(inp["L0_ffn_w_up"][core * EO:(core + 1) * EO]),
        "wed": np.ascontiguousarray(inp["L0_ffn_w_down"][core * EO:(core + 1) * EO]),
    }
    m["ident32"] = np.eye(128, dtype=np.float32)
    m["ident16"] = np.eye(128, dtype=BF)
    tri = np.zeros((128, 129), np.float32)
    tri[:, :128] = np.triu(np.ones((128, 128), np.float32), 1)
    tri[:, 128] = 1.0
    m["trib"] = tri.astype(BF)
    m["ones128b"] = np.ones((128, 128), dtype=BF)
    m["ones128f"] = np.ones((128, 128), dtype=np.float32)
    ms = np.ones((128, TLOC), np.float32)
    ms[:, ::CH] = 0.0
    m["mask_scan"] = ms.astype(BF)
    mA = np.triu(np.ones((CH, CH), np.float32), 0)
    m["maskA"] = np.concatenate([mA, mA], axis=0).copy()
    m["iota_e"] = np.tile(np.arange(E, dtype=np.float32)[None, :], (128, 1))
    ev = np.arange(E)
    gv = ((ev % 8) // 2) * 16 + (ev // 8) * 2 + (ev % 8) % 2
    m["giota_e"] = np.tile(gv.astype(np.float32)[None, :], (128, 1))
    m["msel"] = np.tile((np.arange(8) < core).astype(np.float32)[None, :], (128, 1))
    cb = (np.arange(128) % 16)[:, None]
    m["ctens"] = (cb * 32 + np.arange(32)[None, :]).astype(np.float32)
    m["ecol"] = (EO * core + np.arange(128) // 16).astype(np.float32)[:, None].copy()
    gs = np.zeros((128, 64), np.int16)
    for i in range(1024):
        blk, p = i // 128, i % 128
        gs[i % 16, i // 16] = blk * 64 + EO * core + p // 16
    gs[16:32] = gs[0:16]
    m["gsidx"] = gs
    return m


def _build(cutoff=None):
    nc = bass.Bass("TRN2", target_bir_lowering=False, num_swdge_queues=1)

    def din(name, shape, dt=F32):
        return nc.dram_tensor(name, list(shape), dt, kind="ExternalInput").ap()

    x_loc = din("x_loc", (TLOC, D))
    wq = din("wq", (D, D)); wk = din("wk", (D, D)); wv = din("wv", (D, D))
    wg = din("wg", (D, D)); wo = din("wo", (D, D))
    rwt = din("rwt", (D, E))
    wsgu = din("wsgu", (D, 2 * SF)); wsd = din("wsd", (SF, D))
    weg = din("weg", (EO, D, F)); weu = din("weu", (EO, D, F)); wed = din("wed", (EO, F, D))
    ident32 = din("ident32", (128, 128))
    ident16 = din("ident16", (128, 128), BF16)
    trib = din("trib", (128, 129), BF16)
    ones128b = din("ones128b", (128, 128), BF16)
    ones128f = din("ones128f", (128, 128))
    mask_scan = din("mask_scan", (128, TLOC), BF16)
    maskA = din("maskA", (128, CH))
    iota_e = din("iota_e", (128, E))
    giota_e = din("giota_e", (128, E))
    msel = din("msel", (128, 8))
    ctens = din("ctens", (128, 32))
    ecol = din("ecol", (128, 1))
    gsidx = din("gsidx", (128, 64), I16)

    out = nc.dram_tensor("out", [TLOC, D], F32, kind="ExternalOutput").ap()

    def dint(name, shape, dt=F32):
        return nc.dram_tensor(name, list(shape), dt).ap()

    d_xT32 = dint("d_xT32", (128, 8 * TLOC))
    d_st_in = dint("d_st_in", (H, 129, 128))
    d_st_out = nc.dram_tensor("d_st_out", [NCORES * H, 129, 128], F32, addr_space="Shared").ap()
    d_cnt_in = dint("d_cnt_in", (1, E))
    d_cnt_out = nc.dram_tensor("d_cnt_out", [NCORES, E], F32, addr_space="Shared").ap()
    d_gst_in = dint("d_gst_in", (E, 64))
    d_gst_out = nc.dram_tensor("d_gst_out", [NCORES * E, 64], F32, addr_space="Shared").ap()
    d_gtab_in = dint("d_gtab_in", (NTT * E, 128))
    d_gtab_out = nc.dram_tensor("d_gtab_out", [128 * E, 128], F32, addr_space="Shared").ap()
    d_xmid_in = dint("d_xmid_in", (TLOC, D), BF16)
    d_xmid_out = nc.dram_tensor("d_xmid_out", [TGLB, D], BF16, addr_space="Shared").ap()
    d_rtab_in = dint("d_rtab_in", (TLOC, RTW))
    d_rtab_out = nc.dram_tensor("d_rtab_out", [TGLB, RTW], F32, addr_space="Shared").ap()
    d_ybuf_in = dint("d_ybuf_in", (NSLOT, D), BF16)
    d_ybuf_out = nc.dram_tensor("d_ybuf_out", [NCORES * NSLOT, D], BF16, addr_space="Shared").ap()
    d_scri = dint("d_scri", (128, 256), I16)
    d_scrf = dint("d_scrf", (128, 64))
    dbg = {}
    if DBG:
        def dout(name, shape, dt=F32):
            return nc.dram_tensor(name, list(shape), dt, kind="ExternalOutput").ap()
        dbg["xT32"] = dout("dbg_xT32", (128, TLOC))
        dbg["st"] = dout("dbg_st", (H * 129, 128))
        dbg["xmid"] = dout("dbg_xmid", (256, D), BF16)
        dbg["cnt"] = dout("dbg_cnt", (NCORES, E))
        dbg["gtab"] = dout("dbg_gtab", (2 * E, 128))
        dbg["gst"] = dout("dbg_gst", (E, 64))
        dbg["rtab"] = dout("dbg_rtab", (128, RTW))
        dbg["ybuf"] = dout("dbg_ybuf", (128, D), BF16)
        dbg["fin"] = dout("dbg_fin", (128, TLOC))
        dbg["slot"] = dout("dbg_slot", (128, NTT * 8))
        dbg["stok"] = dout("dbg_stok", (128, 32))
        dbg["wec"] = dout("dbg_wec", (128, 32))
        dbg["w8"] = dout("dbg_w8", (128, NTT * 8))
        dbg["we"] = dout("dbg_we", (128, 32))
        dbg["rt"] = dout("dbg_rt", (128, 32 * RTW))
        dbg["val8"] = dout("dbg_val8", (128, NTT * 8))
        dbg["wn8"] = dout("dbg_wn8", (128, NTT * 8))

    from contextlib import ExitStack
    ctx = ExitStack()
    with ctx:
        def sbuf(name, shape, dt=F32, stack=None):
            cm = nc.sbuf_tensor(name, list(shape), dt)
            return (stack or ctx).enter_context(cm)

        def psum(name, shape, dt=F32):
            return ctx.enter_context(nc.psum_tensor(name, list(shape), dt))

        s_xmid = sbuf("s_xmid", (128, 8 * TLOC), F32)
        s_id32 = sbuf("s_id32", (128, 128))
        s_id16 = sbuf("s_id16", (128, 128), BF16)
        s_trib = sbuf("s_trib", (128, 129), BF16)
        s_ones128b = sbuf("s_ones128b", (128, 128), BF16)
        s_ones128f = sbuf("s_ones128f", (128, 128))
        s_maskA = sbuf("s_maskA", (128, CH))
        s_iotae = sbuf("s_iotae", (128, E))
        s_giotae = sbuf("s_giotae", (128, E))
        s_g8 = sbuf("s_g8", (128, NTT * 8))
        s_msel = sbuf("s_msel", (128, 8))
        s_ctens = sbuf("s_ctens", (128, 32))
        s_ecol = sbuf("s_ecol", (128, 1))
        s_gsidx = sbuf("s_gsidx", (128, 64), I16)
        s_rowsc = sbuf("s_rowsc", (128, 2 * D))
        s_f32a = sbuf("s_f32a", (128, 1024))
        s_f32b = sbuf("s_f32b", (128, 1024))
        s_w8 = sbuf("s_w8", (128, NTT * 8))
        s_e8 = sbuf("s_e8", (128, NTT * 8))
        s_pos8 = sbuf("s_pos8", (128, NTT * 8))
        s_val8 = sbuf("s_val8", (128, NTT * 8))
        s_wn8 = sbuf("s_wn8", (128, NTT * 8))
        s_slot8 = sbuf("s_slot8", (128, NTT * 8))
        s_slot8i = sbuf("s_slot8i", (128, NTT * 8), I16)
        s_ninv = sbuf("s_ninv", (128, NTT))
        s_ninvb = sbuf("s_ninvb", (128, NTT), BF16)
        s_ninvrow = sbuf("s_ninvrow", (1, TLOC), BF16)
        s_yidx = sbuf("s_yidx", (128, TLOC // 2), I16)
        s_sidx = sbuf("s_sidx", (128, 256), I16)
        s_sidx2 = sbuf("s_sidx2", (128, 256), I16)
        s_gidx = sbuf("s_gidx", (128, 256), I16)
        s_kstar = sbuf("s_kstar", (128, 32))
        s_kstari = sbuf("s_kstari", (128, 32), I16)
        s_stok = sbuf("s_stok", (128, 32))
        s_stoki = sbuf("s_stoki", (128, 32), I16)
        s_we = sbuf("s_we", (128, 32))
        s_wec = sbuf("s_wec", (128, 32))
        s_totc = sbuf("s_totc", (128, 1))
        s_base = sbuf("s_base", (128, E))
        s_cnt_all = sbuf("s_cnt_all", (NCORES, E))
        s_gstc = sbuf("s_gstc", (64, 17))
        s_bgl = sbuf("s_bgl", (64, 1))
        s_y0 = sbuf("s_y0", (1, D), BF16)
        s_colsc = sbuf("s_colsc", (128, 8))
        s_epscol = sbuf("s_epscol", (128, 1))
        s_dcol = sbuf("s_dcol", (128, 8))

        pnames = ["pA", "pB", "pC", "pD", "pS", "pT", "pU", "pV"]
        P = {n: psum(n, (128, 512)) for n in pnames}

        KROT = 3
        semnames = ["MM", "VE", "SC", "CC"]
        for base in ["LD", "ST", "GP", "GL", "GM"]:
            semnames += [f"{base}{i}" for i in range(KROT)]
        sems = {n: ctx.enter_context(nc.semaphore(n)) for n in semnames}
        cnt = {n: 0 for n in sems}
        for base in ["LD", "ST", "GP", "GL", "GM"]:
            cnt[base] = 0
        rot = {"LD": 0, "ST": 0, "GP": 0, "GL": 0, "GM": 0}

        block = ctx.enter_context(nc.Block())

        seqc = [0]

        class _Rec:
            __slots__ = ("prog",)

            def __init__(self):
                self.prog = []

            def __getattr__(self, mname):
                prog = self.prog

                def call(*a, **kw):
                    seqc[0] += 1
                    item = [mname, a, kw, None, seqc[0]]
                    prog.append(item)

                    class _Then:
                        def then_inc(self, sem_, n_):
                            item[3] = (sem_, n_)
                    return _Then()
                return call

        ty, tt, tv, ts, tg = _Rec(), _Rec(), _Rec(), _Rec(), _Rec()

        def mark(name):
            _MARKS[name] = seqc[0]

        def bump(name, n=1):
            cnt[name] += n
            return cnt[name]

        def dma(e_, out_, in_, sem="LD", **kw):
            if e_ is tg and sem == "LD":
                sem = "GL"
            kw.setdefault("allow_slow_non_contiguous", True)
            i = rot[sem]
            rot[sem] = (i + 1) % KROT
            n_ = f"{sem}{i}"
            if cnt[n_] > 0:
                e_.wait_ge(sems[n_], cnt[n_])
            e_.dma_start(out=out_, in_=in_, **kw).then_inc(sems[n_], 16)
            bump(n_, 16)
            bump(sem, 16)
            return cnt[n_]

        bank_free = {}

        def wbank(n):
            if n in bank_free:
                s_, t_ = bank_free[n]
                tt.wait_ge(sems[s_], t_)

        def mm(pn, out_, lhsT, rhs, start=True, stop=True):
            if start:
                wbank(pn)
            tt.matmul(out_, lhsT, rhs, start=start, stop=stop).then_inc(sems["MM"], 1)
            return bump("MM")

        def trans(pn, out_, in_, ident):
            wbank(pn)
            tt.transpose(out_, in_, ident).then_inc(sems["MM"], 1)
            return bump("MM")

        def vop(fn, *a, **kw):
            if cnt["VE"] > 0:
                tv.wait_ge(sems["VE"], cnt["VE"])
            fn(*a, **kw).then_inc(sems["VE"], 1)
            return bump("VE")

        def sop(fn, *a, **kw):
            if cnt["SC"] > 0:
                ts.wait_ge(sems["SC"], cnt["SC"])
            fn(*a, **kw).then_inc(sems["SC"], 1)
            return bump("SC")

        def evict_v(pn, fn, *a, **kw):
            t_ = vop(fn, *a, **kw)
            bank_free[pn] = ("VE", t_)
            return t_

        def evict_s(pn, fn, *a, **kw):
            t_ = sop(fn, *a, **kw)
            bank_free[pn] = ("SC", t_)
            return t_

        DMA_SEMS = ("LD", "ST", "GL", "GP", "GM")

        def w(e_, sem, tick):
            if sem in DMA_SEMS:
                for i in range(KROT):
                    n_ = f"{sem}{i}"
                    if cnt[n_] > 0:
                        e_.wait_ge(sems[n_], cnt[n_])
                return
            if tick > 0:
                e_.wait_ge(sems[sem], tick)

        def _wst(e_):
            w(e_, "ST", 0)

        def _wld(e_):
            w(e_, "LD", 0)

        def gather(out_ap, in_ap, idxs_ap, n_idx):
            i = rot["GP"]
            rot["GP"] = (i + 1) % KROT
            n_ = f"GP{i}"
            if cnt[n_] > 0:
                tg.wait_ge(sems[n_], cnt[n_])
            tg.dma_gather(out_ap=out_ap, in_ap=in_ap, idxs_ap=idxs_ap,
                          num_idxs=n_idx, num_idxs_reg=n_idx,
                          elem_size=in_ap.shape[-1]).then_inc(sems[n_], 16)
            bump(n_, 16)
            bump("GP", 16)
            return cnt[n_]

        def allgather(din_, dout_, tick_st):
            w(tg, "ST", tick_st)
            tg.collective_compute(
                "AllGather", ALU.bypass, replica_groups=[list(range(NCORES))],
                ins=[din_], outs=[dout_]).then_inc(sems["CC"], 1)
            return bump("CC")

        def join_all():
            snap = dict(cnt)
            for e_ in (ty, tt, tv, ts, tg):
                for sname in semnames:
                    if snap[sname] > 0:
                        e_.wait_ge(sems[sname], snap[sname])

        st_xT = ExitStack()
        s_xT = sbuf("s_xT", (128, 8 * TLOC), BF16, stack=st_xT)
        st_osb = ExitStack()
        s_osb = sbuf("s_osb", (128, H * TLOC), BF16, stack=st_osb)
        st_attn = ExitStack()
        s_q = sbuf("s_q", (128, TLOC), stack=st_attn)
        s_k = sbuf("s_k", (128, TLOC), stack=st_attn)
        s_lam = sbuf("s_lam", (128, TLOC), stack=st_attn)
        s_eq = sbuf("s_eq", (128, TLOC), stack=st_attn)
        s_ek = sbuf("s_ek", (128, TLOC), stack=st_attn)
        xbv = s_xmid[:].bitcast(BF16)

        def carve16(b0, cols):
            return xbv[:, b0 // 2:b0 // 2 + cols]

        def carve32(b0, cols):
            return s_xmid[:, b0 // 4:b0 // 4 + cols]

        s_qhat = xbv[:, 0:16384]
        s_qd = carve16(32768, TLOC)
        s_kd = carve16(32768 + 4096, TLOC)
        s_khT = carve16(32768 + 8192, TLOC)
        s_v = carve16(32768 + 12288, TLOC)
        s_mscan = carve16(32768 + 16384, TLOC)
        s_wslice = carve16(32768 + 20480, 1024)
        s_sti = carve32(32768 + 22528, 2 * 128)
        s_state = carve32(32768 + 23552, 128)
        s_lend = carve32(32768 + 24064, NCH)
        s_eend = carve32(32768 + 24192, NCH)
        s_linc = carve32(32768 + 24320, NCH)
        s_small = carve32(32768 + 24448, NCH)
        s_sin = carve32(32768 + 24576, 128)
        s_sinb = carve16(32768 + 25088, 128)
        s_am = carve16(32768 + 25344, 16 * CH)
        s_stateb = carve16(32768 + 29696, 128)
        s_wslice2 = carve16(32768 + 29952, 1024)

        for dst, src in [(s_id32[:], ident32), (s_id16[:], ident16),
                         (s_trib[:], trib), (s_ones128b[:], ones128b), (s_ones128f[:], ones128f),
                         (s_mscan[:], mask_scan), (s_maskA[:], maskA),
                         (s_iotae[:], iota_e), (s_giotae[:], giota_e),
                         (s_msel[:], msel),
                         (s_ctens[:], ctens), (s_ecol[:], ecol),
                         (s_gsidx[:], gsidx)]:
            dma(ty, dst, src)
        t_const = cnt["LD"]
        vop(tv.memset, s_epscol[:], EPS_LN)
        vop(tv.memset, s_yidx[:], 0)
        vop(tv.memset, s_sidx[:], 0)
        vop(tv.memset, s_sidx2[:], 0)
        vop(tv.memset, s_gidx[:], 0)
        tg.load_library(_MLP_LIB)

        for tti in range(NTT):
            buf = s_rowsc[:, 0:D]
            stg = s_rowsc[:, D:2 * D]
            lt = dma(ty, buf, x_loc[tti * 128:(tti + 1) * 128, :])
            w(tv, "ST", cnt["ST"])
            for dc in range(8):
                pn = "pT" if dc % 2 == 0 else "pU"
                ps = P[pn][:, 0:128]
                w(tt, "LD", lt)
                trans(pn, ps, buf[:, dc * 128:(dc + 1) * 128], s_id32[:])
                tk = cnt["MM"]
                w(tv, "MM", tk)
                vop(tv.tensor_copy, out=s_xT[:, dc * TLOC + tti * 128:
                                             dc * TLOC + (tti + 1) * 128], in_=ps)
                evict_v(pn, tv.tensor_copy, out=stg[:, dc * 128:(dc + 1) * 128], in_=ps)
            tke = cnt["VE"]
            w(ty, "VE", tke)
            dma(ty, d_xT32.rearrange("p (c t) -> p c t", t=TLOC)[:, :, tti * 128:(tti + 1) * 128],
                stg.rearrange("p (c q) -> p c q", q=128), sem="ST")
            _wst(ty)
        t_xT_all = cnt["VE"]

        mark("P0")
        wseq = []
        for hh in range(H):
            for wt_ in (wg, wq, wk, wv):
                wseq.append((wt_, hh * 128))
        wbufs = [s_wslice, s_wslice2]
        wl = {"i": 0, "c": 0, "use": [0, 0]}

        def wissue():
            i = wl["i"]
            if i >= len(wseq):
                return
            wt_, hof_ = wseq[i]
            par = i % 2
            w(tg, "MM", wl["use"][par])
            dma(tg, wbufs[par][:].rearrange("p (c f) -> p c f", f=128),
                wt_[:, hof_:hof_ + 128].rearrange("(c p) f -> p c f", p=128),
                sem="GL" if par == 0 else "GM")
            wl["i"] = i + 1

        def wopen():
            par = wl["c"] % 2
            w(tt, "GL" if par == 0 else "GM", 0)
            return wbufs[par]

        def wclose():
            par = wl["c"] % 2
            wl["use"][par] = cnt["MM"]
            wl["c"] += 1
            wissue()

        wissue()
        wissue()
        pendC = []

        def drainC(n=None):
            k = len(pendC) if n is None else min(n, len(pendC))
            for _ in range(k):
                pendC.pop(0)()

        for h in range(H):
            hof = h * 128

            def fproj(wt, dst, bks):
                wsl = wopen()
                w(tt, "VE", t_xT_all)
                for tc in range(4):
                    pn = bks[tc % 2]
                    ps = P[pn][:, 0:512]
                    for dc in range(8):
                        mm(pn, ps, wsl[:, dc * 128:(dc + 1) * 128],
                           s_xT[:, dc * TLOC + tc * 512:dc * TLOC + (tc + 1) * 512],
                           start=(dc == 0), stop=(dc == 7))
                    tk = cnt["MM"]
                    w(ts, "MM", tk)
                    evict_s(pn, ts.copy, out=dst[:, tc * 512:(tc + 1) * 512], in_=ps)
                    drainC(3)
                wclose()

            fproj(wg, s_lam, ("pA", "pB"))
            sop(ts.activation, out=s_lam[:], in_=s_lam[:], func=AF.Sigmoid)
            t_sig = sop(ts.activation, out=s_lam[:], in_=s_lam[:], func=AF.Ln)
            fproj(wq, s_q, ("pC", "pD"))
            fproj(wk, s_k, ("pA", "pB"))
            drainC(None)
            tC_last = cnt["MM"]
            t_qkg = cnt["SC"]
            w(tv, "SC", t_sig)
            w(tv, "LD", t_const)
            vop(tv.tensor_tensor_scan, out=s_lam[:], data0=s_mscan[:],
                data1=s_lam[:], initial=0.0, op0=ALU.mult, op1=ALU.add)
            lam3 = s_lam[:].rearrange("p (c k) -> p c k", k=CH)
            vop(tv.tensor_copy, out=s_lend[:], in_=lam3[:, :, CH - 1])
            t_lend = cnt["VE"]
            w(ts, "VE", t_lend)
            t_eend = sop(ts.activation, out=s_eend[:], in_=s_lend[:], func=AF.Exp)
            w(ts, "MM", tC_last)
            sop(ts.activation, out=s_qd[:], in_=s_q[:], func=AF.Square)
            t_q2 = cnt["SC"]
            sop(ts.activation, out=s_kd[:], in_=s_k[:], func=AF.Square)
            t_k2 = cnt["SC"]
            vop(tv.tensor_copy, out=s_linc[:], in_=s_lend[:])
            srcb, dstb = s_linc, s_small
            sh = 1
            while sh < NCH:
                vop(tv.tensor_copy, out=dstb[:, 0:sh], in_=srcb[:, 0:sh])
                vop(tv.tensor_add, out=dstb[:, sh:NCH], in0=srcb[:, sh:NCH],
                    in1=srcb[:, 0:NCH - sh])
                srcb, dstb = dstb, srcb
                sh *= 2
            if srcb is not s_linc:
                vop(tv.tensor_copy, out=s_linc[:], in_=srcb[:])
            vop(tv.tensor_sub, out=s_small[:], in0=s_linc[:], in1=s_lend[:])
            t_lexc = cnt["VE"]
            w(ts, "VE", t_lexc)
            t_elexc = sop(ts.activation, out=s_small[:], in_=s_small[:], func=AF.Exp)
            w(ts, "ST", cnt["ST"])
            sop(ts.activation, out=s_colsc[:, 0:1], in_=s_linc[:, NCH - 1:NCH],
                func=AF.Exp)
            w(ty, "SC", cnt["SC"])
            dma(ty, d_st_in[h, 128:129, :].rearrange("a p -> p a"),
                s_colsc[:, 0:1], sem="ST")

            wsl = wopen()
            for tti in range(NTT):
                pn = "pC" if tti % 2 == 0 else "pD"
                ps = P[pn][:, 0:128]
                for dc in range(8):
                    mm(pn, ps,
                       s_xT[:, dc * TLOC + tti * 128:dc * TLOC + (tti + 1) * 128],
                       wsl[:, dc * 128:(dc + 1) * 128],
                       start=(dc == 0), stop=(dc == 7))
                tk = cnt["MM"]
                w(tv, "MM", tk)
                evict_v(pn, tv.tensor_copy, out=s_v[:, tti * 128:(tti + 1) * 128], in_=ps)
            wclose()

            w(tt, "SC", t_q2)
            for tc in range(4):
                pn = "pA" if tc % 2 == 0 else "pB"
                ps = P[pn][:, 0:512]
                mm(pn, ps, s_ones128b[:], s_qd[:, tc * 512:(tc + 1) * 512])
                tk = cnt["MM"]
                w(ts, "MM", tk)
                evict_s(pn, ts.activation, out=s_eq[:, tc * 512:(tc + 1) * 512],
                        in_=ps, func=AF.Ln)
            w(tt, "SC", t_k2)
            for tc in range(4):
                pn = "pA" if tc % 2 == 0 else "pB"
                ps = P[pn][:, 0:512]
                mm(pn, ps, s_ones128b[:], s_kd[:, tc * 512:(tc + 1) * 512])
                tk = cnt["MM"]
                w(ts, "MM", tk)
                evict_s(pn, ts.activation, out=s_ek[:, tc * 512:(tc + 1) * 512],
                        in_=ps, func=AF.Ln)
            t_lnsk = cnt["SC"]

            w(tv, "SC", t_lnsk)
            t_kd_half = [0, 0]
            t_eqk = 0
            for hf in (0, 1):
                sl = slice(hf * 1024, (hf + 1) * 1024)
                vop(tv.scalar_tensor_tensor, out=s_eq[:, sl], in0=s_eq[:, sl],
                    scalar=-0.5, in1=s_lam[:, sl], op0=ALU.mult, op1=ALU.add)
                vop(tv.scalar_tensor_tensor, out=s_ek[:, sl], in0=s_ek[:, sl],
                    scalar=-0.5, in1=s_lam[:, sl], op0=ALU.mult, op1=ALU.subtract)
                t_lams = cnt["VE"]
                w(ts, "VE", t_lams)
                sop(ts.activation, out=s_eq[:, sl], in_=s_eq[:, sl], func=AF.Exp)
                t_eqk = sop(ts.activation, out=s_ek[:, sl], in_=s_ek[:, sl],
                            func=AF.Exp)
                w(tv, "SC", t_eqk)
                vop(tv.tensor_mul, out=s_qd[:, sl], in0=s_q[:, sl], in1=s_eq[:, sl])
                vop(tv.tensor_mul, out=s_kd[:, sl], in0=s_k[:, sl], in1=s_ek[:, sl])
                t_kd_half[hf] = cnt["VE"]
            eend3 = s_eend[:].unsqueeze(2).broadcast_to([128, NCH, CH])
            ek3 = s_ek[:].rearrange("p (c k) -> p c k", k=CH)
            lam3b = s_lam[:].rearrange("p (c k) -> p c k", k=CH)
            vop(tv.tensor_tensor, out=lam3b, in0=ek3, in1=eend3, op=ALU.mult)
            vop(tv.tensor_mul, out=s_lam[:], in0=s_k[:], in1=s_lam[:])
            t_khf = cnt["VE"]
            w(tv, "SC", t_elexc)
            sm3 = s_small[:].unsqueeze(2).broadcast_to([128, NCH, CH])
            eq3 = s_eq[:].rearrange("p (c k) -> p c k", k=CH)
            vop(tv.tensor_tensor, out=eq3, in0=eq3, in1=sm3, op=ALU.mult)
            vop(tv.tensor_mul, out=s_qhat[:, h * TLOC:(h + 1) * TLOC],
                in0=s_q[:], in1=s_eq[:])
            t_qhat = cnt["VE"]

            w(tt, "VE", t_kd_half[0])
            pnr = ["pA", "pB", "pC", "pD"]
            for c in range(NCH):
                if c == 16:
                    w(tt, "VE", t_kd_half[1])
                cof = c * CH
                base = (c % 2) * 64
                pnA = pnr[c % 4]
                psA = P[pnA][base:base + 64, 0:CH]
                tkA = mm(pnA, psA, s_kd[:, cof:cof + CH], s_qd[:, cof:cof + CH])
                amb = s_am[base:base + 64, (c // 2) * CH:(c // 2 + 1) * CH]
                w(tv, "MM", tkA)
                evict_v(pnA, tv.tensor_tensor, out=amb, in0=psA,
                        in1=s_maskA[base:base + 64, :], op=ALU.mult)
            t_amall = cnt["VE"]
            w(tt, "VE", t_khf)
            for tti in range(NTT):
                pn = "pS" if tti % 2 == 0 else "pT"
                ps = P[pn][:, 0:128]
                trans(pn, ps, s_lam[:, tti * 128:(tti + 1) * 128], s_id32[:])
                tk = cnt["MM"]
                w(tv, "MM", tk)
                evict_v(pn, tv.tensor_copy, out=s_khT[:, tti * 128:(tti + 1) * 128], in_=ps)
            t_khT = cnt["VE"]
            w(tt, "VE", t_amall)
            for c in range(NCH):
                cof = c * CH
                base = (c % 2) * 64
                tti_ = c // 2
                pnO = pnr[c % 4]
                psO = P[pnO][:, 0:CH]
                vsl = s_v[base:base + 64, tti_ * 128:(tti_ + 1) * 128]
                amb = s_am[base:base + 64, tti_ * CH:(tti_ + 1) * CH]
                tkO = mm(pnO, psO, vsl, amb)
                w(tv, "MM", tkO)
                evict_v(pnO, tv.tensor_copy,
                        out=s_osb[:, h * TLOC + cof:h * TLOC + cof + CH], in_=psO)
            t_ointra = cnt["VE"]
            stD = {"prev": 0}

            def _copen(t_oi=t_ointra, D=stD):
                w(tv, "ST", cnt["ST"])
                vop(tv.memset, s_state[:], 0.0)
                vop(tv.memset, s_stateb[:], 0.0)
                w(tt, "VE", t_oi)
                D["prev"] = cnt["VE"]

            pendC.append(_copen)
            for c in range(NCH):
                def _cchunk(c=c, hx=h, D=stD):
                    cof = c * CH
                    base = (c % 2) * 64
                    tti_ = c // 2
                    khsl = s_khT[base:base + 64, tti_ * 128:(tti_ + 1) * 128]
                    vsl = s_v[base:base + 64, tti_ * 128:(tti_ + 1) * 128]
                    if c > 0:
                        pnI = "pU" if c % 2 == 0 else "pV"
                        psI = P[pnI][:, 0:CH]
                        wbank(pnI)
                        w(tt, "VE", D["prev"])
                        tt.matmul(psI, s_stateb[:], s_qd[:, cof:cof + CH],
                                  start=True, stop=True).then_inc(sems["MM"], 1)
                        bump("MM")
                        tkI = cnt["MM"]
                    pnS = "pS" if c % 2 == 0 else "pT"
                    psS = P[pnS][:, 128:256]
                    tkS = mm(pnS, psS, khsl, vsl)
                    w(tv, "MM", tkS)
                    vop(tv.scalar_tensor_tensor, out=s_state[:], in0=s_state[:],
                        scalar=s_eend[:, c:c + 1], in1=psS,
                        op0=ALU.mult, op1=ALU.add)
                    t_st = vop(tv.tensor_copy, out=s_stateb[:], in_=s_state[:])
                    bank_free[pnS] = ("VE", t_st)
                    if c > 0:
                        w(tv, "MM", tkI)
                        evict_v(pnI, tv.tensor_tensor,
                                out=s_osb[:, hx * TLOC + cof:hx * TLOC + cof + CH],
                                in0=s_osb[:, hx * TLOC + cof:hx * TLOC + cof + CH],
                                in1=psI, op=ALU.add)
                    D["prev"] = t_st

                pendC.append(_cchunk)

            def _cstore(hx=h):
                w(ty, "VE", cnt["VE"])
                dma(ty, d_st_in[hx, 0:128, :], s_state[:], sem="ST")

            pendC.append(_cstore)
        drainC(None)
        t_p1_st = cnt["ST"]

        mark("P1")
        t_cc_st = allgather(d_st_in[:, :, :], d_st_out[:, :, :], t_p1_st)
        w(ty, "CC", t_cc_st)
        t_corr_mm = 0
        t_sti_use = 0
        for h in range(H):
            w(ty, "VE", t_sti_use)
            for i in range(7):
                dma(ty, s_dcol[:, i:i + 1],
                    d_st_out[i * H + h, 128:129, :].rearrange("a p -> p a"))
            l0 = dma(ty, s_sti[:, 0:128], d_st_out[0 * H + h, 0:128, :])
            w(tv, "LD", l0)
            w(tv, "MM", t_corr_mm)
            t_half = [0, 0]
            t_half[0] = vop(tv.tensor_scalar, out=s_sin[:], in0=s_sti[:, 0:128],
                            scalar1=s_msel[:, 0:1], scalar2=None, op0=ALU.mult)
            for i in range(1, 7):
                sl_ = s_sti[:, (i % 2) * 128:(i % 2) * 128 + 128]
                w(ty, "VE", t_half[i % 2])
                li = dma(ty, sl_, d_st_out[i * H + h, 0:128, :])
                vop(tv.tensor_scalar, out=s_colsc[:, 7:8], in0=s_dcol[:, i:i + 1],
                    scalar1=-1.0, scalar2=None, op0=ALU.add)
                vop(tv.tensor_scalar, out=s_colsc[:, 7:8], in0=s_colsc[:, 7:8],
                    scalar1=s_msel[:, i:i + 1], scalar2=1.0, op0=ALU.mult, op1=ALU.add)
                vop(tv.tensor_scalar, out=s_sin[:], in0=s_sin[:],
                    scalar1=s_colsc[:, 7:8], scalar2=None, op0=ALU.mult)
                w(tv, "LD", li)
                t_half[i % 2] = vop(tv.scalar_tensor_tensor, out=s_sin[:], in0=sl_,
                                    scalar=s_msel[:, i:i + 1], in1=s_sin[:],
                                    op0=ALU.mult, op1=ALU.add)
                t_sti_use = t_half[i % 2]
            t_sin = vop(tv.tensor_copy, out=s_sinb[:], in_=s_sin[:])
            t_sti_use = t_sin
            w(tt, "VE", t_sin)
            for tc in range(4):
                pn = "pA" if tc % 2 == 0 else "pB"
                ps = P[pn][:, 0:512]
                mm(pn, ps, s_sinb[:],
                   s_qhat[:, h * TLOC + tc * 512:h * TLOC + (tc + 1) * 512])
                tk = cnt["MM"]
                t_corr_mm = tk
                w(tv, "MM", tk)
                evict_v(pn, tv.tensor_tensor,
                        out=s_osb[:, h * TLOC + tc * 512:h * TLOC + (tc + 1) * 512],
                        in0=s_osb[:, h * TLOC + tc * 512:h * TLOC + (tc + 1) * 512],
                        in1=ps, op=ALU.add)
        t_p2 = cnt["VE"]

        mark("P2")
        join_all()
        st_attn.close()
        st_wo = ExitStack()
        s_wob = sbuf("s_wob", (128, 8 * 1024), BF16, stack=st_wo)

        lwo = dma(tg, s_wob[:].rearrange("p (c f) -> p c f", f=1024),
                  wo.rearrange("(c p) f -> p c f", p=128))
        w(tt, "GL", lwo)
        w(tt, "VE", t_p2)
        w(ty, "MM", cnt["MM"])
        w(ty, "VE", cnt["VE"])
        w(ty, "SC", cnt["SC"])
        for dt_ in range(8):
            dma(ty, s_xmid[:, dt_ * TLOC:(dt_ + 1) * TLOC],
                d_xT32[:, dt_ * TLOC:(dt_ + 1) * TLOC])
        pns4 = ["pA", "pB", "pC", "pD"]
        for dt_ in range(8):
            for hv in range(8):
                for tc in range(4):
                    mm(pns4[tc], P[pns4[tc]][:, 0:512],
                       s_wob[:, hv * 1024 + dt_ * 128:hv * 1024 + dt_ * 128 + 128],
                       s_osb[:, hv * TLOC + tc * 512:hv * TLOC + (tc + 1) * 512],
                       start=(hv == 0), stop=(hv == 7))
            tk = cnt["MM"]
            w(tv, "MM", tk)
            w(tv, "LD", 0)
            for tc in range(4):
                xsl = s_xmid[:, dt_ * TLOC + tc * 512:dt_ * TLOC + (tc + 1) * 512]
                t_ev = evict_v(pns4[tc], tv.tensor_tensor, out=xsl, in0=xsl,
                               in1=P[pns4[tc]][:, 0:512], op=ALU.add)
        t_resid = cnt["VE"]

        def layernorm_inplace(t_ready):
            w(ts, "VE", t_ready)
            t_done = 0
            for tc in range(4):
                pnS = "pC" if tc % 2 == 0 else "pD"
                psS = P[pnS][:, 0:512]
                pnQ = "pS" if tc % 2 == 0 else "pT"
                psQ = P[pnQ][:, 0:512]
                for dt_ in range(8):
                    sq_t = sop(ts.activation, out=s_f32b[:, 0:512],
                               in_=s_xmid[:, dt_ * TLOC + tc * 512:dt_ * TLOC + (tc + 1) * 512],
                               func=AF.Square)
                    w(tt, "VE", t_ready)
                    mm(pnS, psS, s_ones128f[:],
                       s_xmid[:, dt_ * TLOC + tc * 512:dt_ * TLOC + (tc + 1) * 512],
                       start=(dt_ == 0), stop=(dt_ == 7))
                    w(tt, "SC", sq_t)
                    mm(pnQ, psQ, s_ones128f[:], s_f32b[:, 0:512],
                       start=(dt_ == 0), stop=(dt_ == 7))
                    ts.wait_ge(sems["MM"], cnt["MM"])
                tkq_ = cnt["MM"]
                w(tv, "MM", tkq_)
                vop(tv.tensor_scalar, out=s_f32a[:, 0:512], in0=psS,
                    scalar1=1.0 / D, scalar2=None, op0=ALU.mult)
                bank_free[pnS] = ("VE", cnt["VE"])
                vop(tv.tensor_scalar, out=s_f32b[:, 0:512], in0=psQ,
                    scalar1=1.0 / D, scalar2=None, op0=ALU.mult)
                bank_free[pnQ] = ("VE", cnt["VE"])
                vop(tv.tensor_tensor, out=s_f32b[:, 512:1024],
                    in0=s_f32a[:, 0:512], in1=s_f32a[:, 0:512], op=ALU.mult)
                vop(tv.tensor_sub, out=s_f32b[:, 512:1024],
                    in0=s_f32b[:, 0:512], in1=s_f32b[:, 512:1024])
                tvar = cnt["VE"]
                w(ts, "VE", tvar)
                sop(ts.activation, out=s_f32b[:, 512:1024],
                    in_=s_f32b[:, 512:1024], func=AF.Ln, bias=s_epscol[:, 0:1])
                trst = sop(ts.activation, out=s_f32b[:, 512:1024],
                           in_=s_f32b[:, 512:1024], func=AF.Exp, scale=-0.5)
                w(tv, "SC", trst)
                for dt_ in range(8):
                    sl = slice(dt_ * TLOC + tc * 512, dt_ * TLOC + (tc + 1) * 512)
                    vop(tv.tensor_sub, out=s_xmid[:, sl], in0=s_xmid[:, sl],
                        in1=s_f32a[:, 0:512])
                    vop(tv.tensor_mul, out=s_xmid[:, sl], in0=s_xmid[:, sl],
                        in1=s_f32b[:, 512:1024])
                t_done = cnt["VE"]
            return t_done

        t_ln1 = layernorm_inplace(t_resid)

        w(tv, "VE", t_ln1)
        for dc in range(8):
            vop(tv.tensor_copy, out=s_xT[:, dc * TLOC:(dc + 1) * TLOC],
                in_=s_xmid[:, dc * TLOC:(dc + 1) * TLOC])
        t_xmb = cnt["VE"]

        rowb = s_rowsc[:].bitcast(BF16)
        for tti in range(NTT):
            w(tv, "ST", cnt["ST"])
            for dc in range(8):
                pn = "pS" if dc % 2 == 0 else "pT"
                ps = P[pn][:, 0:128]
                w(tt, "VE", t_ln1)
                trans(pn, ps, s_xmid[:, dc * TLOC + tti * 128:dc * TLOC + (tti + 1) * 128],
                      s_id32[:])
                tk = cnt["MM"]
                w(tv, "MM", tk)
                evict_v(pn, tv.tensor_copy,
                        out=rowb[:, dc * 128:(dc + 1) * 128], in_=ps)
            tke = cnt["VE"]
            w(ty, "VE", tke)
            dma(ty, d_xmid_in[tti * 128:(tti + 1) * 128, :], rowb[:, 0:D], sem="ST")
            _wst(ty)
        t_xmid_st = cnt["ST"]
        t_cc_xm = allgather(d_xmid_in[:, :], d_xmid_out[:, :], t_xmid_st)

        join_all()
        st_wo.close()
        st_osb.close()
        st_ffn1 = ExitStack()
        s_wgub = sbuf("s_wgub", (128, 8 * 1536), BF16, stack=st_ffn1)
        s_wsdb = sbuf("s_wsdb", (128, 6 * 1024), BF16, stack=st_ffn1)
        s_hsh = sbuf("s_hsh", (128, 6 * 512), BF16, stack=st_ffn1)
        st_p4 = ExitStack()
        s_probs = sbuf("s_probs", (128, NTT * E), stack=st_p4)
        s_h = sbuf("s_h", (128, NTT * E), BF16, stack=st_p4)
        s_cprev = sbuf("s_cprev", (128, NTT * E), stack=st_p4)
        s_cem = sbuf("s_cem", (64, 129), stack=st_p4)
        s_gcol = sbuf("s_gcol", (64, NTT), stack=st_p4)

        mark("P3")
        w(tg, "MM", cnt["MM"])
        lgu = dma(tg, s_wgub[:, 0:8 * 1536].rearrange("p (c f) -> p c f", f=1536),
                  wsgu.rearrange("(c p) f -> p c f", p=128))
        lsd = dma(tg, s_wsdb[:, 0:6 * 1024].rearrange("p (c f) -> p c f", f=1024),
                  wsd.rearrange("(c p) f -> p c f", p=128))
        lrw = dma(ty, s_f32a[:, 0:512].rearrange("p (c e) -> p c e", e=E),
                  rwt.rearrange("(c p) e -> p c e", p=128))
        w(tt, "LD", lrw)
        w(tt, "VE", t_ln1)
        for tti in range(NTT):
            pn = "pU" if tti % 2 == 0 else "pV"
            ps = P[pn][:, 0:E]
            for dc in range(8):
                mm(pn, ps, s_xmid[:, dc * TLOC + tti * 128:dc * TLOC + (tti + 1) * 128],
                   s_f32a[:, dc * E:(dc + 1) * E], start=(dc == 0), stop=(dc == 7))
            tk = cnt["MM"]
            w(tv, "MM", tk)
            evict_v(pn, tv.tensor_copy, out=s_probs[:, tti * E:(tti + 1) * E], in_=ps)
        t_logits = cnt["VE"]

        pr3 = s_probs[:].rearrange("p (t e) -> p t e", e=E)
        vop(tv.tensor_reduce, out=s_f32a[:, 0:NTT], in_=pr3, axis=AX, op=ALU.max)
        mx3 = s_f32a[:, 0:NTT].unsqueeze(2).broadcast_to([128, NTT, E])
        vop(tv.tensor_tensor, out=pr3, in0=pr3, in1=mx3, op=ALU.subtract)
        t_sm1 = cnt["VE"]
        w(ts, "VE", t_sm1)
        t_expp = sop(ts.activation, out=s_probs[:], in_=s_probs[:], func=AF.Exp)
        w(tv, "SC", t_expp)
        vop(tv.tensor_reduce, out=s_f32a[:, 0:NTT], in_=pr3, axis=AX, op=ALU.add)
        vop(tv.reciprocal, out=s_f32a[:, 0:NTT], in_=s_f32a[:, 0:NTT])
        vop(tv.tensor_tensor, out=pr3, in0=pr3, in1=mx3, op=ALU.mult)
        t_sm = cnt["VE"]

        hf3 = s_f32b[:, 0:NTT * E].rearrange("p (t e) -> p t e", e=E)
        vop(tv.memset, s_f32b[:], 0.0)
        w83 = s_w8[:].rearrange("p (t r) -> p t r", r=8)
        e83 = s_e8[:].rearrange("p (t r) -> p t r", r=8)
        for r in range(TOPK):
            vop(tv.tensor_reduce, out=w83[:, :, r], in_=pr3, axis=AX, op=ALU.max)
            wb3 = w83[:, :, r:r + 1].broadcast_to([128, NTT, E])
            eq3_ = s_cprev[:].rearrange("p (t e) -> p t e", e=E)
            vop(tv.tensor_tensor, out=eq3_, in0=pr3, in1=wb3, op=ALU.is_ge)
            vop(tv.tensor_tensor, out=hf3, in0=hf3, in1=eq3_, op=ALU.max)
            io3 = s_iotae[:].unsqueeze(1).broadcast_to([128, NTT, E])
            vop(tv.tensor_tensor, out=eq3_, in0=eq3_, in1=io3, op=ALU.mult)
            vop(tv.tensor_reduce, out=e83[:, :, r], in_=eq3_, axis=AX, op=ALU.max)
            vop(tv.tensor_tensor, out=eq3_, in0=pr3, in1=wb3, op=ALU.is_ge)
            vop(tv.scalar_tensor_tensor, out=pr3, in0=eq3_, scalar=-2.0,
                in1=pr3, op0=ALU.mult, op1=ALU.add)
        vop(tv.tensor_copy, out=s_h[:], in_=s_f32b[:, 0:NTT * E])
        t_top8 = cnt["VE"]

        vop(tv.tensor_reduce, out=s_f32b[:, 0:NTT], in_=w83, axis=AX, op=ALU.add)
        vop(tv.reciprocal, out=s_f32b[:, 0:NTT], in_=s_f32b[:, 0:NTT])
        sb3 = s_f32b[:, 0:NTT].unsqueeze(2).broadcast_to([128, NTT, 8])
        wn3 = s_wn8[:].rearrange("p (t r) -> p t r", r=8)
        vop(tv.tensor_tensor, out=wn3, in0=w83, in1=sb3, op=ALU.mult)
        t_wn = cnt["VE"]

        vop(tv.memset, s_f32b[:, 512:513], 0.0)
        t_base0 = cnt["VE"]
        w(tt, "VE", max(t_top8, t_base0))
        t_gtab_st_list = []
        for tti in range(NTT):
            h_t = s_h[:, tti * E:(tti + 1) * E]
            pnc = "pU" if tti % 2 == 0 else "pV"
            psc = P[pnc][0:64, 0:129]
            w(tv, "ST", cnt["ST"])
            mm(pnc, psc, h_t, s_trib[:])
            tk1 = cnt["MM"]
            pnd = "pS" if tti % 2 == 0 else "pT"
            psd = P[pnd][:, 0:E]
            mm(pnd, psd, s_trib[:, 0:128], h_t)
            tk2 = cnt["MM"]
            w(tv, "MM", tk1)
            bb3 = s_f32b[0:64, 512:513].unsqueeze(2).broadcast_to([64, 1, 129])
            cem3 = s_cem[:].rearrange("p (a c) -> p a c", a=1)
            ps3 = psc.rearrange("p (a c) -> p a c", a=1)
            evict_v(pnc, tv.tensor_tensor, out=cem3, in0=ps3, in1=bb3, op=ALU.add)
            w(tv, "MM", tk2)
            evict_v(pnd, tv.tensor_copy, out=s_cprev[:, tti * E:(tti + 1) * E], in_=psd)
            vop(tv.tensor_copy, out=s_gcol[:, tti:tti + 1], in_=s_cem[0:64, 0:1])
            tb = vop(tv.tensor_add, out=s_f32b[0:64, 512:513],
                     in0=s_f32b[0:64, 512:513], in1=psc[0:64, 128:129])
            bank_free[pnc] = ("VE", tb)
            tkc = cnt["VE"]
            w(ty, "VE", tkc)
            dma(ty, d_gtab_in[tti * E:(tti + 1) * E, :], s_cem[0:64, 0:128], sem="ST")
            _wst(ty)
        t_cnts = cnt["VE"]
        t_gtab_st = cnt["ST"]
        w(ty, "VE", t_cnts)
        dma(ty, d_cnt_in[0:1, :].rearrange("a p -> p a"), s_f32b[0:64, 512:513], sem="ST")
        t_cnt_st = cnt["ST"]
        t_cc_cnt = allgather(d_cnt_in[:, :], d_cnt_out[:, :], t_cnt_st)
        t_cc_gt = allgather(d_gtab_in[:, :], d_gtab_out[:, :], t_gtab_st)

        mark("P4")
        w(tt, "GL", lgu)
        w(tt, "VE", t_xmb)
        t_silu_free = 0
        for tca in range(4):
            for ft in range(6):
                png = "pA" if ft % 2 == 0 else "pB"
                psg = P[png][:, 0:512]
                pnu = "pC" if ft % 2 == 0 else "pD"
                psu = P[pnu][:, 0:512]
                for dc in range(8):
                    mm(png, psg, s_wgub[:, dc * 1536 + ft * 128:dc * 1536 + ft * 128 + 128],
                       s_xT[:, dc * TLOC + tca * 512:dc * TLOC + (tca + 1) * 512],
                       start=(dc == 0), stop=(dc == 7))
                tkg = cnt["MM"]
                for dc in range(8):
                    mm(pnu, psu,
                       s_wgub[:, dc * 1536 + 768 + ft * 128:dc * 1536 + 768 + ft * 128 + 128],
                       s_xT[:, dc * TLOC + tca * 512:dc * TLOC + (tca + 1) * 512],
                       start=(dc == 0), stop=(dc == 7))
                tku = cnt["MM"]
                w(ts, "MM", tkg)
                w(ts, "VE", t_silu_free)
                tsg = evict_s(png, ts.activation, out=s_f32a[:, 0:512],
                              in_=psg, func=AF.Silu)
                w(tv, "SC", tsg)
                w(tv, "MM", tku)
                t_silu_free = evict_v(
                    pnu, tv.tensor_tensor,
                    out=s_hsh[:, ft * 512:(ft + 1) * 512],
                    in0=s_f32a[:, 0:512], in1=psu, op=ALU.mult)
            t_hsh = cnt["VE"]
            w(tt, "GL", lsd)
            w(tt, "VE", t_hsh)
            for dt_ in range(8):
                pn = "pA" if dt_ % 2 == 0 else "pB"
                ps = P[pn][:, 0:512]
                for ft in range(6):
                    mm(pn, ps, s_wsdb[:, ft * 1024 + dt_ * 128:ft * 1024 + dt_ * 128 + 128],
                       s_hsh[:, ft * 512:(ft + 1) * 512],
                       start=(ft == 0), stop=(ft == 5))
                tk = cnt["MM"]
                w(tv, "MM", tk)
                sl = slice(dt_ * TLOC + tca * 512, dt_ * TLOC + (tca + 1) * 512)
                evict_v(pn, tv.tensor_tensor, out=s_xmid[:, sl], in0=s_xmid[:, sl],
                        in1=ps, op=ALU.add)
            w(tt, "VE", cnt["VE"])
        t_acc_sh = cnt["VE"]

        mark("P5a")
        w(ty, "CC", t_cc_cnt)
        lcnt = dma(ty, s_cnt_all[:], d_cnt_out[:, :])
        w(tt, "LD", lcnt)
        tmsel = trans("pU", P["pU"][0:8, 0:128], s_msel[:, 0:8], s_id32[:])
        w(tv, "MM", tmsel)
        tmc = evict_v("pU", tv.tensor_copy, out=s_f32a[0:8, 0:128], in_=P["pU"][0:8, 0:128])
        w(tt, "VE", tmc)
        mm("pV", P["pV"][:, 0:E], s_f32a[0:8, 0:128], s_cnt_all[0:8, :])
        tkb = cnt["MM"]
        w(tv, "MM", tkb)
        evict_v("pV", tv.tensor_copy, out=s_base[:], in_=P["pV"][:, 0:E])
        mm("pU", P["pU"][0:64, 0:1], s_cnt_all[0:8, :], s_f32a[0:8, 0:1])
        tkbe = cnt["MM"]
        w(tv, "MM", tkbe)
        evict_v("pU", tv.tensor_copy, out=s_bgl[:], in_=P["pU"][0:64, 0:1])
        t_bases = cnt["VE"]

        vop(tv.memset, s_f32b[:, 0:E], 0.0)
        vop(tv.memset, s_f32a[:, 512:640], 1.0)
        t_ones = cnt["VE"]
        w(tt, "VE", t_ones)
        for tti in range(NTT):
            sl = slice(tti * E, (tti + 1) * E)
            vop(tv.tensor_add, out=s_cprev[:, sl], in0=s_cprev[:, sl], in1=s_f32b[:, 0:E])
            vop(tv.tensor_add, out=s_cprev[:, sl], in0=s_cprev[:, sl], in1=s_base[:])
            pn = "pU" if tti % 2 == 0 else "pV"
            ps = P[pn][:, 0:E]
            mm(pn, ps, s_ones128b[:], s_h[:, sl])
            tk = cnt["MM"]
            w(tv, "MM", tk)
            evict_v(pn, tv.tensor_tensor, out=s_f32b[:, 0:E], in0=s_f32b[:, 0:E],
                    in1=ps, op=ALU.add)
        t_cglob = cnt["VE"]

        for r in range(TOPK):
            e8c3 = e83[:, :, r:r + 1].broadcast_to([128, NTT, E])
            io3 = s_iotae[:].unsqueeze(1).broadcast_to([128, NTT, E])
            eqw = s_f32a[:, 0:NTT * E].rearrange("p (t e) -> p t e", e=E)
            vop(tv.tensor_tensor, out=eqw, in0=io3, in1=e8c3, op=ALU.is_equal)
            gio3 = s_giotae[:].unsqueeze(1).broadcast_to([128, NTT, E])
            gs3 = s_probs[:].rearrange("p (t e) -> p t e", e=E)
            g83 = s_g8[:].rearrange("p (t r) -> p t r", r=8)
            vop(tv.tensor_tensor, out=gs3, in0=eqw, in1=gio3, op=ALU.mult)
            vop(tv.tensor_reduce, out=g83[:, :, r], in_=gs3, axis=AX, op=ALU.max)
            cg3 = s_cprev[:].rearrange("p (t e) -> p t e", e=E)
            vop(tv.tensor_tensor, out=eqw, in0=eqw, in1=cg3, op=ALU.mult)
            p83 = s_pos8[:].rearrange("p (t r) -> p t r", r=8)
            vop(tv.tensor_reduce, out=p83[:, :, r], in_=eqw, axis=AX, op=ALU.add)
        vop(tv.tensor_scalar, out=s_val8[:], in0=s_pos8[:], scalar1=float(CAP),
            scalar2=None, op0=ALU.is_lt)
        vop(tv.tensor_mul, out=s_wn8[:], in0=s_wn8[:], in1=s_val8[:])
        vop(tv.tensor_scalar, out=s_slot8[:], in0=s_e8[:], scalar1=float(CAP),
            scalar2=None, op0=ALU.mult)
        vop(tv.tensor_add, out=s_slot8[:], in0=s_slot8[:], in1=s_pos8[:])
        vop(tv.tensor_mul, out=s_slot8[:], in0=s_slot8[:], in1=s_val8[:])
        v83 = s_val8[:].rearrange("p (t r) -> p t r", r=8)
        vop(tv.tensor_reduce, out=s_ninv[:], in_=v83, axis=AX, op=ALU.add)
        vop(tv.tensor_scalar, out=s_ninv[:], in0=s_ninv[:], scalar1=-1.0,
            scalar2=8.0, op0=ALU.mult, op1=ALU.add)
        vop(tv.tensor_copy, out=s_ninvb[:], in_=s_ninv[:])
        A_ = s_f32a[:, 0:128]
        vop(tv.tensor_scalar, out=A_, in0=s_g8[:], scalar1=float(CAP),
            scalar2=None, op0=ALU.mult)
        vop(tv.tensor_add, out=A_, in0=A_, in1=s_pos8[:])
        vop(tv.tensor_mul, out=A_, in0=A_, in1=s_val8[:])
        vop(tv.tensor_copy, out=s_slot8i[:], in_=A_)
        t_slot = cnt["VE"]

        for tti in range(NTT):
            w(tv, "ST", cnt["ST"])
            vop(tv.memset, s_f32b[:, 64:128], 0.0)
            vop(tv.tensor_copy, out=s_f32b[:, 64:72], in_=s_e8[:, tti * 8:(tti + 1) * 8])
            vop(tv.tensor_copy, out=s_f32b[:, 72:80], in_=s_wn8[:, tti * 8:(tti + 1) * 8])
            tkr = cnt["VE"]
            w(ty, "VE", tkr)
            dma(ty, d_rtab_in[tti * 128:(tti + 1) * 128, :], s_f32b[:, 64:128], sem="ST")
            _wst(ty)
        t_rtab_st = cnt["ST"]
        t_cc_rt = allgather(d_rtab_in[:, :], d_rtab_out[:, :], t_rtab_st)

        w(tv, "VE", t_bases)
        bg3 = s_bgl[:].unsqueeze(2).broadcast_to([64, 1, NTT])
        gc3 = s_gcol[:].unsqueeze(1)
        vop(tv.tensor_tensor, out=gc3, in0=gc3, in1=bg3, op=ALU.add)
        vop(tv.memset, s_f32b[:, 128:192], 0.0)
        vop(tv.tensor_copy, out=s_f32b[0:64, 128:144], in_=s_gcol[:])
        vop(tv.tensor_copy, out=s_f32b[0:64, 144:145], in_=s_f32b[0:64, 512:513])
        t_gst = cnt["VE"]
        w(ty, "VE", t_gst)
        dma(ty, d_gst_in[:, :], s_f32b[0:64, 128:192], sem="ST")
        t_gst_st = cnt["ST"]
        t_cc_gst = allgather(d_gst_in[:, :], d_gst_out[:, :], t_gst_st)

        w(ty, "VE", t_slot)
        dma(ty, d_scri[:, 0:128], s_slot8i[:], sem="ST")
        _wst(ty)
        for g_ in range(8):
            dma(ty, s_yidx[0:16, :].rearrange("q (f gg) -> q f gg", gg=8)[:, :, g_],
                d_scri[16 * g_:16 * g_ + 16, 0:128])
        _wld(ty)
        dma(ty, s_yidx[16:32, :], s_yidx[0:16, :])
        t_yidx = cnt["LD"]
        dma(ty, d_scri[:, 128:144], s_ninvb[:].bitcast(I16), sem="ST")
        _wst(ty)
        dma(ty, s_ninvrow[0:1, :].bitcast(I16).rearrange("a (t p) -> a t p", t=NTT),
            d_scri[:, 128:144].rearrange("(a p) t -> a t p", a=1))
        t_ninvr = cnt["LD"]

        mark("P5b")
        join_all()
        st_p4.close()
        st_ffn1.close()
        st_xT.close()
        st_p6 = ExitStack()
        s_grow = sbuf("s_grow", (128, 32 * 128), stack=st_p6)
        s_rt = sbuf("s_rt", (128, 32 * RTW), stack=st_p6)
        s_gstart = sbuf("s_gstart", (128, 128), stack=st_p6)

        w(tg, "CC", t_cc_gst)
        w(tg, "LD", t_const)
        tgk = gather(s_grow[:, 0:512].rearrange("p (n d) -> p n d", d=64),
                     d_gst_out[:, :], s_gsidx[:], 1024)
        w(tv, "GP", tgk)
        gat3 = s_grow[:, 0:512].rearrange("p (n d) -> p n d", d=64)
        gst2 = s_gstart[:].rearrange("p (n k) -> p n k", k=16)
        vop(tv.tensor_copy, out=gst2, in_=gat3[:, :, 0:16])
        vop(tv.tensor_reduce, out=s_totc[:], in_=gat3[:, :, 16:17].rearrange(
            "p n a -> p (n a)"), axis=AX, op=ALU.add)
        vop(tv.tensor_copy, out=s_f32b[:, 200:208], in_=gat3[:, :, 16])
        t_gstart = cnt["VE"]

        vop(tv.memset, s_kstar[:], 0.0)
        for k in range(128):
            vop(tv.scalar_tensor_tensor, out=s_kstar[:], in0=s_ctens[:],
                scalar=s_gstart[:, k:k + 1], in1=s_kstar[:], op0=ALU.is_ge, op1=ALU.add)
        vop(tv.tensor_scalar, out=s_kstar[:], in0=s_kstar[:], scalar1=-1.0,
            scalar2=None, op0=ALU.add)
        vop(tv.tensor_scalar, out=s_f32a[:, 0:32], in0=s_kstar[:], scalar1=64.0,
            scalar2=None, op0=ALU.mult)
        vop(tv.tensor_scalar, out=s_f32a[:, 0:32], in0=s_f32a[:, 0:32],
            scalar1=s_ecol[:, 0:1], scalar2=None, op0=ALU.add)
        tgidxf = vop(tv.tensor_copy, out=s_kstari[:], in_=s_f32a[:, 0:32])
        w(ty, "VE", tgidxf)
        dma(ty, d_scri[:, 0:32], s_kstari[:], sem="ST")
        _wst(ty)
        for g_ in range(8):
            dma(ty, s_gidx[0:16, 0:256].rearrange("q (b gg) -> q b gg", gg=8)[:, :, g_],
                d_scri[16 * g_:16 * g_ + 16, 0:32])
        _wld(ty)
        dma(ty, s_gidx[16:32, :], s_gidx[0:16, :])
        t_gidx = cnt["LD"]
        w(tg, "CC", t_cc_gt)
        w(tg, "LD", t_gidx)
        w(tg, "VE", cnt["VE"])
        for cq in range(4):
            gather(s_grow[:, cq * 1024:(cq + 1) * 1024].rearrange(
                "p (n d) -> p n d", d=128),
                d_gtab_out[:, :], s_gidx[:, cq * 64:(cq + 1) * 64], 1024)
        t_growg = cnt["GP"]
        w(tv, "GP", t_growg)
        vop(tv.tensor_copy, out=s_f32a[:, 0:32], in_=s_ctens[:])
        for j in range(7):
            vop(tv.tensor_scalar, out=s_f32a[:, 32:64], in0=s_kstar[:],
                scalar1=float(16 * (j + 1)), scalar2=None, op0=ALU.is_ge)
            vop(tv.tensor_scalar, out=s_f32a[:, 32:64], in0=s_f32a[:, 32:64],
                scalar1=s_f32b[:, 200 + j:201 + j], scalar2=None, op0=ALU.mult)
            vop(tv.tensor_sub, out=s_f32a[:, 0:32], in0=s_f32a[:, 0:32],
                in1=s_f32a[:, 32:64])
        gr3 = s_grow[:].rearrange("p (b j) -> p b j", j=128)
        c3 = s_f32a[:, 0:32].unsqueeze(2).broadcast_to([128, 32, 128])
        vop(tv.tensor_tensor, out=gr3, in0=gr3, in1=c3, op=ALU.is_le)
        vop(tv.tensor_reduce, out=s_stok[:], in_=gr3, axis=AX, op=ALU.add)
        vop(tv.tensor_scalar, out=s_f32a[:, 0:32], in0=s_kstar[:], scalar1=128.0,
            scalar2=None, op0=ALU.mult)
        vop(tv.tensor_add, out=s_stok[:], in0=s_stok[:], in1=s_f32a[:, 0:32])
        vop(tv.tensor_scalar, out=s_stok[:], in0=s_stok[:], scalar1=-1.0,
            scalar2=None, op0=ALU.add)
        tstok = vop(tv.tensor_copy, out=s_stoki[:], in_=s_stok[:])
        w(ty, "VE", tstok)
        dma(ty, d_scri[:, 0:32], s_stoki[:], sem="ST")
        _wst(ty)
        for bh in range(2):
            dma(ty, s_sidx[0:16, :].rearrange("q (el cb bb) -> q el cb bb",
                                              el=8, bb=2)[:, :, :, bh],
                d_scri[:, 0:32].rearrange("(el cb) (bb q) -> q el cb bb",
                                          el=8, bb=2)[:, :, :, bh])
        for g_ in range(8):
            dma(ty, s_sidx2[0:16, :].rearrange("q (b gg) -> q b gg", gg=8)[:, :, g_],
                d_scri[16 * g_:16 * g_ + 16, 0:32])
        _wld(ty)
        dma(ty, s_sidx[16:32, :], s_sidx[0:16, :])
        dma(ty, s_sidx2[16:32, :], s_sidx2[0:16, :])
        t_sidx = cnt["LD"]

        w(tg, "CC", t_cc_rt)
        w(tg, "LD", t_sidx)
        for cq in range(4):
            gather(s_rt[:, cq * 512:(cq + 1) * 512].rearrange("p (n d) -> p n d", d=RTW),
                   d_rtab_out[:, :], s_sidx2[:, cq * 64:(cq + 1) * 64], 1024)
        t_rtg = cnt["GP"]
        w(tv, "GP", t_rtg)
        rt3 = s_rt[:].rearrange("p (b d) -> p b d", d=RTW)
        wetmp = s_f32a[:, 0:256].rearrange("p (b r) -> p b r", r=8)
        vop(tv.tensor_scalar, out=wetmp, in0=rt3[:, :, 0:8],
            scalar1=s_ecol[:, 0:1], scalar2=None, op0=ALU.is_equal)
        vop(tv.tensor_tensor, out=wetmp, in0=wetmp, in1=rt3[:, :, 8:16], op=ALU.mult)
        vop(tv.tensor_reduce, out=s_we[:], in_=wetmp, axis=AX, op=ALU.add)
        vop(tv.tensor_scalar, out=s_f32a[:, 0:32], in0=s_ctens[:],
            scalar1=s_totc[:, 0:1], scalar2=None, op0=ALU.is_lt)
        vop(tv.tensor_mul, out=s_we[:], in0=s_we[:], in1=s_f32a[:, 0:32])
        twe = cnt["VE"]
        w(ty, "VE", twe)
        dma(ty, d_scrf[:, 0:32], s_we[:], sem="ST")
        _wst(ty)
        for gg in range(4):
            dma(ty, s_wec[gg * 32:(gg + 1) * 32, :].rearrange("q (el cc) -> q el cc", el=8),
                d_scrf[:, 0:32].rearrange("(el cc g2) q -> g2 q el cc", el=8, cc=4)[gg])
        t_wec = cnt["LD"]

        mark("P6")
        join_all()
        st_p6.close()
        st_p7 = ExitStack()
        s_wex = [sbuf(f"s_wex{i}", (128, 2 * 8 * F + 6 * 1024), BF16, stack=st_p7)
                 for i in range(2)]
        s_xb = sbuf("s_xb", (128, 4 * D), BF16, stack=st_p7)
        s_xbT = sbuf("s_xbT", (128, 8 * 512), BF16, stack=st_p7)
        s_he = sbuf("s_he", (128, 6 * 512), BF16, stack=st_p7)
        s_ye = sbuf("s_ye", (128, 4 * D), BF16, stack=st_p7)

        def load_expert(el, buf):
            dma(tg, buf[:, 0:6144].rearrange("p (c f) -> p c f", f=F),
                weg[el].rearrange("(c p) f -> p c f", p=128))
            dma(tg, buf[:, 6144:12288].rearrange("p (c f) -> p c f", f=F),
                weu[el].rearrange("(c p) f -> p c f", p=128))
            return dma(tg, buf[:, 12288:18432].rearrange("p (c f) -> p c f", f=1024),
                       wed[el].rearrange("(c p) f -> p c f", p=128))

        t_wexl = [0, 0]
        t_silu_free7 = 0
        t_wexl[0] = load_expert(0, s_wex[0])
        t_ye_st = 0
        t_xb_mm = 0
        w(tg, "CC", t_cc_xm)
        w(tg, "LD", t_sidx)
        for el in range(EO):
            buf = s_wex[el % 2]
            w(tg, "MM", t_xb_mm)
            tgx = gather(s_xb[:].rearrange("p (n d) -> p n d", d=D),
                         d_xmid_out[:, :], s_sidx[:, el * 32:(el + 1) * 32], 512)
            w(tt, "GP", tgx)
            for blk in range(4):
                for dc in range(8):
                    pn = "pS" if (blk * 8 + dc) % 2 == 0 else "pT"
                    ps = P[pn][:, 0:64].bitcast(BF16)
                    trans(pn, ps, s_xb[:, blk * D + dc * 128:blk * D + (dc + 1) * 128],
                          s_id16[:])
                    tk = cnt["MM"]
                    w(tv, "MM", tk)
                    evict_v(pn, tv.tensor_copy,
                            out=s_xbT[:, dc * 512 + blk * 128:dc * 512 + (blk + 1) * 128],
                            in_=ps)
            t_xbT = cnt["VE"]
            t_xb_mm = cnt["MM"]
            w(tt, "GL", t_wexl[el % 2])
            w(tt, "VE", t_xbT)
            for ft in range(6):
                png = "pA" if ft % 2 == 0 else "pB"
                psg = P[png][:, 0:512]
                pnu = "pC" if ft % 2 == 0 else "pD"
                psu = P[pnu][:, 0:512]
                for dc in range(8):
                    mm(png, psg, buf[:, dc * F + ft * 128:dc * F + ft * 128 + 128],
                       s_xbT[:, dc * 512:(dc + 1) * 512],
                       start=(dc == 0), stop=(dc == 7))
                tkg = cnt["MM"]
                for dc in range(8):
                    mm(pnu, psu, buf[:, 6144 + dc * F + ft * 128:6144 + dc * F + ft * 128 + 128],
                       s_xbT[:, dc * 512:(dc + 1) * 512],
                       start=(dc == 0), stop=(dc == 7))
                tku = cnt["MM"]
                w(ts, "MM", tkg)
                w(ts, "VE", t_silu_free7)
                tsg = evict_s(png, ts.activation, out=s_f32a[:, 0:512],
                              in_=psg, func=AF.Silu)
                w(tv, "SC", tsg)
                w(tv, "MM", tku)
                t_silu_free7 = evict_v(
                    pnu, tv.tensor_tensor, out=s_he[:, ft * 512:(ft + 1) * 512],
                    in0=s_f32a[:, 0:512], in1=psu, op=ALU.mult)
            t_he = cnt["VE"]
            if el + 1 < EO:
                w(tg, "MM", cnt["MM"])
                t_wexl[(el + 1) % 2] = load_expert(el + 1, s_wex[(el + 1) % 2])
            w(tt, "VE", t_he)
            w(tt, "LD", t_wec)
            for ct in range(4):
                for hfd in range(2):
                    pn = "pU" if (ct * 2 + hfd) % 2 == 0 else "pV"
                    ps = P[pn][:, 0:512]
                    for ft in range(6):
                        mm(pn, ps, s_he[:, ft * 512 + ct * 128:ft * 512 + ct * 128 + 128],
                           buf[:, 12288 + ft * 1024 + hfd * 512:12288 + ft * 1024 + (hfd + 1) * 512],
                           start=(ft == 0), stop=(ft == 5))
                    tk = cnt["MM"]
                    w(tv, "MM", tk)
                    w(tv, "ST", t_ye_st)
                    evict_v(pn, tv.tensor_scalar,
                            out=s_ye[:, ct * D + hfd * 512:ct * D + (hfd + 1) * 512],
                            in0=ps, scalar1=s_wec[:, el * 4 + ct:el * 4 + ct + 1],
                            scalar2=None, op0=ALU.mult)
            t_ye = cnt["VE"]
            w(ty, "VE", t_ye)
            dma(ty, d_ybuf_in[el * CAP:(el + 1) * CAP, :].rearrange(
                "(b p) d -> p b d", p=128),
                s_ye[:].rearrange("p (b d) -> p b d", d=D), sem="ST")
            t_ye_st = cnt["ST"]
            if el % 2 == 1:
                q_ = el // 2
                t_cc_yb = allgather(d_ybuf_in[q_ * 1024:(q_ + 1) * 1024, :],
                                    d_ybuf_out[q_ * 8192:(q_ + 1) * 8192, :],
                                    cnt["ST"])

        mark("P7")
        join_all()
        st_p7.close()
        st_p8 = ExitStack()
        s_yg = sbuf("s_yg", (128, 8 * D), BF16, stack=st_p8)
        s_acc1 = sbuf("s_acc1", (128, 4 * D), BF16, stack=st_p8)

        w(ty, "CC", t_cc_yb)
        ly0 = dma(ty, s_y0[0:1, :], d_ybuf_out[0:1, :])
        t_y0 = cnt["LD"]

        t_sum_prev = 0
        for tti in range(NTT):
            w(tg, "VE", t_sum_prev)
            w(tg, "CC", t_cc_yb)
            w(tg, "LD", t_yidx)
            tgy = gather(s_yg[:].rearrange("p (n d) -> p n d", d=D),
                         d_ybuf_out[:, :], s_yidx[:, tti * 64:(tti + 1) * 64], 1024)
            w(tv, "GP", tgy)
            for j in range(4):
                vop(tv.tensor_add, out=s_acc1[:, j * D:(j + 1) * D],
                    in0=s_yg[:, (2 * j) * D:(2 * j + 1) * D],
                    in1=s_yg[:, (2 * j + 1) * D:(2 * j + 2) * D])
            for j in range(2):
                vop(tv.tensor_add, out=s_acc1[:, j * D:(j + 1) * D],
                    in0=s_acc1[:, (2 * j) * D:(2 * j + 1) * D],
                    in1=s_acc1[:, (2 * j + 1) * D:(2 * j + 2) * D])
            vop(tv.tensor_add, out=s_acc1[:, 0:D], in0=s_acc1[:, 0:D],
                in1=s_acc1[:, D:2 * D])
            t_sum = cnt["VE"]
            t_sum_prev = t_sum
            w(tt, "LD", max(t_y0, t_ninvr))
            pnc = "pU" if tti % 2 == 0 else "pV"
            mm(pnc, P[pnc][:, 0:512], s_ninvrow[0:1, tti * 128:(tti + 1) * 128],
               s_y0[0:1, 0:512])
            tk1 = cnt["MM"]
            pnd = "pS" if tti % 2 == 0 else "pT"
            mm(pnd, P[pnd][:, 0:512], s_ninvrow[0:1, tti * 128:(tti + 1) * 128],
               s_y0[0:1, 512:1024])
            tk2 = cnt["MM"]
            w(tv, "VE", t_sum)
            w(tv, "MM", tk2)
            evict_v(pnc, tv.tensor_tensor, out=s_acc1[:, 0:512], in0=s_acc1[:, 0:512],
                    in1=P[pnc][:, 0:512], op=ALU.subtract)
            evict_v(pnd, tv.tensor_tensor, out=s_acc1[:, 512:1024], in0=s_acc1[:, 512:1024],
                    in1=P[pnd][:, 0:512], op=ALU.subtract)
            t_rt_tile = cnt["VE"]
            for dc in range(8):
                pn = "pA" if dc % 2 == 0 else "pB"
                ps = P[pn][:, 0:64].bitcast(BF16)
                w(tt, "VE", t_rt_tile)
                trans(pn, ps, s_acc1[:, dc * 128:(dc + 1) * 128], s_id16[:])
                tk = cnt["MM"]
                w(tv, "MM", tk)
                evict_v(pn, tv.tensor_tensor,
                        out=s_xmid[:, dc * TLOC + tti * 128:dc * TLOC + (tti + 1) * 128],
                        in0=s_xmid[:, dc * TLOC + tti * 128:dc * TLOC + (tti + 1) * 128],
                        in1=ps, op=ALU.add)
        t_routed = cnt["VE"]

        t_ln2 = layernorm_inplace(t_routed)

        for tti in range(NTT):
            w(tv, "ST", cnt["ST"])
            for dc in range(8):
                pn = "pS" if dc % 2 == 0 else "pT"
                ps = P[pn][:, 0:128]
                w(tt, "VE", t_ln2)
                trans(pn, ps, s_xmid[:, dc * TLOC + tti * 128:dc * TLOC + (tti + 1) * 128],
                      s_id32[:])
                tk = cnt["MM"]
                w(tv, "MM", tk)
                evict_v(pn, tv.tensor_copy, out=s_rowsc[:, dc * 128:(dc + 1) * 128], in_=ps)
            tke = cnt["VE"]
            w(ty, "VE", tke)
            dma(ty, out[tti * 128:(tti + 1) * 128, :], s_rowsc[:, 0:D], sem="ST")
            _wst(ty)
        _wst(ty)
        if DBG:
            join_all()
            dma(ty, dbg["xT32"], d_xT32[:, 0:TLOC], sem="ST")
            dma(ty, dbg["st"], d_st_in.rearrange("h a p -> (h a) p"), sem="ST")
            dma(ty, dbg["xmid"], d_xmid_in[0:256, :], sem="ST")
            dma(ty, dbg["cnt"], d_cnt_out[:, :], sem="ST")
            dma(ty, dbg["gtab"], d_gtab_in[0:2 * E, :], sem="ST")
            dma(ty, dbg["gst"], d_gst_in[:, :], sem="ST")
            dma(ty, dbg["rtab"], d_rtab_in[0:128, :], sem="ST")
            dma(ty, dbg["ybuf"], d_ybuf_in[0:128, :], sem="ST")
            dma(ty, dbg["fin"], s_xmid[:, 0:TLOC], sem="ST")
            dma(ty, dbg["slot"], s_slot8[:], sem="ST")
            dma(ty, dbg["stok"], s_stok[:], sem="ST")
            dma(ty, dbg["wec"], s_wec[:], sem="ST")
            dma(ty, dbg["w8"], s_w8[:], sem="ST")
            dma(ty, dbg["val8"], s_val8[:], sem="ST")
            dma(ty, dbg["wn8"], s_wn8[:], sem="ST")
            _wst(ty)
        mark("P8")
        st_p8.close()

        def _replay(rec):
            def f(engine):
                for mname, a, kw, inc, seq in rec.prog:
                    if cutoff is not None and seq > cutoff:
                        continue
                    r = getattr(engine, mname)(*a, **kw)
                    if inc is not None:
                        r.then_inc(*inc)
            return f

        block.sync(_replay(ty))
        block.tensor(_replay(tt))
        block.vector(_replay(tv))
        block.scalar(_replay(ts))
        block.gpsimd(_replay(tg))

    mybir.codegen_inst_isa_subclasses(nc)
    return nc


def kernel(**inputs):
    if "nc" not in _CACHE:
        _CACHE["nc"] = _build()
    nc = _CACHE["nc"]
    in_maps = [_host_inputs(c, inputs) for c in range(NCORES)]
    res = run_bass_kernel_spmd(nc, in_maps, core_ids=list(range(NCORES)))
    outs = [res.results[c]["out"] for c in range(NCORES)]
    full = np.concatenate(outs, axis=0).reshape(2, TGLB // 2, D)
    return full


# version 10
# speedup vs baseline: 1.0418x; 1.0418x over previous
import numpy as np
import ml_dtypes
import concourse.bass as bass
import concourse.mybir as mybir
from concourse.bass_utils import run_bass_kernel_spmd
from concourse.library_config import mlp as _MLP_LIB

F32 = mybir.dt.float32
BF16 = mybir.dt.bfloat16
I16 = mybir.dt.int16
AF = mybir.ActivationFunctionType
ALU = mybir.AluOpType
AX = mybir.AxisListType.X
BF = ml_dtypes.bfloat16

NCORES = 8
TLOC = 2048
TGLB = NCORES * TLOC
D = 1024
H = 8
CH = 64
NCH = TLOC // CH
NTT = TLOC // 128
E = 64
TOPK = 8
CAP = 512
EO = 8
NSLOT = EO * CAP
F = 768
SF = 768
RTW = 64
EPS_LN = 1e-5

_CACHE = {}
DBG = False
_MARKS = {}


def _host_inputs(core, inp):
    xf = inp["L0_attn_x"].reshape(TGLB, D)
    m = {
        "x_loc": np.ascontiguousarray(xf[core * TLOC:(core + 1) * TLOC]),
        "wq": inp["L0_attn_Wq"], "wk": inp["L0_attn_Wk"], "wv": inp["L0_attn_Wv"],
        "wg": inp["L0_attn_Wg"], "wo": inp["L0_attn_Wo"],
        "rwt": np.ascontiguousarray(inp["L0_ffn_router_w"].T),
        "wsgu": inp["L0_ffn_ws_gate_up"], "wsd": inp["L0_ffn_ws_down"],
        "weg": np.ascontiguousarray(inp["L0_ffn_w_gate"][core * EO:(core + 1) * EO]),
        "weu": np.ascontiguousarray(inp["L0_ffn_w_up"][core * EO:(core + 1) * EO]),
        "wed": np.ascontiguousarray(inp["L0_ffn_w_down"][core * EO:(core + 1) * EO]),
    }
    m["ident32"] = np.eye(128, dtype=np.float32)
    m["ident16"] = np.eye(128, dtype=BF)
    tri = np.zeros((128, 129), np.float32)
    tri[:, :128] = np.triu(np.ones((128, 128), np.float32), 1)
    tri[:, 128] = 1.0
    m["trib"] = tri.astype(BF)
    m["ones128b"] = np.ones((128, 128), dtype=BF)
    m["ones128f"] = np.ones((128, 128), dtype=np.float32)
    ms = np.ones((128, TLOC), np.float32)
    ms[:, ::CH] = 0.0
    m["mask_scan"] = ms.astype(BF)
    mA = np.triu(np.ones((CH, CH), np.float32), 0)
    m["maskA"] = np.concatenate([mA, mA], axis=0).copy()
    m["iota_e"] = np.tile(np.arange(E, dtype=np.float32)[None, :], (128, 1))
    ev = np.arange(E)
    gv = ((ev % 8) // 2) * 16 + (ev // 8) * 2 + (ev % 8) % 2
    m["giota_e"] = np.tile(gv.astype(np.float32)[None, :], (128, 1))
    m["msel"] = np.tile((np.arange(8) < core).astype(np.float32)[None, :], (128, 1))
    cb = (np.arange(128) % 16)[:, None]
    m["ctens"] = (cb * 32 + np.arange(32)[None, :]).astype(np.float32)
    m["ecol"] = (EO * core + np.arange(128) // 16).astype(np.float32)[:, None].copy()
    gs = np.zeros((128, 64), np.int16)
    for i in range(1024):
        blk, p = i // 128, i % 128
        gs[i % 16, i // 16] = blk * 64 + EO * core + p // 16
    gs[16:32] = gs[0:16]
    m["gsidx"] = gs
    return m


def _build(cutoff=None):
    nc = bass.Bass("TRN2", target_bir_lowering=False, num_swdge_queues=1)

    def din(name, shape, dt=F32):
        return nc.dram_tensor(name, list(shape), dt, kind="ExternalInput").ap()

    x_loc = din("x_loc", (TLOC, D))
    wq = din("wq", (D, D)); wk = din("wk", (D, D)); wv = din("wv", (D, D))
    wg = din("wg", (D, D)); wo = din("wo", (D, D))
    rwt = din("rwt", (D, E))
    wsgu = din("wsgu", (D, 2 * SF)); wsd = din("wsd", (SF, D))
    weg = din("weg", (EO, D, F)); weu = din("weu", (EO, D, F)); wed = din("wed", (EO, F, D))
    ident32 = din("ident32", (128, 128))
    ident16 = din("ident16", (128, 128), BF16)
    trib = din("trib", (128, 129), BF16)
    ones128b = din("ones128b", (128, 128), BF16)
    ones128f = din("ones128f", (128, 128))
    mask_scan = din("mask_scan", (128, TLOC), BF16)
    maskA = din("maskA", (128, CH))
    iota_e = din("iota_e", (128, E))
    giota_e = din("giota_e", (128, E))
    msel = din("msel", (128, 8))
    ctens = din("ctens", (128, 32))
    ecol = din("ecol", (128, 1))
    gsidx = din("gsidx", (128, 64), I16)

    out = nc.dram_tensor("out", [TLOC, D], F32, kind="ExternalOutput").ap()

    def dint(name, shape, dt=F32):
        return nc.dram_tensor(name, list(shape), dt).ap()

    d_xT32 = dint("d_xT32", (128, 8 * TLOC))
    d_st_in = dint("d_st_in", (H, 129, 128))
    d_st_out = nc.dram_tensor("d_st_out", [NCORES * H, 129, 128], F32, addr_space="Shared").ap()
    d_cnt_in = dint("d_cnt_in", (1, E))
    d_cnt_out = nc.dram_tensor("d_cnt_out", [NCORES, E], F32, addr_space="Shared").ap()
    d_gst_in = dint("d_gst_in", (E, 64))
    d_gst_out = nc.dram_tensor("d_gst_out", [NCORES * E, 64], F32, addr_space="Shared").ap()
    d_gtab_in = dint("d_gtab_in", (NTT * E, 128))
    d_gtab_out = nc.dram_tensor("d_gtab_out", [128 * E, 128], F32, addr_space="Shared").ap()
    d_xmid_in = dint("d_xmid_in", (TLOC, D), BF16)
    d_xmid_out = nc.dram_tensor("d_xmid_out", [TGLB, D], BF16, addr_space="Shared").ap()
    d_rtab_in = dint("d_rtab_in", (TLOC, RTW))
    d_rtab_out = nc.dram_tensor("d_rtab_out", [TGLB, RTW], F32, addr_space="Shared").ap()
    d_ybuf_in = dint("d_ybuf_in", (NSLOT, D), BF16)
    d_ybuf_out = nc.dram_tensor("d_ybuf_out", [NCORES * NSLOT, D], BF16, addr_space="Shared").ap()
    d_scri = dint("d_scri", (128, 256), I16)
    d_scrf = dint("d_scrf", (128, 64))
    dbg = {}
    if DBG:
        def dout(name, shape, dt=F32):
            return nc.dram_tensor(name, list(shape), dt, kind="ExternalOutput").ap()
        dbg["xT32"] = dout("dbg_xT32", (128, TLOC))
        dbg["st"] = dout("dbg_st", (H * 129, 128))
        dbg["xmid"] = dout("dbg_xmid", (256, D), BF16)
        dbg["cnt"] = dout("dbg_cnt", (NCORES, E))
        dbg["gtab"] = dout("dbg_gtab", (2 * E, 128))
        dbg["gst"] = dout("dbg_gst", (E, 64))
        dbg["rtab"] = dout("dbg_rtab", (128, RTW))
        dbg["ybuf"] = dout("dbg_ybuf", (128, D), BF16)
        dbg["fin"] = dout("dbg_fin", (128, TLOC))
        dbg["slot"] = dout("dbg_slot", (128, NTT * 8))
        dbg["stok"] = dout("dbg_stok", (128, 32))
        dbg["wec"] = dout("dbg_wec", (128, 32))
        dbg["w8"] = dout("dbg_w8", (128, NTT * 8))
        dbg["we"] = dout("dbg_we", (128, 32))
        dbg["rt"] = dout("dbg_rt", (128, 32 * RTW))
        dbg["val8"] = dout("dbg_val8", (128, NTT * 8))
        dbg["wn8"] = dout("dbg_wn8", (128, NTT * 8))

    from contextlib import ExitStack
    ctx = ExitStack()
    with ctx:
        def sbuf(name, shape, dt=F32, stack=None):
            cm = nc.sbuf_tensor(name, list(shape), dt)
            return (stack or ctx).enter_context(cm)

        def psum(name, shape, dt=F32):
            return ctx.enter_context(nc.psum_tensor(name, list(shape), dt))

        s_xmid = sbuf("s_xmid", (128, 8 * TLOC), F32)
        s_id32 = sbuf("s_id32", (128, 128))
        s_id16 = sbuf("s_id16", (128, 128), BF16)
        s_trib = sbuf("s_trib", (128, 129), BF16)
        s_ones128b = sbuf("s_ones128b", (128, 128), BF16)
        s_ones128f = sbuf("s_ones128f", (128, 128))
        s_maskA = sbuf("s_maskA", (128, CH))
        s_iotae = sbuf("s_iotae", (128, E))
        s_giotae = sbuf("s_giotae", (128, E))
        s_g8 = sbuf("s_g8", (128, NTT * 8))
        s_msel = sbuf("s_msel", (128, 8))
        s_ctens = sbuf("s_ctens", (128, 32))
        s_ecol = sbuf("s_ecol", (128, 1))
        s_gsidx = sbuf("s_gsidx", (128, 64), I16)
        s_rowsc = sbuf("s_rowsc", (128, 2 * D))
        s_f32a = sbuf("s_f32a", (128, 1024))
        s_f32b = sbuf("s_f32b", (128, 1024))
        s_w8 = sbuf("s_w8", (128, NTT * 8))
        s_e8 = sbuf("s_e8", (128, NTT * 8))
        s_pos8 = sbuf("s_pos8", (128, NTT * 8))
        s_val8 = sbuf("s_val8", (128, NTT * 8))
        s_wn8 = sbuf("s_wn8", (128, NTT * 8))
        s_slot8 = sbuf("s_slot8", (128, NTT * 8))
        s_slot8i = sbuf("s_slot8i", (128, NTT * 8), I16)
        s_ninv = sbuf("s_ninv", (128, NTT))
        s_ninvb = sbuf("s_ninvb", (128, NTT), BF16)
        s_ninvrow = sbuf("s_ninvrow", (1, TLOC), BF16)
        s_yidx = sbuf("s_yidx", (128, TLOC // 2), I16)
        s_sidx = sbuf("s_sidx", (128, 256), I16)
        s_sidx2 = sbuf("s_sidx2", (128, 256), I16)
        s_gidx = sbuf("s_gidx", (128, 256), I16)
        s_kstar = sbuf("s_kstar", (128, 32))
        s_kstari = sbuf("s_kstari", (128, 32), I16)
        s_stok = sbuf("s_stok", (128, 32))
        s_stoki = sbuf("s_stoki", (128, 32), I16)
        s_we = sbuf("s_we", (128, 32))
        s_wec = sbuf("s_wec", (128, 32))
        s_totc = sbuf("s_totc", (128, 1))
        s_base = sbuf("s_base", (128, E))
        s_cnt_all = sbuf("s_cnt_all", (NCORES, E))
        s_gstc = sbuf("s_gstc", (64, 17))
        s_bgl = sbuf("s_bgl", (64, 1))
        s_y0 = sbuf("s_y0", (1, D), BF16)
        s_colsc = sbuf("s_colsc", (128, 8))
        s_epscol = sbuf("s_epscol", (128, 1))
        s_dcol = sbuf("s_dcol", (128, 8))

        pnames = ["pA", "pB", "pC", "pD", "pS", "pT", "pU", "pV"]
        P = {n: psum(n, (128, 512)) for n in pnames}

        KROT = 3
        semnames = ["MM", "VE", "SC", "CC"]
        for base in ["LD", "ST", "GP", "GL", "GM"]:
            semnames += [f"{base}{i}" for i in range(KROT)]
        sems = {n: ctx.enter_context(nc.semaphore(n)) for n in semnames}
        cnt = {n: 0 for n in sems}
        for base in ["LD", "ST", "GP", "GL", "GM"]:
            cnt[base] = 0
        rot = {"LD": 0, "ST": 0, "GP": 0, "GL": 0, "GM": 0}

        block = ctx.enter_context(nc.Block())

        seqc = [0]

        class _Rec:
            __slots__ = ("prog",)

            def __init__(self):
                self.prog = []

            def __getattr__(self, mname):
                prog = self.prog

                def call(*a, **kw):
                    seqc[0] += 1
                    item = [mname, a, kw, None, seqc[0]]
                    prog.append(item)

                    class _Then:
                        def then_inc(self, sem_, n_):
                            item[3] = (sem_, n_)
                    return _Then()
                return call

        ty, tt, tv, ts, tg = _Rec(), _Rec(), _Rec(), _Rec(), _Rec()

        def mark(name):
            _MARKS[name] = seqc[0]

        def bump(name, n=1):
            cnt[name] += n
            return cnt[name]

        def dma(e_, out_, in_, sem="LD", **kw):
            if e_ is tg and sem == "LD":
                sem = "GL"
            kw.setdefault("allow_slow_non_contiguous", True)
            i = rot[sem]
            rot[sem] = (i + 1) % KROT
            n_ = f"{sem}{i}"
            if cnt[n_] > 0:
                e_.wait_ge(sems[n_], cnt[n_])
            e_.dma_start(out=out_, in_=in_, **kw).then_inc(sems[n_], 16)
            bump(n_, 16)
            bump(sem, 16)
            return cnt[n_]

        bank_free = {}

        def wbank(n):
            if n in bank_free:
                s_, t_ = bank_free[n]
                tt.wait_ge(sems[s_], t_)

        def mm(pn, out_, lhsT, rhs, start=True, stop=True):
            if start:
                wbank(pn)
            tt.matmul(out_, lhsT, rhs, start=start, stop=stop).then_inc(sems["MM"], 1)
            return bump("MM")

        def trans(pn, out_, in_, ident):
            wbank(pn)
            tt.transpose(out_, in_, ident).then_inc(sems["MM"], 1)
            return bump("MM")

        def vop(fn, *a, **kw):
            if cnt["VE"] > 0:
                tv.wait_ge(sems["VE"], cnt["VE"])
            fn(*a, **kw).then_inc(sems["VE"], 1)
            return bump("VE")

        def sop(fn, *a, **kw):
            if cnt["SC"] > 0:
                ts.wait_ge(sems["SC"], cnt["SC"])
            fn(*a, **kw).then_inc(sems["SC"], 1)
            return bump("SC")

        def evict_v(pn, fn, *a, **kw):
            t_ = vop(fn, *a, **kw)
            bank_free[pn] = ("VE", t_)
            return t_

        def evict_s(pn, fn, *a, **kw):
            t_ = sop(fn, *a, **kw)
            bank_free[pn] = ("SC", t_)
            return t_

        DMA_SEMS = ("LD", "ST", "GL", "GP", "GM")

        def w(e_, sem, tick):
            if sem in DMA_SEMS:
                for i in range(KROT):
                    n_ = f"{sem}{i}"
                    if cnt[n_] > 0:
                        e_.wait_ge(sems[n_], cnt[n_])
                return
            if tick > 0:
                e_.wait_ge(sems[sem], tick)

        def _wst(e_):
            w(e_, "ST", 0)

        def _wld(e_):
            w(e_, "LD", 0)

        def gather(out_ap, in_ap, idxs_ap, n_idx):
            i = rot["GP"]
            rot["GP"] = (i + 1) % KROT
            n_ = f"GP{i}"
            if cnt[n_] > 0:
                tg.wait_ge(sems[n_], cnt[n_])
            tg.dma_gather(out_ap=out_ap, in_ap=in_ap, idxs_ap=idxs_ap,
                          num_idxs=n_idx, num_idxs_reg=n_idx,
                          elem_size=in_ap.shape[-1]).then_inc(sems[n_], 16)
            bump(n_, 16)
            bump("GP", 16)
            return cnt[n_]

        def allgather(din_, dout_, tick_st):
            w(tg, "ST", tick_st)
            tg.collective_compute(
                "AllGather", ALU.bypass, replica_groups=[list(range(NCORES))],
                ins=[din_], outs=[dout_]).then_inc(sems["CC"], 1)
            return bump("CC")

        def join_all():
            snap = dict(cnt)
            for e_ in (ty, tt, tv, ts, tg):
                for sname in semnames:
                    if snap[sname] > 0:
                        e_.wait_ge(sems[sname], snap[sname])

        st_xT = ExitStack()
        s_xT = sbuf("s_xT", (128, 8 * TLOC), BF16, stack=st_xT)
        st_osb = ExitStack()
        s_osb = sbuf("s_osb", (128, H * TLOC), BF16, stack=st_osb)
        st_attn = ExitStack()
        s_q = sbuf("s_q", (128, TLOC), stack=st_attn)
        s_k = sbuf("s_k", (128, TLOC), stack=st_attn)
        s_lam = sbuf("s_lam", (128, TLOC), stack=st_attn)
        s_eq = sbuf("s_eq", (128, TLOC), stack=st_attn)
        s_ek = sbuf("s_ek", (128, TLOC), stack=st_attn)
        xbv = s_xmid[:].bitcast(BF16)

        def carve16(b0, cols):
            return xbv[:, b0 // 2:b0 // 2 + cols]

        def carve32(b0, cols):
            return s_xmid[:, b0 // 4:b0 // 4 + cols]

        s_qhat = xbv[:, 0:16384]
        s_qd = carve16(32768, TLOC)
        s_kd = carve16(32768 + 4096, TLOC)
        s_khT = carve16(32768 + 8192, TLOC)
        s_v = carve16(32768 + 12288, TLOC)
        s_mscan = carve16(32768 + 16384, TLOC)
        s_wslice = carve16(32768 + 20480, 1024)
        s_sti = carve32(32768 + 22528, 2 * 128)
        s_state = carve32(32768 + 23552, 128)
        s_lend = carve32(32768 + 24064, NCH)
        s_eend = carve32(32768 + 24192, NCH)
        s_linc = carve32(32768 + 24320, NCH)
        s_small = carve32(32768 + 24448, NCH)
        s_sin = carve32(32768 + 24576, 128)
        s_sinb = carve16(32768 + 25088, 128)
        s_am = carve16(32768 + 25344, 16 * CH)
        s_stateb = carve16(32768 + 29696, 128)
        s_wslice2 = carve16(32768 + 29952, 1024)

        for dst, src in [(s_id32[:], ident32), (s_id16[:], ident16),
                         (s_trib[:], trib), (s_ones128b[:], ones128b), (s_ones128f[:], ones128f),
                         (s_mscan[:], mask_scan), (s_maskA[:], maskA),
                         (s_iotae[:], iota_e), (s_giotae[:], giota_e),
                         (s_msel[:], msel),
                         (s_ctens[:], ctens), (s_ecol[:], ecol),
                         (s_gsidx[:], gsidx)]:
            dma(ty, dst, src)
        t_const = cnt["LD"]
        vop(tv.memset, s_epscol[:], EPS_LN)
        vop(tv.memset, s_yidx[:], 0)
        vop(tv.memset, s_sidx[:], 0)
        vop(tv.memset, s_sidx2[:], 0)
        vop(tv.memset, s_gidx[:], 0)
        tg.load_library(_MLP_LIB)

        for tti in range(NTT):
            buf = s_rowsc[:, 0:D]
            stg = s_rowsc[:, D:2 * D]
            lt = dma(ty, buf, x_loc[tti * 128:(tti + 1) * 128, :])
            w(tv, "ST", cnt["ST"])
            for dc in range(8):
                pn = "pT" if dc % 2 == 0 else "pU"
                ps = P[pn][:, 0:128]
                w(tt, "LD", lt)
                trans(pn, ps, buf[:, dc * 128:(dc + 1) * 128], s_id32[:])
                tk = cnt["MM"]
                w(tv, "MM", tk)
                vop(tv.tensor_copy, out=s_xT[:, dc * TLOC + tti * 128:
                                             dc * TLOC + (tti + 1) * 128], in_=ps)
                evict_v(pn, tv.tensor_copy, out=stg[:, dc * 128:(dc + 1) * 128], in_=ps)
            tke = cnt["VE"]
            w(ty, "VE", tke)
            dma(ty, d_xT32.rearrange("p (c t) -> p c t", t=TLOC)[:, :, tti * 128:(tti + 1) * 128],
                stg.rearrange("p (c q) -> p c q", q=128), sem="ST")
            _wst(ty)
        t_xT_all = cnt["VE"]

        mark("P0")
        wseq = []
        for hh in range(H):
            for wt_ in (wg, wq, wk, wv):
                wseq.append((wt_, hh * 128))
        wbufs = [s_wslice, s_wslice2]
        wl = {"i": 0, "c": 0, "use": [0, 0]}

        def wissue():
            i = wl["i"]
            if i >= len(wseq):
                return
            wt_, hof_ = wseq[i]
            par = i % 2
            w(tg, "MM", wl["use"][par])
            dma(tg, wbufs[par][:].rearrange("p (c f) -> p c f", f=128),
                wt_[:, hof_:hof_ + 128].rearrange("(c p) f -> p c f", p=128),
                sem="GL" if par == 0 else "GM")
            wl["i"] = i + 1

        def wopen():
            par = wl["c"] % 2
            w(tt, "GL" if par == 0 else "GM", 0)
            return wbufs[par]

        def wclose():
            par = wl["c"] % 2
            wl["use"][par] = cnt["MM"]
            wl["c"] += 1
            wissue()

        wissue()
        wissue()
        pendC = []

        def drainC(n=None):
            k = len(pendC) if n is None else min(n, len(pendC))
            for _ in range(k):
                pendC.pop(0)()

        for h in range(H):
            hof = h * 128

            def fproj(wt, dst, bks):
                wsl = wopen()
                w(tt, "VE", t_xT_all)
                for tc in range(4):
                    pn = bks[tc % 2]
                    ps = P[pn][:, 0:512]
                    for dc in range(8):
                        mm(pn, ps, wsl[:, dc * 128:(dc + 1) * 128],
                           s_xT[:, dc * TLOC + tc * 512:dc * TLOC + (tc + 1) * 512],
                           start=(dc == 0), stop=(dc == 7))
                    tk = cnt["MM"]
                    w(ts, "MM", tk)
                    evict_s(pn, ts.copy, out=dst[:, tc * 512:(tc + 1) * 512], in_=ps)
                    drainC(3)
                wclose()

            fproj(wg, s_lam, ("pA", "pB"))
            sop(ts.activation, out=s_lam[:], in_=s_lam[:], func=AF.Sigmoid)
            t_sig = sop(ts.activation, out=s_lam[:], in_=s_lam[:], func=AF.Ln)
            fproj(wq, s_q, ("pC", "pD"))
            fproj(wk, s_k, ("pA", "pB"))
            drainC(None)
            tC_last = cnt["MM"]
            t_qkg = cnt["SC"]
            w(tv, "SC", t_sig)
            w(tv, "LD", t_const)
            vop(tv.tensor_tensor_scan, out=s_lam[:], data0=s_mscan[:],
                data1=s_lam[:], initial=0.0, op0=ALU.mult, op1=ALU.add)
            lam3 = s_lam[:].rearrange("p (c k) -> p c k", k=CH)
            vop(tv.tensor_copy, out=s_lend[:], in_=lam3[:, :, CH - 1])
            t_lend = cnt["VE"]
            w(ts, "VE", t_lend)
            t_eend = sop(ts.activation, out=s_eend[:], in_=s_lend[:], func=AF.Exp)
            w(ts, "MM", tC_last)
            sop(ts.activation, out=s_qd[:], in_=s_q[:], func=AF.Square)
            t_q2 = cnt["SC"]
            sop(ts.activation, out=s_kd[:], in_=s_k[:], func=AF.Square)
            t_k2 = cnt["SC"]
            vop(tv.tensor_copy, out=s_linc[:], in_=s_lend[:])
            srcb, dstb = s_linc, s_small
            sh = 1
            while sh < NCH:
                vop(tv.tensor_copy, out=dstb[:, 0:sh], in_=srcb[:, 0:sh])
                vop(tv.tensor_add, out=dstb[:, sh:NCH], in0=srcb[:, sh:NCH],
                    in1=srcb[:, 0:NCH - sh])
                srcb, dstb = dstb, srcb
                sh *= 2
            if srcb is not s_linc:
                vop(tv.tensor_copy, out=s_linc[:], in_=srcb[:])
            vop(tv.tensor_sub, out=s_small[:], in0=s_linc[:], in1=s_lend[:])
            t_lexc = cnt["VE"]
            w(ts, "VE", t_lexc)
            t_elexc = sop(ts.activation, out=s_small[:], in_=s_small[:], func=AF.Exp)
            w(ts, "ST", cnt["ST"])
            sop(ts.activation, out=s_colsc[:, 0:1], in_=s_linc[:, NCH - 1:NCH],
                func=AF.Exp)
            w(ty, "SC", cnt["SC"])
            dma(ty, d_st_in[h, 128:129, :].rearrange("a p -> p a"),
                s_colsc[:, 0:1], sem="ST")

            wsl = wopen()
            for tti in range(NTT):
                pn = "pC" if tti % 2 == 0 else "pD"
                ps = P[pn][:, 0:128]
                for dc in range(8):
                    mm(pn, ps,
                       s_xT[:, dc * TLOC + tti * 128:dc * TLOC + (tti + 1) * 128],
                       wsl[:, dc * 128:(dc + 1) * 128],
                       start=(dc == 0), stop=(dc == 7))
                tk = cnt["MM"]
                w(tv, "MM", tk)
                evict_v(pn, tv.tensor_copy, out=s_v[:, tti * 128:(tti + 1) * 128], in_=ps)
            wclose()

            w(tt, "SC", t_q2)
            for tc in range(4):
                pn = "pA" if tc % 2 == 0 else "pB"
                ps = P[pn][:, 0:512]
                mm(pn, ps, s_ones128b[:], s_qd[:, tc * 512:(tc + 1) * 512])
                tk = cnt["MM"]
                w(ts, "MM", tk)
                evict_s(pn, ts.activation, out=s_eq[:, tc * 512:(tc + 1) * 512],
                        in_=ps, func=AF.Ln)
            w(tt, "SC", t_k2)
            for tc in range(4):
                pn = "pA" if tc % 2 == 0 else "pB"
                ps = P[pn][:, 0:512]
                mm(pn, ps, s_ones128b[:], s_kd[:, tc * 512:(tc + 1) * 512])
                tk = cnt["MM"]
                w(ts, "MM", tk)
                evict_s(pn, ts.activation, out=s_ek[:, tc * 512:(tc + 1) * 512],
                        in_=ps, func=AF.Ln)
            t_lnsk = cnt["SC"]

            w(tv, "SC", t_lnsk)
            t_kd_half = [0, 0]
            t_eqk = 0
            for hf in (0, 1):
                sl = slice(hf * 1024, (hf + 1) * 1024)
                vop(tv.scalar_tensor_tensor, out=s_eq[:, sl], in0=s_eq[:, sl],
                    scalar=-0.5, in1=s_lam[:, sl], op0=ALU.mult, op1=ALU.add)
                vop(tv.scalar_tensor_tensor, out=s_ek[:, sl], in0=s_ek[:, sl],
                    scalar=-0.5, in1=s_lam[:, sl], op0=ALU.mult, op1=ALU.subtract)
                t_lams = cnt["VE"]
                w(ts, "VE", t_lams)
                sop(ts.activation, out=s_eq[:, sl], in_=s_eq[:, sl], func=AF.Exp)
                t_eqk = sop(ts.activation, out=s_ek[:, sl], in_=s_ek[:, sl],
                            func=AF.Exp)
                w(tv, "SC", t_eqk)
                vop(tv.tensor_mul, out=s_qd[:, sl], in0=s_q[:, sl], in1=s_eq[:, sl])
                vop(tv.tensor_mul, out=s_kd[:, sl], in0=s_k[:, sl], in1=s_ek[:, sl])
                t_kd_half[hf] = cnt["VE"]
            eend3 = s_eend[:].unsqueeze(2).broadcast_to([128, NCH, CH])
            ek3 = s_ek[:].rearrange("p (c k) -> p c k", k=CH)
            lam3b = s_lam[:].rearrange("p (c k) -> p c k", k=CH)
            vop(tv.tensor_tensor, out=lam3b, in0=ek3, in1=eend3, op=ALU.mult)
            vop(tv.tensor_mul, out=s_lam[:], in0=s_k[:], in1=s_lam[:])
            t_khf = cnt["VE"]
            w(tv, "SC", t_elexc)
            sm3 = s_small[:].unsqueeze(2).broadcast_to([128, NCH, CH])
            eq3 = s_eq[:].rearrange("p (c k) -> p c k", k=CH)
            vop(tv.tensor_tensor, out=eq3, in0=eq3, in1=sm3, op=ALU.mult)
            vop(tv.tensor_mul, out=s_qhat[:, h * TLOC:(h + 1) * TLOC],
                in0=s_q[:], in1=s_eq[:])
            t_qhat = cnt["VE"]

            w(tt, "VE", t_kd_half[0])
            pnr = ["pA", "pB", "pC", "pD"]
            for c in range(NCH):
                if c == 16:
                    w(tt, "VE", t_kd_half[1])
                cof = c * CH
                base = (c % 2) * 64
                pnA = pnr[c % 4]
                psA = P[pnA][base:base + 64, 0:CH]
                tkA = mm(pnA, psA, s_kd[:, cof:cof + CH], s_qd[:, cof:cof + CH])
                amb = s_am[base:base + 64, (c // 2) * CH:(c // 2 + 1) * CH]
                w(tv, "MM", tkA)
                evict_v(pnA, tv.tensor_tensor, out=amb, in0=psA,
                        in1=s_maskA[base:base + 64, :], op=ALU.mult)
            t_amall = cnt["VE"]
            w(tt, "VE", t_khf)
            for tti in range(NTT):
                pn = "pS" if tti % 2 == 0 else "pT"
                ps = P[pn][:, 0:128]
                trans(pn, ps, s_lam[:, tti * 128:(tti + 1) * 128], s_id32[:])
                tk = cnt["MM"]
                w(tv, "MM", tk)
                evict_v(pn, tv.tensor_copy, out=s_khT[:, tti * 128:(tti + 1) * 128], in_=ps)
            t_khT = cnt["VE"]
            w(tt, "VE", t_amall)
            for c in range(NCH):
                cof = c * CH
                base = (c % 2) * 64
                tti_ = c // 2
                pnO = pnr[c % 4]
                psO = P[pnO][:, 0:CH]
                vsl = s_v[base:base + 64, tti_ * 128:(tti_ + 1) * 128]
                amb = s_am[base:base + 64, tti_ * CH:(tti_ + 1) * CH]
                tkO = mm(pnO, psO, vsl, amb)
                w(tv, "MM", tkO)
                evict_v(pnO, tv.tensor_copy,
                        out=s_osb[:, h * TLOC + cof:h * TLOC + cof + CH], in_=psO)
            t_ointra = cnt["VE"]
            stD = {"prev": 0}

            def _copen(t_oi=t_ointra, D=stD):
                w(tv, "ST", cnt["ST"])
                vop(tv.memset, s_state[:], 0.0)
                vop(tv.memset, s_stateb[:], 0.0)
                w(tt, "VE", t_oi)
                D["prev"] = cnt["VE"]

            pendC.append(_copen)
            for c in range(NCH):
                def _cchunk(c=c, hx=h, D=stD):
                    cof = c * CH
                    base = (c % 2) * 64
                    tti_ = c // 2
                    khsl = s_khT[base:base + 64, tti_ * 128:(tti_ + 1) * 128]
                    vsl = s_v[base:base + 64, tti_ * 128:(tti_ + 1) * 128]
                    if c > 0:
                        pnI = "pU" if c % 2 == 0 else "pV"
                        psI = P[pnI][:, 0:CH]
                        wbank(pnI)
                        w(tt, "VE", D["prev"])
                        tt.matmul(psI, s_stateb[:], s_qd[:, cof:cof + CH],
                                  start=True, stop=True).then_inc(sems["MM"], 1)
                        bump("MM")
                        tkI = cnt["MM"]
                    pnS = "pS" if c % 2 == 0 else "pT"
                    psS = P[pnS][:, 128:256]
                    tkS = mm(pnS, psS, khsl, vsl)
                    w(tv, "MM", tkS)
                    vop(tv.scalar_tensor_tensor, out=s_state[:], in0=s_state[:],
                        scalar=s_eend[:, c:c + 1], in1=psS,
                        op0=ALU.mult, op1=ALU.add)
                    t_st = vop(tv.tensor_copy, out=s_stateb[:], in_=s_state[:])
                    bank_free[pnS] = ("VE", t_st)
                    if c > 0:
                        w(tv, "MM", tkI)
                        evict_v(pnI, tv.tensor_tensor,
                                out=s_osb[:, hx * TLOC + cof:hx * TLOC + cof + CH],
                                in0=s_osb[:, hx * TLOC + cof:hx * TLOC + cof + CH],
                                in1=psI, op=ALU.add)
                    D["prev"] = t_st

                pendC.append(_cchunk)

            def _cstore(hx=h):
                w(ty, "VE", cnt["VE"])
                dma(ty, d_st_in[hx, 0:128, :], s_state[:], sem="ST")

            pendC.append(_cstore)
        drainC(None)
        t_p1_st = cnt["ST"]

        mark("P1")
        t_cc_st = allgather(d_st_in[:, :, :], d_st_out[:, :, :], t_p1_st)
        w(ty, "CC", t_cc_st)
        t_corr_mm = 0
        t_sti_use = 0
        for h in range(H):
            w(ty, "VE", t_sti_use)
            for i in range(7):
                dma(ty, s_dcol[:, i:i + 1],
                    d_st_out[i * H + h, 128:129, :].rearrange("a p -> p a"))
            l0 = dma(ty, s_sti[:, 0:128], d_st_out[0 * H + h, 0:128, :])
            w(tv, "LD", l0)
            w(tv, "MM", t_corr_mm)
            t_half = [0, 0]
            t_half[0] = vop(tv.tensor_scalar, out=s_sin[:], in0=s_sti[:, 0:128],
                            scalar1=s_msel[:, 0:1], scalar2=None, op0=ALU.mult)
            for i in range(1, 7):
                sl_ = s_sti[:, (i % 2) * 128:(i % 2) * 128 + 128]
                w(ty, "VE", t_half[i % 2])
                li = dma(ty, sl_, d_st_out[i * H + h, 0:128, :])
                vop(tv.tensor_scalar, out=s_colsc[:, 7:8], in0=s_dcol[:, i:i + 1],
                    scalar1=-1.0, scalar2=None, op0=ALU.add)
                vop(tv.tensor_scalar, out=s_colsc[:, 7:8], in0=s_colsc[:, 7:8],
                    scalar1=s_msel[:, i:i + 1], scalar2=1.0, op0=ALU.mult, op1=ALU.add)
                vop(tv.tensor_scalar, out=s_sin[:], in0=s_sin[:],
                    scalar1=s_colsc[:, 7:8], scalar2=None, op0=ALU.mult)
                w(tv, "LD", li)
                t_half[i % 2] = vop(tv.scalar_tensor_tensor, out=s_sin[:], in0=sl_,
                                    scalar=s_msel[:, i:i + 1], in1=s_sin[:],
                                    op0=ALU.mult, op1=ALU.add)
                t_sti_use = t_half[i % 2]
            t_sin = vop(tv.tensor_copy, out=s_sinb[:], in_=s_sin[:])
            t_sti_use = t_sin
            w(tt, "VE", t_sin)
            for tc in range(4):
                pn = "pA" if tc % 2 == 0 else "pB"
                ps = P[pn][:, 0:512]
                mm(pn, ps, s_sinb[:],
                   s_qhat[:, h * TLOC + tc * 512:h * TLOC + (tc + 1) * 512])
                tk = cnt["MM"]
                t_corr_mm = tk
                w(tv, "MM", tk)
                evict_v(pn, tv.tensor_tensor,
                        out=s_osb[:, h * TLOC + tc * 512:h * TLOC + (tc + 1) * 512],
                        in0=s_osb[:, h * TLOC + tc * 512:h * TLOC + (tc + 1) * 512],
                        in1=ps, op=ALU.add)
        t_p2 = cnt["VE"]

        mark("P2")
        join_all()
        st_attn.close()
        st_wo = ExitStack()
        s_wob = sbuf("s_wob", (128, 8 * 1024), BF16, stack=st_wo)

        lwo = dma(tg, s_wob[:].rearrange("p (c f) -> p c f", f=1024),
                  wo.rearrange("(c p) f -> p c f", p=128))
        w(tt, "GL", lwo)
        w(tt, "VE", t_p2)
        w(ty, "MM", cnt["MM"])
        w(ty, "VE", cnt["VE"])
        w(ty, "SC", cnt["SC"])
        for dt_ in range(8):
            dma(ty, s_xmid[:, dt_ * TLOC:(dt_ + 1) * TLOC],
                d_xT32[:, dt_ * TLOC:(dt_ + 1) * TLOC])
        pns4 = ["pA", "pB", "pC", "pD"]
        for dt_ in range(8):
            for hv in range(8):
                for tc in range(4):
                    mm(pns4[tc], P[pns4[tc]][:, 0:512],
                       s_wob[:, hv * 1024 + dt_ * 128:hv * 1024 + dt_ * 128 + 128],
                       s_osb[:, hv * TLOC + tc * 512:hv * TLOC + (tc + 1) * 512],
                       start=(hv == 0), stop=(hv == 7))
            tk = cnt["MM"]
            w(tv, "MM", tk)
            w(tv, "LD", 0)
            for tc in range(4):
                xsl = s_xmid[:, dt_ * TLOC + tc * 512:dt_ * TLOC + (tc + 1) * 512]
                t_ev = evict_v(pns4[tc], tv.tensor_tensor, out=xsl, in0=xsl,
                               in1=P[pns4[tc]][:, 0:512], op=ALU.add)
        t_resid = cnt["VE"]

        def layernorm_inplace(t_ready):
            w(ts, "VE", t_ready)
            t_done = 0
            for tc in range(4):
                pnS = "pC" if tc % 2 == 0 else "pD"
                psS = P[pnS][:, 0:512]
                pnQ = "pS" if tc % 2 == 0 else "pT"
                psQ = P[pnQ][:, 0:512]
                for dt_ in range(8):
                    sq_t = sop(ts.activation, out=s_f32b[:, 0:512],
                               in_=s_xmid[:, dt_ * TLOC + tc * 512:dt_ * TLOC + (tc + 1) * 512],
                               func=AF.Square)
                    w(tt, "VE", t_ready)
                    mm(pnS, psS, s_ones128f[:],
                       s_xmid[:, dt_ * TLOC + tc * 512:dt_ * TLOC + (tc + 1) * 512],
                       start=(dt_ == 0), stop=(dt_ == 7))
                    w(tt, "SC", sq_t)
                    mm(pnQ, psQ, s_ones128f[:], s_f32b[:, 0:512],
                       start=(dt_ == 0), stop=(dt_ == 7))
                    ts.wait_ge(sems["MM"], cnt["MM"])
                tkq_ = cnt["MM"]
                w(tv, "MM", tkq_)
                vop(tv.tensor_scalar, out=s_f32a[:, 0:512], in0=psS,
                    scalar1=1.0 / D, scalar2=None, op0=ALU.mult)
                bank_free[pnS] = ("VE", cnt["VE"])
                vop(tv.tensor_scalar, out=s_f32b[:, 0:512], in0=psQ,
                    scalar1=1.0 / D, scalar2=None, op0=ALU.mult)
                bank_free[pnQ] = ("VE", cnt["VE"])
                vop(tv.tensor_tensor, out=s_f32b[:, 512:1024],
                    in0=s_f32a[:, 0:512], in1=s_f32a[:, 0:512], op=ALU.mult)
                vop(tv.tensor_sub, out=s_f32b[:, 512:1024],
                    in0=s_f32b[:, 0:512], in1=s_f32b[:, 512:1024])
                tvar = cnt["VE"]
                w(ts, "VE", tvar)
                sop(ts.activation, out=s_f32b[:, 512:1024],
                    in_=s_f32b[:, 512:1024], func=AF.Ln, bias=s_epscol[:, 0:1])
                trst = sop(ts.activation, out=s_f32b[:, 512:1024],
                           in_=s_f32b[:, 512:1024], func=AF.Exp, scale=-0.5)
                w(tv, "SC", trst)
                for dt_ in range(8):
                    sl = slice(dt_ * TLOC + tc * 512, dt_ * TLOC + (tc + 1) * 512)
                    vop(tv.tensor_sub, out=s_xmid[:, sl], in0=s_xmid[:, sl],
                        in1=s_f32a[:, 0:512])
                    vop(tv.tensor_mul, out=s_xmid[:, sl], in0=s_xmid[:, sl],
                        in1=s_f32b[:, 512:1024])
                t_done = cnt["VE"]
            return t_done

        t_ln1 = layernorm_inplace(t_resid)

        w(tv, "VE", t_ln1)
        for dc in range(8):
            vop(tv.tensor_copy, out=s_xT[:, dc * TLOC:(dc + 1) * TLOC],
                in_=s_xmid[:, dc * TLOC:(dc + 1) * TLOC])
        t_xmb = cnt["VE"]

        rowb = s_rowsc[:].bitcast(BF16)
        for tti in range(NTT):
            w(tv, "ST", cnt["ST"])
            for dc in range(8):
                pn = "pS" if dc % 2 == 0 else "pT"
                ps = P[pn][:, 0:128]
                w(tt, "VE", t_ln1)
                trans(pn, ps, s_xmid[:, dc * TLOC + tti * 128:dc * TLOC + (tti + 1) * 128],
                      s_id32[:])
                tk = cnt["MM"]
                w(tv, "MM", tk)
                evict_v(pn, tv.tensor_copy,
                        out=rowb[:, dc * 128:(dc + 1) * 128], in_=ps)
            tke = cnt["VE"]
            w(ty, "VE", tke)
            dma(ty, d_xmid_in[tti * 128:(tti + 1) * 128, :], rowb[:, 0:D], sem="ST")
            _wst(ty)
        t_xmid_st = cnt["ST"]
        t_cc_xm = allgather(d_xmid_in[:, :], d_xmid_out[:, :], t_xmid_st)

        join_all()
        st_wo.close()
        st_osb.close()
        st_ffn1 = ExitStack()
        s_wgub = sbuf("s_wgub", (128, 8 * 1536), BF16, stack=st_ffn1)
        s_wsdb = sbuf("s_wsdb", (128, 6 * 1024), BF16, stack=st_ffn1)
        s_hsh = sbuf("s_hsh", (128, 6 * 512), BF16, stack=st_ffn1)
        st_p4 = ExitStack()
        s_probs = sbuf("s_probs", (128, NTT * E), stack=st_p4)
        s_h = sbuf("s_h", (128, NTT * E), BF16, stack=st_p4)
        s_cprev = sbuf("s_cprev", (128, NTT * E), stack=st_p4)
        s_cem = sbuf("s_cem", (64, 129), stack=st_p4)
        s_gcol = sbuf("s_gcol", (64, NTT), stack=st_p4)

        mark("P3")
        w(tg, "MM", cnt["MM"])
        lgu = dma(tg, s_wgub[:, 0:8 * 1536].rearrange("p (c f) -> p c f", f=1536),
                  wsgu.rearrange("(c p) f -> p c f", p=128))
        lsd = dma(tg, s_wsdb[:, 0:6 * 1024].rearrange("p (c f) -> p c f", f=1024),
                  wsd.rearrange("(c p) f -> p c f", p=128))
        lrw = dma(ty, s_f32a[:, 0:512].rearrange("p (c e) -> p c e", e=E),
                  rwt.rearrange("(c p) e -> p c e", p=128))
        w(tt, "LD", lrw)
        w(tt, "VE", t_ln1)
        for tti in range(NTT):
            pn = "pU" if tti % 2 == 0 else "pV"
            ps = P[pn][:, 0:E]
            for dc in range(8):
                mm(pn, ps, s_xmid[:, dc * TLOC + tti * 128:dc * TLOC + (tti + 1) * 128],
                   s_f32a[:, dc * E:(dc + 1) * E], start=(dc == 0), stop=(dc == 7))
            tk = cnt["MM"]
            w(tv, "MM", tk)
            evict_v(pn, tv.tensor_copy, out=s_probs[:, tti * E:(tti + 1) * E], in_=ps)
        t_logits = cnt["VE"]

        pr3 = s_probs[:].rearrange("p (t e) -> p t e", e=E)
        vop(tv.tensor_reduce, out=s_f32a[:, 0:NTT], in_=pr3, axis=AX, op=ALU.max)
        mx3 = s_f32a[:, 0:NTT].unsqueeze(2).broadcast_to([128, NTT, E])
        vop(tv.tensor_tensor, out=pr3, in0=pr3, in1=mx3, op=ALU.subtract)
        t_sm1 = cnt["VE"]
        w(ts, "VE", t_sm1)
        t_expp = sop(ts.activation, out=s_probs[:], in_=s_probs[:], func=AF.Exp)
        w(tv, "SC", t_expp)
        vop(tv.tensor_reduce, out=s_f32a[:, 0:NTT], in_=pr3, axis=AX, op=ALU.add)
        vop(tv.reciprocal, out=s_f32a[:, 0:NTT], in_=s_f32a[:, 0:NTT])
        vop(tv.tensor_tensor, out=pr3, in0=pr3, in1=mx3, op=ALU.mult)
        t_sm = cnt["VE"]

        hf3 = s_f32b[:, 0:NTT * E].rearrange("p (t e) -> p t e", e=E)
        vop(tv.memset, s_f32b[:], 0.0)
        w83 = s_w8[:].rearrange("p (t r) -> p t r", r=8)
        e83 = s_e8[:].rearrange("p (t r) -> p t r", r=8)
        for r in range(TOPK):
            vop(tv.tensor_reduce, out=w83[:, :, r], in_=pr3, axis=AX, op=ALU.max)
            wb3 = w83[:, :, r:r + 1].broadcast_to([128, NTT, E])
            eq3_ = s_cprev[:].rearrange("p (t e) -> p t e", e=E)
            vop(tv.tensor_tensor, out=eq3_, in0=pr3, in1=wb3, op=ALU.is_ge)
            vop(tv.tensor_tensor, out=hf3, in0=hf3, in1=eq3_, op=ALU.max)
            io3 = s_iotae[:].unsqueeze(1).broadcast_to([128, NTT, E])
            vop(tv.tensor_tensor, out=eq3_, in0=eq3_, in1=io3, op=ALU.mult)
            vop(tv.tensor_reduce, out=e83[:, :, r], in_=eq3_, axis=AX, op=ALU.max)
            vop(tv.tensor_tensor, out=eq3_, in0=pr3, in1=wb3, op=ALU.is_ge)
            vop(tv.scalar_tensor_tensor, out=pr3, in0=eq3_, scalar=-2.0,
                in1=pr3, op0=ALU.mult, op1=ALU.add)
        vop(tv.tensor_copy, out=s_h[:], in_=s_f32b[:, 0:NTT * E])
        t_top8 = cnt["VE"]

        vop(tv.tensor_reduce, out=s_f32b[:, 0:NTT], in_=w83, axis=AX, op=ALU.add)
        vop(tv.reciprocal, out=s_f32b[:, 0:NTT], in_=s_f32b[:, 0:NTT])
        sb3 = s_f32b[:, 0:NTT].unsqueeze(2).broadcast_to([128, NTT, 8])
        wn3 = s_wn8[:].rearrange("p (t r) -> p t r", r=8)
        vop(tv.tensor_tensor, out=wn3, in0=w83, in1=sb3, op=ALU.mult)
        t_wn = cnt["VE"]

        vop(tv.memset, s_f32b[:, 512:513], 0.0)
        t_base0 = cnt["VE"]
        w(tt, "VE", max(t_top8, t_base0))
        t_gtab_st_list = []
        for tti in range(NTT):
            h_t = s_h[:, tti * E:(tti + 1) * E]
            pnc = "pU" if tti % 2 == 0 else "pV"
            psc = P[pnc][0:64, 0:129]
            w(tv, "ST", cnt["ST"])
            mm(pnc, psc, h_t, s_trib[:])
            tk1 = cnt["MM"]
            pnd = "pS" if tti % 2 == 0 else "pT"
            psd = P[pnd][:, 0:E]
            mm(pnd, psd, s_trib[:, 0:128], h_t)
            tk2 = cnt["MM"]
            w(tv, "MM", tk1)
            bb3 = s_f32b[0:64, 512:513].unsqueeze(2).broadcast_to([64, 1, 129])
            cem3 = s_cem[:].rearrange("p (a c) -> p a c", a=1)
            ps3 = psc.rearrange("p (a c) -> p a c", a=1)
            evict_v(pnc, tv.tensor_tensor, out=cem3, in0=ps3, in1=bb3, op=ALU.add)
            w(tv, "MM", tk2)
            evict_v(pnd, tv.tensor_copy, out=s_cprev[:, tti * E:(tti + 1) * E], in_=psd)
            vop(tv.tensor_copy, out=s_gcol[:, tti:tti + 1], in_=s_cem[0:64, 0:1])
            tb = vop(tv.tensor_add, out=s_f32b[0:64, 512:513],
                     in0=s_f32b[0:64, 512:513], in1=psc[0:64, 128:129])
            bank_free[pnc] = ("VE", tb)
            tkc = cnt["VE"]
            w(ty, "VE", tkc)
            dma(ty, d_gtab_in[tti * E:(tti + 1) * E, :], s_cem[0:64, 0:128], sem="ST")
            _wst(ty)
        t_cnts = cnt["VE"]
        t_gtab_st = cnt["ST"]
        w(ty, "VE", t_cnts)
        dma(ty, d_cnt_in[0:1, :].rearrange("a p -> p a"), s_f32b[0:64, 512:513], sem="ST")
        t_cnt_st = cnt["ST"]
        t_cc_cnt = allgather(d_cnt_in[:, :], d_cnt_out[:, :], t_cnt_st)
        t_cc_gt = allgather(d_gtab_in[:, :], d_gtab_out[:, :], t_gtab_st)

        mark("P4")
        w(tt, "GL", lgu)
        w(tt, "VE", t_xmb)
        t_silu_free = 0
        for tca in range(4):
            for ft in range(6):
                png = "pA" if ft % 2 == 0 else "pB"
                psg = P[png][:, 0:512]
                pnu = "pC" if ft % 2 == 0 else "pD"
                psu = P[pnu][:, 0:512]
                for dc in range(8):
                    mm(png, psg, s_wgub[:, dc * 1536 + ft * 128:dc * 1536 + ft * 128 + 128],
                       s_xT[:, dc * TLOC + tca * 512:dc * TLOC + (tca + 1) * 512],
                       start=(dc == 0), stop=(dc == 7))
                tkg = cnt["MM"]
                for dc in range(8):
                    mm(pnu, psu,
                       s_wgub[:, dc * 1536 + 768 + ft * 128:dc * 1536 + 768 + ft * 128 + 128],
                       s_xT[:, dc * TLOC + tca * 512:dc * TLOC + (tca + 1) * 512],
                       start=(dc == 0), stop=(dc == 7))
                tku = cnt["MM"]
                w(ts, "MM", tkg)
                w(ts, "VE", t_silu_free)
                tsg = evict_s(png, ts.activation, out=s_f32a[:, 0:512],
                              in_=psg, func=AF.Silu)
                w(tv, "SC", tsg)
                w(tv, "MM", tku)
                t_silu_free = evict_v(
                    pnu, tv.tensor_tensor,
                    out=s_hsh[:, ft * 512:(ft + 1) * 512],
                    in0=s_f32a[:, 0:512], in1=psu, op=ALU.mult)
            t_hsh = cnt["VE"]
            w(tt, "GL", lsd)
            w(tt, "VE", t_hsh)
            for dt_ in range(8):
                pn = "pA" if dt_ % 2 == 0 else "pB"
                ps = P[pn][:, 0:512]
                for ft in range(6):
                    mm(pn, ps, s_wsdb[:, ft * 1024 + dt_ * 128:ft * 1024 + dt_ * 128 + 128],
                       s_hsh[:, ft * 512:(ft + 1) * 512],
                       start=(ft == 0), stop=(ft == 5))
                tk = cnt["MM"]
                w(tv, "MM", tk)
                sl = slice(dt_ * TLOC + tca * 512, dt_ * TLOC + (tca + 1) * 512)
                evict_v(pn, tv.tensor_tensor, out=s_xmid[:, sl], in0=s_xmid[:, sl],
                        in1=ps, op=ALU.add)
            w(tt, "VE", cnt["VE"])
        t_acc_sh = cnt["VE"]

        mark("P5a")
        w(ty, "CC", t_cc_cnt)
        lcnt = dma(ty, s_cnt_all[:], d_cnt_out[:, :])
        w(tt, "LD", lcnt)
        tmsel = trans("pU", P["pU"][0:8, 0:128], s_msel[:, 0:8], s_id32[:])
        w(tv, "MM", tmsel)
        tmc = evict_v("pU", tv.tensor_copy, out=s_f32a[0:8, 0:128], in_=P["pU"][0:8, 0:128])
        w(tt, "VE", tmc)
        mm("pV", P["pV"][:, 0:E], s_f32a[0:8, 0:128], s_cnt_all[0:8, :])
        tkb = cnt["MM"]
        w(tv, "MM", tkb)
        evict_v("pV", tv.tensor_copy, out=s_base[:], in_=P["pV"][:, 0:E])
        mm("pU", P["pU"][0:64, 0:1], s_cnt_all[0:8, :], s_f32a[0:8, 0:1])
        tkbe = cnt["MM"]
        w(tv, "MM", tkbe)
        evict_v("pU", tv.tensor_copy, out=s_bgl[:], in_=P["pU"][0:64, 0:1])
        t_bases = cnt["VE"]

        vop(tv.memset, s_f32b[:, 0:E], 0.0)
        vop(tv.memset, s_f32a[:, 512:640], 1.0)
        t_ones = cnt["VE"]
        w(tt, "VE", t_ones)
        for tti in range(NTT):
            sl = slice(tti * E, (tti + 1) * E)
            vop(tv.tensor_add, out=s_cprev[:, sl], in0=s_cprev[:, sl], in1=s_f32b[:, 0:E])
            vop(tv.tensor_add, out=s_cprev[:, sl], in0=s_cprev[:, sl], in1=s_base[:])
            pn = "pU" if tti % 2 == 0 else "pV"
            ps = P[pn][:, 0:E]
            mm(pn, ps, s_ones128b[:], s_h[:, sl])
            tk = cnt["MM"]
            w(tv, "MM", tk)
            evict_v(pn, tv.tensor_tensor, out=s_f32b[:, 0:E], in0=s_f32b[:, 0:E],
                    in1=ps, op=ALU.add)
        t_cglob = cnt["VE"]

        for r in range(TOPK):
            e8c3 = e83[:, :, r:r + 1].broadcast_to([128, NTT, E])
            io3 = s_iotae[:].unsqueeze(1).broadcast_to([128, NTT, E])
            eqw = s_f32a[:, 0:NTT * E].rearrange("p (t e) -> p t e", e=E)
            vop(tv.tensor_tensor, out=eqw, in0=io3, in1=e8c3, op=ALU.is_equal)
            gio3 = s_giotae[:].unsqueeze(1).broadcast_to([128, NTT, E])
            gs3 = s_probs[:].rearrange("p (t e) -> p t e", e=E)
            g83 = s_g8[:].rearrange("p (t r) -> p t r", r=8)
            vop(tv.tensor_tensor, out=gs3, in0=eqw, in1=gio3, op=ALU.mult)
            vop(tv.tensor_reduce, out=g83[:, :, r], in_=gs3, axis=AX, op=ALU.max)
            cg3 = s_cprev[:].rearrange("p (t e) -> p t e", e=E)
            vop(tv.tensor_tensor, out=eqw, in0=eqw, in1=cg3, op=ALU.mult)
            p83 = s_pos8[:].rearrange("p (t r) -> p t r", r=8)
            vop(tv.tensor_reduce, out=p83[:, :, r], in_=eqw, axis=AX, op=ALU.add)
        vop(tv.tensor_scalar, out=s_val8[:], in0=s_pos8[:], scalar1=float(CAP),
            scalar2=None, op0=ALU.is_lt)
        vop(tv.tensor_mul, out=s_wn8[:], in0=s_wn8[:], in1=s_val8[:])
        vop(tv.tensor_scalar, out=s_slot8[:], in0=s_e8[:], scalar1=float(CAP),
            scalar2=None, op0=ALU.mult)
        vop(tv.tensor_add, out=s_slot8[:], in0=s_slot8[:], in1=s_pos8[:])
        vop(tv.tensor_mul, out=s_slot8[:], in0=s_slot8[:], in1=s_val8[:])
        v83 = s_val8[:].rearrange("p (t r) -> p t r", r=8)
        vop(tv.tensor_reduce, out=s_ninv[:], in_=v83, axis=AX, op=ALU.add)
        vop(tv.tensor_scalar, out=s_ninv[:], in0=s_ninv[:], scalar1=-1.0,
            scalar2=8.0, op0=ALU.mult, op1=ALU.add)
        vop(tv.tensor_copy, out=s_ninvb[:], in_=s_ninv[:])
        A_ = s_f32a[:, 0:128]
        vop(tv.tensor_scalar, out=A_, in0=s_g8[:], scalar1=float(CAP),
            scalar2=None, op0=ALU.mult)
        vop(tv.tensor_add, out=A_, in0=A_, in1=s_pos8[:])
        vop(tv.tensor_mul, out=A_, in0=A_, in1=s_val8[:])
        vop(tv.tensor_copy, out=s_slot8i[:], in_=A_)
        t_slot = cnt["VE"]

        for tti in range(NTT):
            w(tv, "ST", cnt["ST"])
            vop(tv.memset, s_f32b[:, 64:128], 0.0)
            vop(tv.tensor_copy, out=s_f32b[:, 64:72], in_=s_e8[:, tti * 8:(tti + 1) * 8])
            vop(tv.tensor_copy, out=s_f32b[:, 72:80], in_=s_wn8[:, tti * 8:(tti + 1) * 8])
            tkr = cnt["VE"]
            w(ty, "VE", tkr)
            dma(ty, d_rtab_in[tti * 128:(tti + 1) * 128, :], s_f32b[:, 64:128], sem="ST")
            _wst(ty)
        t_rtab_st = cnt["ST"]
        t_cc_rt = allgather(d_rtab_in[:, :], d_rtab_out[:, :], t_rtab_st)

        w(tv, "VE", t_bases)
        bg3 = s_bgl[:].unsqueeze(2).broadcast_to([64, 1, NTT])
        gc3 = s_gcol[:].unsqueeze(1)
        vop(tv.tensor_tensor, out=gc3, in0=gc3, in1=bg3, op=ALU.add)
        vop(tv.memset, s_f32b[:, 128:192], 0.0)
        vop(tv.tensor_copy, out=s_f32b[0:64, 128:144], in_=s_gcol[:])
        vop(tv.tensor_copy, out=s_f32b[0:64, 144:145], in_=s_f32b[0:64, 512:513])
        t_gst = cnt["VE"]
        w(ty, "VE", t_gst)
        dma(ty, d_gst_in[:, :], s_f32b[0:64, 128:192], sem="ST")
        t_gst_st = cnt["ST"]
        t_cc_gst = allgather(d_gst_in[:, :], d_gst_out[:, :], t_gst_st)

        w(ty, "VE", t_slot)
        dma(ty, d_scri[:, 0:128], s_slot8i[:], sem="ST")
        _wst(ty)
        for g_ in range(8):
            dma(ty, s_yidx[0:16, :].rearrange("q (f gg) -> q f gg", gg=8)[:, :, g_],
                d_scri[16 * g_:16 * g_ + 16, 0:128])
        _wld(ty)
        dma(ty, s_yidx[16:32, :], s_yidx[0:16, :])
        t_yidx = cnt["LD"]
        dma(ty, d_scri[:, 128:144], s_ninvb[:].bitcast(I16), sem="ST")
        _wst(ty)
        dma(ty, s_ninvrow[0:1, :].bitcast(I16).rearrange("a (t p) -> a t p", t=NTT),
            d_scri[:, 128:144].rearrange("(a p) t -> a t p", a=1))
        t_ninvr = cnt["LD"]

        mark("P5b")
        join_all()
        st_p4.close()
        st_ffn1.close()
        st_xT.close()
        st_p6 = ExitStack()
        s_grow = sbuf("s_grow", (128, 32 * 128), stack=st_p6)
        s_rt = sbuf("s_rt", (128, 32 * RTW), stack=st_p6)
        s_gstart = sbuf("s_gstart", (128, 128), stack=st_p6)

        w(tg, "CC", t_cc_gst)
        w(tg, "LD", t_const)
        tgk = gather(s_grow[:, 0:512].rearrange("p (n d) -> p n d", d=64),
                     d_gst_out[:, :], s_gsidx[:], 1024)
        w(tv, "GP", tgk)
        gat3 = s_grow[:, 0:512].rearrange("p (n d) -> p n d", d=64)
        gst2 = s_gstart[:].rearrange("p (n k) -> p n k", k=16)
        vop(tv.tensor_copy, out=gst2, in_=gat3[:, :, 0:16])
        vop(tv.tensor_reduce, out=s_totc[:], in_=gat3[:, :, 16:17].rearrange(
            "p n a -> p (n a)"), axis=AX, op=ALU.add)
        vop(tv.tensor_copy, out=s_f32b[:, 200:208], in_=gat3[:, :, 16])
        t_gstart = cnt["VE"]

        vop(tv.memset, s_kstar[:], 0.0)
        for k in range(128):
            vop(tv.scalar_tensor_tensor, out=s_kstar[:], in0=s_ctens[:],
                scalar=s_gstart[:, k:k + 1], in1=s_kstar[:], op0=ALU.is_ge, op1=ALU.add)
        vop(tv.tensor_scalar, out=s_kstar[:], in0=s_kstar[:], scalar1=-1.0,
            scalar2=None, op0=ALU.add)
        vop(tv.tensor_scalar, out=s_f32a[:, 0:32], in0=s_kstar[:], scalar1=64.0,
            scalar2=None, op0=ALU.mult)
        vop(tv.tensor_scalar, out=s_f32a[:, 0:32], in0=s_f32a[:, 0:32],
            scalar1=s_ecol[:, 0:1], scalar2=None, op0=ALU.add)
        tgidxf = vop(tv.tensor_copy, out=s_kstari[:], in_=s_f32a[:, 0:32])
        w(ty, "VE", tgidxf)
        dma(ty, d_scri[:, 0:32], s_kstari[:], sem="ST")
        _wst(ty)
        for g_ in range(8):
            dma(ty, s_gidx[0:16, 0:256].rearrange("q (b gg) -> q b gg", gg=8)[:, :, g_],
                d_scri[16 * g_:16 * g_ + 16, 0:32])
        _wld(ty)
        dma(ty, s_gidx[16:32, :], s_gidx[0:16, :])
        t_gidx = cnt["LD"]
        w(tg, "CC", t_cc_gt)
        w(tg, "LD", t_gidx)
        w(tg, "VE", cnt["VE"])
        for cq in range(4):
            gather(s_grow[:, cq * 1024:(cq + 1) * 1024].rearrange(
                "p (n d) -> p n d", d=128),
                d_gtab_out[:, :], s_gidx[:, cq * 64:(cq + 1) * 64], 1024)
        t_growg = cnt["GP"]
        w(tv, "GP", t_growg)
        vop(tv.tensor_copy, out=s_f32a[:, 0:32], in_=s_ctens[:])
        for j in range(7):
            vop(tv.tensor_scalar, out=s_f32a[:, 32:64], in0=s_kstar[:],
                scalar1=float(16 * (j + 1)), scalar2=None, op0=ALU.is_ge)
            vop(tv.tensor_scalar, out=s_f32a[:, 32:64], in0=s_f32a[:, 32:64],
                scalar1=s_f32b[:, 200 + j:201 + j], scalar2=None, op0=ALU.mult)
            vop(tv.tensor_sub, out=s_f32a[:, 0:32], in0=s_f32a[:, 0:32],
                in1=s_f32a[:, 32:64])
        gr3 = s_grow[:].rearrange("p (b j) -> p b j", j=128)
        c3 = s_f32a[:, 0:32].unsqueeze(2).broadcast_to([128, 32, 128])
        vop(tv.tensor_tensor, out=gr3, in0=gr3, in1=c3, op=ALU.is_le)
        vop(tv.tensor_reduce, out=s_stok[:], in_=gr3, axis=AX, op=ALU.add)
        vop(tv.tensor_scalar, out=s_f32a[:, 0:32], in0=s_kstar[:], scalar1=128.0,
            scalar2=None, op0=ALU.mult)
        vop(tv.tensor_add, out=s_stok[:], in0=s_stok[:], in1=s_f32a[:, 0:32])
        vop(tv.tensor_scalar, out=s_stok[:], in0=s_stok[:], scalar1=-1.0,
            scalar2=None, op0=ALU.add)
        tstok = vop(tv.tensor_copy, out=s_stoki[:], in_=s_stok[:])
        w(ty, "VE", tstok)
        dma(ty, d_scri[:, 0:32], s_stoki[:], sem="ST")
        _wst(ty)
        for bh in range(2):
            dma(ty, s_sidx[0:16, :].rearrange("q (el cb bb) -> q el cb bb",
                                              el=8, bb=2)[:, :, :, bh],
                d_scri[:, 0:32].rearrange("(el cb) (bb q) -> q el cb bb",
                                          el=8, bb=2)[:, :, :, bh])
        for g_ in range(8):
            dma(ty, s_sidx2[0:16, :].rearrange("q (b gg) -> q b gg", gg=8)[:, :, g_],
                d_scri[16 * g_:16 * g_ + 16, 0:32])
        _wld(ty)
        dma(ty, s_sidx[16:32, :], s_sidx[0:16, :])
        dma(ty, s_sidx2[16:32, :], s_sidx2[0:16, :])
        t_sidx = cnt["LD"]

        w(tg, "CC", t_cc_rt)
        w(tg, "LD", t_sidx)
        for cq in range(4):
            gather(s_rt[:, cq * 512:(cq + 1) * 512].rearrange("p (n d) -> p n d", d=RTW),
                   d_rtab_out[:, :], s_sidx2[:, cq * 64:(cq + 1) * 64], 1024)
        t_rtg = cnt["GP"]
        w(tv, "GP", t_rtg)
        rt3 = s_rt[:].rearrange("p (b d) -> p b d", d=RTW)
        wetmp = s_f32a[:, 0:256].rearrange("p (b r) -> p b r", r=8)
        vop(tv.tensor_scalar, out=wetmp, in0=rt3[:, :, 0:8],
            scalar1=s_ecol[:, 0:1], scalar2=None, op0=ALU.is_equal)
        vop(tv.tensor_tensor, out=wetmp, in0=wetmp, in1=rt3[:, :, 8:16], op=ALU.mult)
        vop(tv.tensor_reduce, out=s_we[:], in_=wetmp, axis=AX, op=ALU.add)
        vop(tv.tensor_scalar, out=s_f32a[:, 0:32], in0=s_ctens[:],
            scalar1=s_totc[:, 0:1], scalar2=None, op0=ALU.is_lt)
        vop(tv.tensor_mul, out=s_we[:], in0=s_we[:], in1=s_f32a[:, 0:32])
        twe = cnt["VE"]
        w(ty, "VE", twe)
        dma(ty, d_scrf[:, 0:32], s_we[:], sem="ST")
        _wst(ty)
        for gg in range(4):
            dma(ty, s_wec[gg * 32:(gg + 1) * 32, :].rearrange("q (el cc) -> q el cc", el=8),
                d_scrf[:, 0:32].rearrange("(el cc g2) q -> g2 q el cc", el=8, cc=4)[gg])
        t_wec = cnt["LD"]

        mark("P6")
        join_all()
        st_p6.close()
        st_p7 = ExitStack()
        s_wex = [sbuf(f"s_wex{i}", (128, 2 * 8 * F + 6 * 1024), BF16, stack=st_p7)
                 for i in range(2)]
        s_xb = sbuf("s_xb", (128, 4 * D), BF16, stack=st_p7)
        s_xbT = sbuf("s_xbT", (128, 8 * 512), BF16, stack=st_p7)
        s_he = sbuf("s_he", (128, 6 * 512), BF16, stack=st_p7)
        s_ye = sbuf("s_ye", (128, 4 * D), BF16, stack=st_p7)

        def load_expert(el, buf):
            dma(tg, buf[:, 0:6144].rearrange("p (c f) -> p c f", f=F),
                weg[el].rearrange("(c p) f -> p c f", p=128))
            dma(tg, buf[:, 6144:12288].rearrange("p (c f) -> p c f", f=F),
                weu[el].rearrange("(c p) f -> p c f", p=128))
            return dma(tg, buf[:, 12288:18432].rearrange("p (c f) -> p c f", f=1024),
                       wed[el].rearrange("(c p) f -> p c f", p=128))

        t_wexl = [0, 0]
        t_silu_free7 = 0
        t_wexl[0] = load_expert(0, s_wex[0])
        t_ye_st = 0
        t_xb_mm = 0
        w(tg, "CC", t_cc_xm)
        w(tg, "LD", t_sidx)
        for el in range(EO):
            buf = s_wex[el % 2]
            w(tg, "MM", t_xb_mm)
            tgx = gather(s_xb[:].rearrange("p (n d) -> p n d", d=D),
                         d_xmid_out[:, :], s_sidx[:, el * 32:(el + 1) * 32], 512)
            w(tt, "GP", tgx)
            for blk in range(4):
                for dc in range(8):
                    pn = "pS" if (blk * 8 + dc) % 2 == 0 else "pT"
                    ps = P[pn][:, 0:64].bitcast(BF16)
                    trans(pn, ps, s_xb[:, blk * D + dc * 128:blk * D + (dc + 1) * 128],
                          s_id16[:])
                    tk = cnt["MM"]
                    w(tv, "MM", tk)
                    evict_v(pn, tv.tensor_copy,
                            out=s_xbT[:, dc * 512 + blk * 128:dc * 512 + (blk + 1) * 128],
                            in_=ps)
            t_xbT = cnt["VE"]
            t_xb_mm = cnt["MM"]
            w(tt, "GL", t_wexl[el % 2])
            w(tt, "VE", t_xbT)
            for ft in range(6):
                png = "pA" if ft % 2 == 0 else "pB"
                psg = P[png][:, 0:512]
                pnu = "pC" if ft % 2 == 0 else "pD"
                psu = P[pnu][:, 0:512]
                for dc in range(8):
                    mm(png, psg, buf[:, dc * F + ft * 128:dc * F + ft * 128 + 128],
                       s_xbT[:, dc * 512:(dc + 1) * 512],
                       start=(dc == 0), stop=(dc == 7))
                tkg = cnt["MM"]
                for dc in range(8):
                    mm(pnu, psu, buf[:, 6144 + dc * F + ft * 128:6144 + dc * F + ft * 128 + 128],
                       s_xbT[:, dc * 512:(dc + 1) * 512],
                       start=(dc == 0), stop=(dc == 7))
                tku = cnt["MM"]
                w(ts, "MM", tkg)
                w(ts, "VE", t_silu_free7)
                tsg = evict_s(png, ts.activation, out=s_f32a[:, 0:512],
                              in_=psg, func=AF.Silu)
                w(tv, "SC", tsg)
                w(tv, "MM", tku)
                t_silu_free7 = evict_v(
                    pnu, tv.tensor_tensor, out=s_he[:, ft * 512:(ft + 1) * 512],
                    in0=s_f32a[:, 0:512], in1=psu, op=ALU.mult)
            t_he = cnt["VE"]
            if el + 1 < EO:
                w(tg, "MM", cnt["MM"])
                t_wexl[(el + 1) % 2] = load_expert(el + 1, s_wex[(el + 1) % 2])
            w(tt, "VE", t_he)
            w(tt, "LD", t_wec)
            for ct in range(4):
                for hfd in range(2):
                    pn = "pU" if (ct * 2 + hfd) % 2 == 0 else "pV"
                    ps = P[pn][:, 0:512]
                    for ft in range(6):
                        mm(pn, ps, s_he[:, ft * 512 + ct * 128:ft * 512 + ct * 128 + 128],
                           buf[:, 12288 + ft * 1024 + hfd * 512:12288 + ft * 1024 + (hfd + 1) * 512],
                           start=(ft == 0), stop=(ft == 5))
                    tk = cnt["MM"]
                    w(tv, "MM", tk)
                    w(tv, "ST", t_ye_st)
                    evict_v(pn, tv.tensor_scalar,
                            out=s_ye[:, ct * D + hfd * 512:ct * D + (hfd + 1) * 512],
                            in0=ps, scalar1=s_wec[:, el * 4 + ct:el * 4 + ct + 1],
                            scalar2=None, op0=ALU.mult)
            t_ye = cnt["VE"]
            w(ty, "VE", t_ye)
            dma(ty, d_ybuf_in[el * CAP:(el + 1) * CAP, :].rearrange(
                "(b p) d -> p b d", p=128),
                s_ye[:].rearrange("p (b d) -> p b d", d=D), sem="ST")
            t_ye_st = cnt["ST"]
            if el % 2 == 1:
                q_ = el // 2
                t_cc_yb = allgather(d_ybuf_in[q_ * 1024:(q_ + 1) * 1024, :],
                                    d_ybuf_out[q_ * 8192:(q_ + 1) * 8192, :],
                                    cnt["ST"])

        mark("P7")
        join_all()
        st_p7.close()
        st_p8 = ExitStack()
        s_yg = sbuf("s_yg", (128, 2 * 8 * D), BF16, stack=st_p8)
        s_acc1 = sbuf("s_acc1", (128, 2 * 4 * D), BF16, stack=st_p8)

        w(ty, "CC", t_cc_yb)
        ly0 = dma(ty, s_y0[0:1, :], d_ybuf_out[0:1, :])
        t_y0 = cnt["LD"]

        t_ygread = [0, 0]
        t_trans = [0, 0]
        for tti in range(NTT):
            b_ = tti % 2
            yg = s_yg[:, b_ * 8 * D:(b_ + 1) * 8 * D]
            ac = s_acc1[:, b_ * 4 * D:(b_ + 1) * 4 * D]
            w(tg, "VE", t_ygread[b_])
            w(tg, "CC", t_cc_yb)
            w(tg, "LD", t_yidx)
            tgy = gather(yg.rearrange("p (n d) -> p n d", d=D),
                         d_ybuf_out[:, :], s_yidx[:, tti * 64:(tti + 1) * 64], 1024)
            w(tv, "GP", tgy)
            w(tv, "MM", t_trans[b_])
            for j in range(4):
                vop(tv.tensor_add, out=ac[:, j * D:(j + 1) * D],
                    in0=yg[:, (2 * j) * D:(2 * j + 1) * D],
                    in1=yg[:, (2 * j + 1) * D:(2 * j + 2) * D])
            t_ygread[b_] = cnt["VE"]
            for j in range(2):
                vop(tv.tensor_add, out=ac[:, j * D:(j + 1) * D],
                    in0=ac[:, (2 * j) * D:(2 * j + 1) * D],
                    in1=ac[:, (2 * j + 1) * D:(2 * j + 2) * D])
            vop(tv.tensor_add, out=ac[:, 0:D], in0=ac[:, 0:D],
                in1=ac[:, D:2 * D])
            t_sum = cnt["VE"]
            w(tt, "LD", max(t_y0, t_ninvr))
            pnc = "pU" if tti % 2 == 0 else "pV"
            mm(pnc, P[pnc][:, 0:512], s_ninvrow[0:1, tti * 128:(tti + 1) * 128],
               s_y0[0:1, 0:512])
            tk1 = cnt["MM"]
            pnd = "pS" if tti % 2 == 0 else "pT"
            mm(pnd, P[pnd][:, 0:512], s_ninvrow[0:1, tti * 128:(tti + 1) * 128],
               s_y0[0:1, 512:1024])
            tk2 = cnt["MM"]
            w(tv, "VE", t_sum)
            w(tv, "MM", tk2)
            evict_v(pnc, tv.tensor_tensor, out=ac[:, 0:512], in0=ac[:, 0:512],
                    in1=P[pnc][:, 0:512], op=ALU.subtract)
            evict_v(pnd, tv.tensor_tensor, out=ac[:, 512:1024], in0=ac[:, 512:1024],
                    in1=P[pnd][:, 0:512], op=ALU.subtract)
            t_rt_tile = cnt["VE"]
            for dc in range(8):
                pn = "pA" if dc % 2 == 0 else "pB"
                ps = P[pn][:, 0:64].bitcast(BF16)
                w(tt, "VE", t_rt_tile)
                trans(pn, ps, ac[:, dc * 128:(dc + 1) * 128], s_id16[:])
                tk = cnt["MM"]
                w(tv, "MM", tk)
                evict_v(pn, tv.tensor_tensor,
                        out=s_xmid[:, dc * TLOC + tti * 128:dc * TLOC + (tti + 1) * 128],
                        in0=s_xmid[:, dc * TLOC + tti * 128:dc * TLOC + (tti + 1) * 128],
                        in1=ps, op=ALU.add)
            t_trans[b_] = tk
        t_routed = cnt["VE"]

        t_ln2 = layernorm_inplace(t_routed)

        for tti in range(NTT):
            hb = (tti % 2) * D
            if tti >= 2:
                w(tv, "ST", cnt["ST"])
            for dc in range(8):
                pn = "pS" if dc % 2 == 0 else "pT"
                ps = P[pn][:, 0:128]
                w(tt, "VE", t_ln2)
                trans(pn, ps, s_xmid[:, dc * TLOC + tti * 128:dc * TLOC + (tti + 1) * 128],
                      s_id32[:])
                tk = cnt["MM"]
                w(tv, "MM", tk)
                evict_v(pn, tv.tensor_copy,
                        out=s_rowsc[:, hb + dc * 128:hb + (dc + 1) * 128], in_=ps)
            tke = cnt["VE"]
            w(ty, "VE", tke)
            dma(ty, out[tti * 128:(tti + 1) * 128, :], s_rowsc[:, hb:hb + D], sem="ST")
        _wst(ty)
        if DBG:
            join_all()
            dma(ty, dbg["xT32"], d_xT32[:, 0:TLOC], sem="ST")
            dma(ty, dbg["st"], d_st_in.rearrange("h a p -> (h a) p"), sem="ST")
            dma(ty, dbg["xmid"], d_xmid_in[0:256, :], sem="ST")
            dma(ty, dbg["cnt"], d_cnt_out[:, :], sem="ST")
            dma(ty, dbg["gtab"], d_gtab_in[0:2 * E, :], sem="ST")
            dma(ty, dbg["gst"], d_gst_in[:, :], sem="ST")
            dma(ty, dbg["rtab"], d_rtab_in[0:128, :], sem="ST")
            dma(ty, dbg["ybuf"], d_ybuf_in[0:128, :], sem="ST")
            dma(ty, dbg["fin"], s_xmid[:, 0:TLOC], sem="ST")
            dma(ty, dbg["slot"], s_slot8[:], sem="ST")
            dma(ty, dbg["stok"], s_stok[:], sem="ST")
            dma(ty, dbg["wec"], s_wec[:], sem="ST")
            dma(ty, dbg["w8"], s_w8[:], sem="ST")
            dma(ty, dbg["val8"], s_val8[:], sem="ST")
            dma(ty, dbg["wn8"], s_wn8[:], sem="ST")
            _wst(ty)
        mark("P8")
        st_p8.close()

        def _replay(rec):
            def f(engine):
                for mname, a, kw, inc, seq in rec.prog:
                    if cutoff is not None and seq > cutoff:
                        continue
                    r = getattr(engine, mname)(*a, **kw)
                    if inc is not None:
                        r.then_inc(*inc)
            return f

        block.sync(_replay(ty))
        block.tensor(_replay(tt))
        block.vector(_replay(tv))
        block.scalar(_replay(ts))
        block.gpsimd(_replay(tg))

    mybir.codegen_inst_isa_subclasses(nc)
    return nc


def kernel(**inputs):
    if "nc" not in _CACHE:
        _CACHE["nc"] = _build()
    nc = _CACHE["nc"]
    in_maps = [_host_inputs(c, inputs) for c in range(NCORES)]
    res = run_bass_kernel_spmd(nc, in_maps, core_ids=list(range(NCORES)))
    outs = [res.results[c]["out"] for c in range(NCORES)]
    full = np.concatenate(outs, axis=0).reshape(2, TGLB // 2, D)
    return full


# version 11
# speedup vs baseline: 1.0738x; 1.0308x over previous
import numpy as np
import ml_dtypes
import concourse.bass as bass
import concourse.mybir as mybir
from concourse.bass_utils import run_bass_kernel_spmd
from concourse.library_config import mlp as _MLP_LIB

F32 = mybir.dt.float32
BF16 = mybir.dt.bfloat16
I16 = mybir.dt.int16
AF = mybir.ActivationFunctionType
ALU = mybir.AluOpType
AX = mybir.AxisListType.X
BF = ml_dtypes.bfloat16

NCORES = 8
TLOC = 2048
TGLB = NCORES * TLOC
D = 1024
H = 8
CH = 64
NCH = TLOC // CH
NTT = TLOC // 128
E = 64
TOPK = 8
CAP = 512
EO = 8
NSLOT = EO * CAP
F = 768
SF = 768
RTW = 64
EPS_LN = 1e-5

_CACHE = {}
DBG = False
_MARKS = {}


def _host_inputs(core, inp):
    xf = inp["L0_attn_x"].reshape(TGLB, D)
    m = {
        "x_loc": np.ascontiguousarray(xf[core * TLOC:(core + 1) * TLOC]),
        "wq": inp["L0_attn_Wq"], "wk": inp["L0_attn_Wk"], "wv": inp["L0_attn_Wv"],
        "wg": inp["L0_attn_Wg"], "wo": inp["L0_attn_Wo"],
        "rwt": np.ascontiguousarray(inp["L0_ffn_router_w"].T),
        "wsgu": inp["L0_ffn_ws_gate_up"], "wsd": inp["L0_ffn_ws_down"],
        "weg": np.ascontiguousarray(inp["L0_ffn_w_gate"][core * EO:(core + 1) * EO]),
        "weu": np.ascontiguousarray(inp["L0_ffn_w_up"][core * EO:(core + 1) * EO]),
        "wed": np.ascontiguousarray(inp["L0_ffn_w_down"][core * EO:(core + 1) * EO]),
    }
    m["ident32"] = np.eye(128, dtype=np.float32)
    m["ident16"] = np.eye(128, dtype=BF)
    tri = np.zeros((128, 129), np.float32)
    tri[:, :128] = np.triu(np.ones((128, 128), np.float32), 1)
    tri[:, 128] = 1.0
    m["trib"] = tri.astype(BF)
    m["ones128b"] = np.ones((128, 128), dtype=BF)
    m["ones128f"] = np.ones((128, 128), dtype=np.float32)
    ms = np.ones((128, TLOC), np.float32)
    ms[:, ::CH] = 0.0
    m["mask_scan"] = ms.astype(BF)
    mA = np.triu(np.ones((CH, CH), np.float32), 0)
    m["maskA"] = np.concatenate([mA, mA], axis=0).copy()
    m["iota_e"] = np.tile(np.arange(E, dtype=np.float32)[None, :], (128, 1))
    ev = np.arange(E)
    gv = ((ev % 8) // 2) * 16 + (ev // 8) * 2 + (ev % 8) % 2
    m["giota_e"] = np.tile(gv.astype(np.float32)[None, :], (128, 1))
    m["msel"] = np.tile((np.arange(8) < core).astype(np.float32)[None, :], (128, 1))
    cb = (np.arange(128) % 16)[:, None]
    m["ctens"] = (cb * 32 + np.arange(32)[None, :]).astype(np.float32)
    m["ecol"] = (EO * core + np.arange(128) // 16).astype(np.float32)[:, None].copy()
    gs = np.zeros((128, 64), np.int16)
    for i in range(1024):
        blk, p = i // 128, i % 128
        gs[i % 16, i // 16] = blk * 64 + EO * core + p // 16
    gs[16:32] = gs[0:16]
    m["gsidx"] = gs
    return m


def _build(cutoff=None):
    nc = bass.Bass("TRN2", target_bir_lowering=False, num_swdge_queues=1)

    def din(name, shape, dt=F32):
        return nc.dram_tensor(name, list(shape), dt, kind="ExternalInput").ap()

    x_loc = din("x_loc", (TLOC, D))
    wq = din("wq", (D, D)); wk = din("wk", (D, D)); wv = din("wv", (D, D))
    wg = din("wg", (D, D)); wo = din("wo", (D, D))
    rwt = din("rwt", (D, E))
    wsgu = din("wsgu", (D, 2 * SF)); wsd = din("wsd", (SF, D))
    weg = din("weg", (EO, D, F)); weu = din("weu", (EO, D, F)); wed = din("wed", (EO, F, D))
    ident32 = din("ident32", (128, 128))
    ident16 = din("ident16", (128, 128), BF16)
    trib = din("trib", (128, 129), BF16)
    ones128b = din("ones128b", (128, 128), BF16)
    ones128f = din("ones128f", (128, 128))
    mask_scan = din("mask_scan", (128, TLOC), BF16)
    maskA = din("maskA", (128, CH))
    iota_e = din("iota_e", (128, E))
    giota_e = din("giota_e", (128, E))
    msel = din("msel", (128, 8))
    ctens = din("ctens", (128, 32))
    ecol = din("ecol", (128, 1))
    gsidx = din("gsidx", (128, 64), I16)

    out = nc.dram_tensor("out", [TLOC, D], F32, kind="ExternalOutput").ap()

    def dint(name, shape, dt=F32):
        return nc.dram_tensor(name, list(shape), dt).ap()

    d_xT32 = dint("d_xT32", (128, 8 * TLOC))
    d_st_in = dint("d_st_in", (H, 129, 128))
    d_st_out = nc.dram_tensor("d_st_out", [NCORES * H, 129, 128], F32, addr_space="Shared").ap()
    d_cnt_in = dint("d_cnt_in", (1, E))
    d_cnt_out = nc.dram_tensor("d_cnt_out", [NCORES, E], F32, addr_space="Shared").ap()
    d_gst_in = dint("d_gst_in", (E, 64))
    d_gst_out = nc.dram_tensor("d_gst_out", [NCORES * E, 64], F32, addr_space="Shared").ap()
    d_gtab_in = dint("d_gtab_in", (NTT * E, 128))
    d_gtab_out = nc.dram_tensor("d_gtab_out", [128 * E, 128], F32, addr_space="Shared").ap()
    d_xmid_in = dint("d_xmid_in", (TLOC, D), BF16)
    d_xmid_out = nc.dram_tensor("d_xmid_out", [TGLB, D], BF16, addr_space="Shared").ap()
    d_rtab_in = dint("d_rtab_in", (TLOC, RTW))
    d_rtab_out = nc.dram_tensor("d_rtab_out", [TGLB, RTW], F32, addr_space="Shared").ap()
    d_ybuf_in = dint("d_ybuf_in", (NSLOT, D), BF16)
    d_ybuf_out = nc.dram_tensor("d_ybuf_out", [NCORES * NSLOT, D], BF16, addr_space="Shared").ap()
    d_scri = dint("d_scri", (128, 256), I16)
    d_scrf = dint("d_scrf", (128, 64))
    dbg = {}
    if DBG:
        def dout(name, shape, dt=F32):
            return nc.dram_tensor(name, list(shape), dt, kind="ExternalOutput").ap()
        dbg["xT32"] = dout("dbg_xT32", (128, TLOC))
        dbg["st"] = dout("dbg_st", (H * 129, 128))
        dbg["xmid"] = dout("dbg_xmid", (256, D), BF16)
        dbg["cnt"] = dout("dbg_cnt", (NCORES, E))
        dbg["gtab"] = dout("dbg_gtab", (2 * E, 128))
        dbg["gst"] = dout("dbg_gst", (E, 64))
        dbg["rtab"] = dout("dbg_rtab", (128, RTW))
        dbg["ybuf"] = dout("dbg_ybuf", (128, D), BF16)
        dbg["fin"] = dout("dbg_fin", (128, TLOC))
        dbg["slot"] = dout("dbg_slot", (128, NTT * 8))
        dbg["stok"] = dout("dbg_stok", (128, 32))
        dbg["wec"] = dout("dbg_wec", (128, 32))
        dbg["w8"] = dout("dbg_w8", (128, NTT * 8))
        dbg["we"] = dout("dbg_we", (128, 32))
        dbg["rt"] = dout("dbg_rt", (128, 32 * RTW))
        dbg["val8"] = dout("dbg_val8", (128, NTT * 8))
        dbg["wn8"] = dout("dbg_wn8", (128, NTT * 8))

    from contextlib import ExitStack
    ctx = ExitStack()
    with ctx:
        def sbuf(name, shape, dt=F32, stack=None):
            cm = nc.sbuf_tensor(name, list(shape), dt)
            return (stack or ctx).enter_context(cm)

        def psum(name, shape, dt=F32):
            return ctx.enter_context(nc.psum_tensor(name, list(shape), dt))

        s_xmid = sbuf("s_xmid", (128, 8 * TLOC), F32)
        s_id32 = sbuf("s_id32", (128, 128))
        s_id16 = sbuf("s_id16", (128, 128), BF16)
        s_trib = sbuf("s_trib", (128, 129), BF16)
        s_ones128b = sbuf("s_ones128b", (128, 128), BF16)
        s_ones128f = sbuf("s_ones128f", (128, 128))
        s_maskA = sbuf("s_maskA", (128, CH))
        s_iotae = sbuf("s_iotae", (128, E))
        s_giotae = sbuf("s_giotae", (128, E))
        s_g8 = sbuf("s_g8", (128, NTT * 8))
        s_msel = sbuf("s_msel", (128, 8))
        s_ctens = sbuf("s_ctens", (128, 32))
        s_ecol = sbuf("s_ecol", (128, 1))
        s_gsidx = sbuf("s_gsidx", (128, 64), I16)
        s_rowsc = sbuf("s_rowsc", (128, 2 * D))
        s_f32a = sbuf("s_f32a", (128, 1024))
        s_f32b = sbuf("s_f32b", (128, 1024))
        s_w8 = sbuf("s_w8", (128, NTT * 8))
        s_e8 = sbuf("s_e8", (128, NTT * 8))
        s_pos8 = sbuf("s_pos8", (128, NTT * 8))
        s_val8 = sbuf("s_val8", (128, NTT * 8))
        s_wn8 = sbuf("s_wn8", (128, NTT * 8))
        s_slot8 = sbuf("s_slot8", (128, NTT * 8))
        s_slot8i = sbuf("s_slot8i", (128, NTT * 8), I16)
        s_ninv = sbuf("s_ninv", (128, NTT))
        s_ninvb = sbuf("s_ninvb", (128, NTT), BF16)
        s_ninvrow = sbuf("s_ninvrow", (1, TLOC), BF16)
        s_yidx = sbuf("s_yidx", (128, TLOC // 2), I16)
        s_sidx = sbuf("s_sidx", (128, 256), I16)
        s_sidx2 = sbuf("s_sidx2", (128, 256), I16)
        s_gidx = sbuf("s_gidx", (128, 256), I16)
        s_kstar = sbuf("s_kstar", (128, 32))
        s_kstari = sbuf("s_kstari", (128, 32), I16)
        s_stok = sbuf("s_stok", (128, 32))
        s_stoki = sbuf("s_stoki", (128, 32), I16)
        s_we = sbuf("s_we", (128, 32))
        s_wec = sbuf("s_wec", (128, 32))
        s_totc = sbuf("s_totc", (128, 1))
        s_base = sbuf("s_base", (128, E))
        s_cnt_all = sbuf("s_cnt_all", (NCORES, E))
        s_gstc = sbuf("s_gstc", (64, 17))
        s_bgl = sbuf("s_bgl", (64, 1))
        s_y0 = sbuf("s_y0", (1, D), BF16)
        s_colsc = sbuf("s_colsc", (128, 8))
        s_epscol = sbuf("s_epscol", (128, 1))
        s_dcol = sbuf("s_dcol", (128, 8))

        pnames = ["pA", "pB", "pC", "pD", "pS", "pT", "pU", "pV"]
        P = {n: psum(n, (128, 512)) for n in pnames}

        KROT = 3
        semnames = ["MM", "VE", "SC", "CC"]
        for base in ["LD", "ST", "GP", "GL", "GM"]:
            semnames += [f"{base}{i}" for i in range(KROT)]
        sems = {n: ctx.enter_context(nc.semaphore(n)) for n in semnames}
        cnt = {n: 0 for n in sems}
        for base in ["LD", "ST", "GP", "GL", "GM"]:
            cnt[base] = 0
        rot = {"LD": 0, "ST": 0, "GP": 0, "GL": 0, "GM": 0}

        block = ctx.enter_context(nc.Block())

        seqc = [0]

        class _Rec:
            __slots__ = ("prog",)

            def __init__(self):
                self.prog = []

            def __getattr__(self, mname):
                prog = self.prog

                def call(*a, **kw):
                    seqc[0] += 1
                    item = [mname, a, kw, None, seqc[0]]
                    prog.append(item)

                    class _Then:
                        def then_inc(self, sem_, n_):
                            item[3] = (sem_, n_)
                    return _Then()
                return call

        ty, tt, tv, ts, tg = _Rec(), _Rec(), _Rec(), _Rec(), _Rec()

        def mark(name):
            _MARKS[name] = seqc[0]

        def bump(name, n=1):
            cnt[name] += n
            return cnt[name]

        def dma(e_, out_, in_, sem="LD", **kw):
            if e_ is tg and sem == "LD":
                sem = "GL"
            kw.setdefault("allow_slow_non_contiguous", True)
            i = rot[sem]
            rot[sem] = (i + 1) % KROT
            n_ = f"{sem}{i}"
            if cnt[n_] > 0:
                e_.wait_ge(sems[n_], cnt[n_])
            e_.dma_start(out=out_, in_=in_, **kw).then_inc(sems[n_], 16)
            bump(n_, 16)
            bump(sem, 16)
            return cnt[n_]

        bank_free = {}

        def wbank(n):
            if n in bank_free:
                s_, t_ = bank_free[n]
                tt.wait_ge(sems[s_], t_)

        def mm(pn, out_, lhsT, rhs, start=True, stop=True):
            if start:
                wbank(pn)
            tt.matmul(out_, lhsT, rhs, start=start, stop=stop).then_inc(sems["MM"], 1)
            return bump("MM")

        def trans(pn, out_, in_, ident):
            wbank(pn)
            tt.transpose(out_, in_, ident).then_inc(sems["MM"], 1)
            return bump("MM")

        def vop(fn, *a, **kw):
            if cnt["VE"] > 0:
                tv.wait_ge(sems["VE"], cnt["VE"])
            fn(*a, **kw).then_inc(sems["VE"], 1)
            return bump("VE")

        def sop(fn, *a, **kw):
            if cnt["SC"] > 0:
                ts.wait_ge(sems["SC"], cnt["SC"])
            fn(*a, **kw).then_inc(sems["SC"], 1)
            return bump("SC")

        def evict_v(pn, fn, *a, **kw):
            t_ = vop(fn, *a, **kw)
            bank_free[pn] = ("VE", t_)
            return t_

        def evict_s(pn, fn, *a, **kw):
            t_ = sop(fn, *a, **kw)
            bank_free[pn] = ("SC", t_)
            return t_

        DMA_SEMS = ("LD", "ST", "GL", "GP", "GM")

        def w(e_, sem, tick):
            if sem in DMA_SEMS:
                for i in range(KROT):
                    n_ = f"{sem}{i}"
                    if cnt[n_] > 0:
                        e_.wait_ge(sems[n_], cnt[n_])
                return
            if tick > 0:
                e_.wait_ge(sems[sem], tick)

        def _wst(e_):
            w(e_, "ST", 0)

        def _wld(e_):
            w(e_, "LD", 0)

        def gather(out_ap, in_ap, idxs_ap, n_idx):
            i = rot["GP"]
            rot["GP"] = (i + 1) % KROT
            n_ = f"GP{i}"
            if cnt[n_] > 0:
                tg.wait_ge(sems[n_], cnt[n_])
            tg.dma_gather(out_ap=out_ap, in_ap=in_ap, idxs_ap=idxs_ap,
                          num_idxs=n_idx, num_idxs_reg=n_idx,
                          elem_size=in_ap.shape[-1]).then_inc(sems[n_], 16)
            bump(n_, 16)
            bump("GP", 16)
            return cnt[n_]

        def allgather(din_, dout_, tick_st):
            w(tg, "ST", tick_st)
            tg.collective_compute(
                "AllGather", ALU.bypass, replica_groups=[list(range(NCORES))],
                ins=[din_], outs=[dout_]).then_inc(sems["CC"], 1)
            return bump("CC")

        def join_all():
            snap = dict(cnt)
            for e_ in (ty, tt, tv, ts, tg):
                for sname in semnames:
                    if sname != "CC" and snap[sname] > 0:
                        e_.wait_ge(sems[sname], snap[sname])

        st_xT = ExitStack()
        s_xT = sbuf("s_xT", (128, 8 * TLOC), BF16, stack=st_xT)
        st_osb = ExitStack()
        s_osb = sbuf("s_osb", (128, H * TLOC), BF16, stack=st_osb)
        st_attn = ExitStack()
        s_q = sbuf("s_q", (128, TLOC), stack=st_attn)
        s_k = sbuf("s_k", (128, TLOC), stack=st_attn)
        s_lam = sbuf("s_lam", (128, TLOC), stack=st_attn)
        s_eq = sbuf("s_eq", (128, TLOC), stack=st_attn)
        s_ek = sbuf("s_ek", (128, TLOC), stack=st_attn)
        xbv = s_xmid[:].bitcast(BF16)

        def carve16(b0, cols):
            return xbv[:, b0 // 2:b0 // 2 + cols]

        def carve32(b0, cols):
            return s_xmid[:, b0 // 4:b0 // 4 + cols]

        s_qhat = xbv[:, 0:16384]
        s_qd = carve16(32768, TLOC)
        s_kd = carve16(32768 + 4096, TLOC)
        s_khT = carve16(32768 + 8192, TLOC)
        s_v = carve16(32768 + 12288, TLOC)
        s_mscan = carve16(32768 + 16384, TLOC)
        s_wslice = carve16(32768 + 20480, 1024)
        s_sti = carve32(32768 + 22528, 2 * 128)
        s_state = carve32(32768 + 23552, 128)
        s_lend = carve32(32768 + 24064, NCH)
        s_eend = carve32(32768 + 24192, NCH)
        s_linc = carve32(32768 + 24320, NCH)
        s_small = carve32(32768 + 24448, NCH)
        s_sin = carve32(32768 + 24576, 128)
        s_sinb = carve16(32768 + 25088, 128)
        s_am = carve16(32768 + 25344, 16 * CH)
        s_stateb = carve16(32768 + 29696, 128)
        s_wslice2 = carve16(32768 + 29952, 1024)

        for dst, src in [(s_id32[:], ident32), (s_id16[:], ident16),
                         (s_trib[:], trib), (s_ones128b[:], ones128b), (s_ones128f[:], ones128f),
                         (s_mscan[:], mask_scan), (s_maskA[:], maskA),
                         (s_iotae[:], iota_e), (s_giotae[:], giota_e),
                         (s_msel[:], msel),
                         (s_ctens[:], ctens), (s_ecol[:], ecol),
                         (s_gsidx[:], gsidx)]:
            dma(ty, dst, src)
        t_const = cnt["LD"]
        vop(tv.memset, s_epscol[:], EPS_LN)
        vop(tv.memset, s_yidx[:], 0)
        vop(tv.memset, s_sidx[:], 0)
        vop(tv.memset, s_sidx2[:], 0)
        vop(tv.memset, s_gidx[:], 0)
        tg.load_library(_MLP_LIB)

        for tti in range(NTT):
            buf = s_rowsc[:, 0:D]
            stg = s_rowsc[:, D:2 * D]
            lt = dma(ty, buf, x_loc[tti * 128:(tti + 1) * 128, :])
            w(tv, "ST", cnt["ST"])
            for dc in range(8):
                pn = "pT" if dc % 2 == 0 else "pU"
                ps = P[pn][:, 0:128]
                w(tt, "LD", lt)
                trans(pn, ps, buf[:, dc * 128:(dc + 1) * 128], s_id32[:])
                tk = cnt["MM"]
                w(tv, "MM", tk)
                vop(tv.tensor_copy, out=s_xT[:, dc * TLOC + tti * 128:
                                             dc * TLOC + (tti + 1) * 128], in_=ps)
                evict_v(pn, tv.tensor_copy, out=stg[:, dc * 128:(dc + 1) * 128], in_=ps)
            tke = cnt["VE"]
            w(ty, "VE", tke)
            dma(ty, d_xT32.rearrange("p (c t) -> p c t", t=TLOC)[:, :, tti * 128:(tti + 1) * 128],
                stg.rearrange("p (c q) -> p c q", q=128), sem="ST")
            _wst(ty)
        t_xT_all = cnt["VE"]

        mark("P0")
        wseq = []
        for hh in range(H):
            for wt_ in (wg, wq, wk, wv):
                wseq.append((wt_, hh * 128))
        wbufs = [s_wslice, s_wslice2]
        wl = {"i": 0, "c": 0, "use": [0, 0]}

        def wissue():
            i = wl["i"]
            if i >= len(wseq):
                return
            wt_, hof_ = wseq[i]
            par = i % 2
            w(tg, "MM", wl["use"][par])
            dma(tg, wbufs[par][:].rearrange("p (c f) -> p c f", f=128),
                wt_[:, hof_:hof_ + 128].rearrange("(c p) f -> p c f", p=128),
                sem="GL" if par == 0 else "GM")
            wl["i"] = i + 1

        def wopen():
            par = wl["c"] % 2
            w(tt, "GL" if par == 0 else "GM", 0)
            return wbufs[par]

        def wclose():
            par = wl["c"] % 2
            wl["use"][par] = cnt["MM"]
            wl["c"] += 1
            wissue()

        wissue()
        wissue()
        pendC = []

        def drainC(n=None):
            k = len(pendC) if n is None else min(n, len(pendC))
            for _ in range(k):
                pendC.pop(0)()

        for h in range(H):
            hof = h * 128

            def fproj(wt, dst, bks):
                wsl = wopen()
                w(tt, "VE", t_xT_all)
                for tc in range(4):
                    pn = bks[tc % 2]
                    ps = P[pn][:, 0:512]
                    for dc in range(8):
                        mm(pn, ps, wsl[:, dc * 128:(dc + 1) * 128],
                           s_xT[:, dc * TLOC + tc * 512:dc * TLOC + (tc + 1) * 512],
                           start=(dc == 0), stop=(dc == 7))
                    tk = cnt["MM"]
                    w(ts, "MM", tk)
                    evict_s(pn, ts.copy, out=dst[:, tc * 512:(tc + 1) * 512], in_=ps)
                    drainC(3)
                wclose()

            fproj(wg, s_lam, ("pA", "pB"))
            sop(ts.activation, out=s_lam[:], in_=s_lam[:], func=AF.Sigmoid)
            t_sig = sop(ts.activation, out=s_lam[:], in_=s_lam[:], func=AF.Ln)
            fproj(wq, s_q, ("pC", "pD"))
            fproj(wk, s_k, ("pA", "pB"))
            drainC(None)
            tC_last = cnt["MM"]
            t_qkg = cnt["SC"]
            w(tv, "SC", t_sig)
            w(tv, "LD", t_const)
            vop(tv.tensor_tensor_scan, out=s_lam[:], data0=s_mscan[:],
                data1=s_lam[:], initial=0.0, op0=ALU.mult, op1=ALU.add)
            lam3 = s_lam[:].rearrange("p (c k) -> p c k", k=CH)
            vop(tv.tensor_copy, out=s_lend[:], in_=lam3[:, :, CH - 1])
            t_lend = cnt["VE"]
            w(ts, "VE", t_lend)
            t_eend = sop(ts.activation, out=s_eend[:], in_=s_lend[:], func=AF.Exp)
            w(ts, "MM", tC_last)
            sop(ts.activation, out=s_qd[:], in_=s_q[:], func=AF.Square)
            t_q2 = cnt["SC"]
            sop(ts.activation, out=s_kd[:], in_=s_k[:], func=AF.Square)
            t_k2 = cnt["SC"]
            vop(tv.tensor_copy, out=s_linc[:], in_=s_lend[:])
            srcb, dstb = s_linc, s_small
            sh = 1
            while sh < NCH:
                vop(tv.tensor_copy, out=dstb[:, 0:sh], in_=srcb[:, 0:sh])
                vop(tv.tensor_add, out=dstb[:, sh:NCH], in0=srcb[:, sh:NCH],
                    in1=srcb[:, 0:NCH - sh])
                srcb, dstb = dstb, srcb
                sh *= 2
            if srcb is not s_linc:
                vop(tv.tensor_copy, out=s_linc[:], in_=srcb[:])
            vop(tv.tensor_sub, out=s_small[:], in0=s_linc[:], in1=s_lend[:])
            t_lexc = cnt["VE"]
            w(ts, "VE", t_lexc)
            t_elexc = sop(ts.activation, out=s_small[:], in_=s_small[:], func=AF.Exp)
            w(ts, "ST", cnt["ST"])
            sop(ts.activation, out=s_colsc[:, 0:1], in_=s_linc[:, NCH - 1:NCH],
                func=AF.Exp)
            w(ty, "SC", cnt["SC"])
            dma(ty, d_st_in[h, 128:129, :].rearrange("a p -> p a"),
                s_colsc[:, 0:1], sem="ST")

            wsl = wopen()
            for tti in range(NTT):
                pn = "pC" if tti % 2 == 0 else "pD"
                ps = P[pn][:, 0:128]
                for dc in range(8):
                    mm(pn, ps,
                       s_xT[:, dc * TLOC + tti * 128:dc * TLOC + (tti + 1) * 128],
                       wsl[:, dc * 128:(dc + 1) * 128],
                       start=(dc == 0), stop=(dc == 7))
                tk = cnt["MM"]
                w(tv, "MM", tk)
                evict_v(pn, tv.tensor_copy, out=s_v[:, tti * 128:(tti + 1) * 128], in_=ps)
            wclose()

            w(tt, "SC", t_q2)
            for tc in range(4):
                pn = "pA" if tc % 2 == 0 else "pB"
                ps = P[pn][:, 0:512]
                mm(pn, ps, s_ones128b[:], s_qd[:, tc * 512:(tc + 1) * 512])
                tk = cnt["MM"]
                w(ts, "MM", tk)
                evict_s(pn, ts.activation, out=s_eq[:, tc * 512:(tc + 1) * 512],
                        in_=ps, func=AF.Ln)
            w(tt, "SC", t_k2)
            for tc in range(4):
                pn = "pA" if tc % 2 == 0 else "pB"
                ps = P[pn][:, 0:512]
                mm(pn, ps, s_ones128b[:], s_kd[:, tc * 512:(tc + 1) * 512])
                tk = cnt["MM"]
                w(ts, "MM", tk)
                evict_s(pn, ts.activation, out=s_ek[:, tc * 512:(tc + 1) * 512],
                        in_=ps, func=AF.Ln)
            t_lnsk = cnt["SC"]

            w(tv, "SC", t_lnsk)
            t_kd_half = [0, 0]
            t_eqk = 0
            for hf in (0, 1):
                sl = slice(hf * 1024, (hf + 1) * 1024)
                vop(tv.scalar_tensor_tensor, out=s_eq[:, sl], in0=s_eq[:, sl],
                    scalar=-0.5, in1=s_lam[:, sl], op0=ALU.mult, op1=ALU.add)
                vop(tv.scalar_tensor_tensor, out=s_ek[:, sl], in0=s_ek[:, sl],
                    scalar=-0.5, in1=s_lam[:, sl], op0=ALU.mult, op1=ALU.subtract)
                t_lams = cnt["VE"]
                w(ts, "VE", t_lams)
                sop(ts.activation, out=s_eq[:, sl], in_=s_eq[:, sl], func=AF.Exp)
                t_eqk = sop(ts.activation, out=s_ek[:, sl], in_=s_ek[:, sl],
                            func=AF.Exp)
                w(tv, "SC", t_eqk)
                vop(tv.tensor_mul, out=s_qd[:, sl], in0=s_q[:, sl], in1=s_eq[:, sl])
                vop(tv.tensor_mul, out=s_kd[:, sl], in0=s_k[:, sl], in1=s_ek[:, sl])
                t_kd_half[hf] = cnt["VE"]
            eend3 = s_eend[:].unsqueeze(2).broadcast_to([128, NCH, CH])
            ek3 = s_ek[:].rearrange("p (c k) -> p c k", k=CH)
            lam3b = s_lam[:].rearrange("p (c k) -> p c k", k=CH)
            vop(tv.tensor_tensor, out=lam3b, in0=ek3, in1=eend3, op=ALU.mult)
            vop(tv.tensor_mul, out=s_lam[:], in0=s_k[:], in1=s_lam[:])
            t_khf = cnt["VE"]
            w(tv, "SC", t_elexc)
            sm3 = s_small[:].unsqueeze(2).broadcast_to([128, NCH, CH])
            eq3 = s_eq[:].rearrange("p (c k) -> p c k", k=CH)
            vop(tv.tensor_tensor, out=eq3, in0=eq3, in1=sm3, op=ALU.mult)
            vop(tv.tensor_mul, out=s_qhat[:, h * TLOC:(h + 1) * TLOC],
                in0=s_q[:], in1=s_eq[:])
            t_qhat = cnt["VE"]

            w(tt, "VE", t_kd_half[0])
            pnr = ["pA", "pB", "pC", "pD"]
            for c in range(NCH):
                if c == 16:
                    w(tt, "VE", t_kd_half[1])
                cof = c * CH
                base = (c % 2) * 64
                pnA = pnr[c % 4]
                psA = P[pnA][base:base + 64, 0:CH]
                tkA = mm(pnA, psA, s_kd[:, cof:cof + CH], s_qd[:, cof:cof + CH])
                amb = s_am[base:base + 64, (c // 2) * CH:(c // 2 + 1) * CH]
                w(tv, "MM", tkA)
                evict_v(pnA, tv.tensor_tensor, out=amb, in0=psA,
                        in1=s_maskA[base:base + 64, :], op=ALU.mult)
            t_amall = cnt["VE"]
            w(tt, "VE", t_khf)
            for tti in range(NTT):
                pn = "pS" if tti % 2 == 0 else "pT"
                ps = P[pn][:, 0:128]
                trans(pn, ps, s_lam[:, tti * 128:(tti + 1) * 128], s_id32[:])
                tk = cnt["MM"]
                w(tv, "MM", tk)
                evict_v(pn, tv.tensor_copy, out=s_khT[:, tti * 128:(tti + 1) * 128], in_=ps)
            t_khT = cnt["VE"]
            w(tt, "VE", t_amall)
            for c in range(NCH):
                cof = c * CH
                base = (c % 2) * 64
                tti_ = c // 2
                pnO = pnr[c % 4]
                psO = P[pnO][:, 0:CH]
                vsl = s_v[base:base + 64, tti_ * 128:(tti_ + 1) * 128]
                amb = s_am[base:base + 64, tti_ * CH:(tti_ + 1) * CH]
                tkO = mm(pnO, psO, vsl, amb)
                w(tv, "MM", tkO)
                evict_v(pnO, tv.tensor_copy,
                        out=s_osb[:, h * TLOC + cof:h * TLOC + cof + CH], in_=psO)
            t_ointra = cnt["VE"]
            stD = {"prev": 0}

            def _copen(t_oi=t_ointra, D=stD):
                w(tv, "ST", cnt["ST"])
                vop(tv.memset, s_state[:], 0.0)
                vop(tv.memset, s_stateb[:], 0.0)
                w(tt, "VE", t_oi)
                D["prev"] = cnt["VE"]

            pendC.append(_copen)
            for c in range(NCH):
                def _cchunk(c=c, hx=h, D=stD):
                    cof = c * CH
                    base = (c % 2) * 64
                    tti_ = c // 2
                    khsl = s_khT[base:base + 64, tti_ * 128:(tti_ + 1) * 128]
                    vsl = s_v[base:base + 64, tti_ * 128:(tti_ + 1) * 128]
                    if c > 0:
                        pnI = "pU" if c % 2 == 0 else "pV"
                        psI = P[pnI][:, 0:CH]
                        wbank(pnI)
                        w(tt, "VE", D["prev"])
                        tt.matmul(psI, s_stateb[:], s_qd[:, cof:cof + CH],
                                  start=True, stop=True).then_inc(sems["MM"], 1)
                        bump("MM")
                        tkI = cnt["MM"]
                    pnS = "pS" if c % 2 == 0 else "pT"
                    psS = P[pnS][:, 128:256]
                    tkS = mm(pnS, psS, khsl, vsl)
                    w(tv, "MM", tkS)
                    vop(tv.scalar_tensor_tensor, out=s_state[:], in0=s_state[:],
                        scalar=s_eend[:, c:c + 1], in1=psS,
                        op0=ALU.mult, op1=ALU.add)
                    t_st = vop(tv.tensor_copy, out=s_stateb[:], in_=s_state[:])
                    bank_free[pnS] = ("VE", t_st)
                    if c > 0:
                        w(tv, "MM", tkI)
                        evict_v(pnI, tv.tensor_tensor,
                                out=s_osb[:, hx * TLOC + cof:hx * TLOC + cof + CH],
                                in0=s_osb[:, hx * TLOC + cof:hx * TLOC + cof + CH],
                                in1=psI, op=ALU.add)
                    D["prev"] = t_st

                pendC.append(_cchunk)

            def _cstore(hx=h):
                w(ty, "VE", cnt["VE"])
                dma(ty, d_st_in[hx, 0:128, :], s_state[:], sem="ST")

            pendC.append(_cstore)
        drainC(None)
        t_p1_st = cnt["ST"]

        mark("P1")
        t_cc_st = allgather(d_st_in[:, :, :], d_st_out[:, :, :], t_p1_st)
        w(ty, "CC", t_cc_st)
        t_corr_mm = 0
        t_sti_use = 0
        for h in range(H):
            w(ty, "VE", t_sti_use)
            for i in range(7):
                dma(ty, s_dcol[:, i:i + 1],
                    d_st_out[i * H + h, 128:129, :].rearrange("a p -> p a"))
            l0 = dma(ty, s_sti[:, 0:128], d_st_out[0 * H + h, 0:128, :])
            w(tv, "LD", l0)
            w(tv, "MM", t_corr_mm)
            t_half = [0, 0]
            t_half[0] = vop(tv.tensor_scalar, out=s_sin[:], in0=s_sti[:, 0:128],
                            scalar1=s_msel[:, 0:1], scalar2=None, op0=ALU.mult)
            for i in range(1, 7):
                sl_ = s_sti[:, (i % 2) * 128:(i % 2) * 128 + 128]
                w(ty, "VE", t_half[i % 2])
                li = dma(ty, sl_, d_st_out[i * H + h, 0:128, :])
                vop(tv.tensor_scalar, out=s_colsc[:, 7:8], in0=s_dcol[:, i:i + 1],
                    scalar1=-1.0, scalar2=None, op0=ALU.add)
                vop(tv.tensor_scalar, out=s_colsc[:, 7:8], in0=s_colsc[:, 7:8],
                    scalar1=s_msel[:, i:i + 1], scalar2=1.0, op0=ALU.mult, op1=ALU.add)
                vop(tv.tensor_scalar, out=s_sin[:], in0=s_sin[:],
                    scalar1=s_colsc[:, 7:8], scalar2=None, op0=ALU.mult)
                w(tv, "LD", li)
                t_half[i % 2] = vop(tv.scalar_tensor_tensor, out=s_sin[:], in0=sl_,
                                    scalar=s_msel[:, i:i + 1], in1=s_sin[:],
                                    op0=ALU.mult, op1=ALU.add)
                t_sti_use = t_half[i % 2]
            t_sin = vop(tv.tensor_copy, out=s_sinb[:], in_=s_sin[:])
            t_sti_use = t_sin
            w(tt, "VE", t_sin)
            for tc in range(4):
                pn = "pA" if tc % 2 == 0 else "pB"
                ps = P[pn][:, 0:512]
                mm(pn, ps, s_sinb[:],
                   s_qhat[:, h * TLOC + tc * 512:h * TLOC + (tc + 1) * 512])
                tk = cnt["MM"]
                t_corr_mm = tk
                w(tv, "MM", tk)
                evict_v(pn, tv.tensor_tensor,
                        out=s_osb[:, h * TLOC + tc * 512:h * TLOC + (tc + 1) * 512],
                        in0=s_osb[:, h * TLOC + tc * 512:h * TLOC + (tc + 1) * 512],
                        in1=ps, op=ALU.add)
        t_p2 = cnt["VE"]

        mark("P2")
        join_all()
        st_attn.close()
        st_wo = ExitStack()
        s_wob = sbuf("s_wob", (128, 8 * 1024), BF16, stack=st_wo)

        lwo = dma(tg, s_wob[:].rearrange("p (c f) -> p c f", f=1024),
                  wo.rearrange("(c p) f -> p c f", p=128))
        w(tt, "GL", lwo)
        w(tt, "VE", t_p2)
        w(ty, "MM", cnt["MM"])
        w(ty, "VE", cnt["VE"])
        w(ty, "SC", cnt["SC"])
        for dt_ in range(8):
            dma(ty, s_xmid[:, dt_ * TLOC:(dt_ + 1) * TLOC],
                d_xT32[:, dt_ * TLOC:(dt_ + 1) * TLOC])
        pns4 = ["pA", "pB", "pC", "pD"]
        for dt_ in range(8):
            for hv in range(8):
                for tc in range(4):
                    mm(pns4[tc], P[pns4[tc]][:, 0:512],
                       s_wob[:, hv * 1024 + dt_ * 128:hv * 1024 + dt_ * 128 + 128],
                       s_osb[:, hv * TLOC + tc * 512:hv * TLOC + (tc + 1) * 512],
                       start=(hv == 0), stop=(hv == 7))
            tk = cnt["MM"]
            w(tv, "MM", tk)
            w(tv, "LD", 0)
            for tc in range(4):
                xsl = s_xmid[:, dt_ * TLOC + tc * 512:dt_ * TLOC + (tc + 1) * 512]
                t_ev = evict_v(pns4[tc], tv.tensor_tensor, out=xsl, in0=xsl,
                               in1=P[pns4[tc]][:, 0:512], op=ALU.add)
        t_resid = cnt["VE"]

        def layernorm_inplace(t_ready):
            w(ts, "VE", t_ready)
            t_done = 0
            for tc in range(4):
                pnS = "pC" if tc % 2 == 0 else "pD"
                psS = P[pnS][:, 0:512]
                pnQ = "pS" if tc % 2 == 0 else "pT"
                psQ = P[pnQ][:, 0:512]
                for dt_ in range(8):
                    sq_t = sop(ts.activation, out=s_f32b[:, 0:512],
                               in_=s_xmid[:, dt_ * TLOC + tc * 512:dt_ * TLOC + (tc + 1) * 512],
                               func=AF.Square)
                    w(tt, "VE", t_ready)
                    mm(pnS, psS, s_ones128f[:],
                       s_xmid[:, dt_ * TLOC + tc * 512:dt_ * TLOC + (tc + 1) * 512],
                       start=(dt_ == 0), stop=(dt_ == 7))
                    w(tt, "SC", sq_t)
                    mm(pnQ, psQ, s_ones128f[:], s_f32b[:, 0:512],
                       start=(dt_ == 0), stop=(dt_ == 7))
                    ts.wait_ge(sems["MM"], cnt["MM"])
                tkq_ = cnt["MM"]
                w(tv, "MM", tkq_)
                vop(tv.tensor_scalar, out=s_f32a[:, 0:512], in0=psS,
                    scalar1=1.0 / D, scalar2=None, op0=ALU.mult)
                bank_free[pnS] = ("VE", cnt["VE"])
                vop(tv.tensor_scalar, out=s_f32b[:, 0:512], in0=psQ,
                    scalar1=1.0 / D, scalar2=None, op0=ALU.mult)
                bank_free[pnQ] = ("VE", cnt["VE"])
                vop(tv.tensor_tensor, out=s_f32b[:, 512:1024],
                    in0=s_f32a[:, 0:512], in1=s_f32a[:, 0:512], op=ALU.mult)
                vop(tv.tensor_sub, out=s_f32b[:, 512:1024],
                    in0=s_f32b[:, 0:512], in1=s_f32b[:, 512:1024])
                tvar = cnt["VE"]
                w(ts, "VE", tvar)
                sop(ts.activation, out=s_f32b[:, 512:1024],
                    in_=s_f32b[:, 512:1024], func=AF.Ln, bias=s_epscol[:, 0:1])
                trst = sop(ts.activation, out=s_f32b[:, 512:1024],
                           in_=s_f32b[:, 512:1024], func=AF.Exp, scale=-0.5)
                w(tv, "SC", trst)
                for dt_ in range(8):
                    sl = slice(dt_ * TLOC + tc * 512, dt_ * TLOC + (tc + 1) * 512)
                    vop(tv.tensor_sub, out=s_xmid[:, sl], in0=s_xmid[:, sl],
                        in1=s_f32a[:, 0:512])
                    vop(tv.tensor_mul, out=s_xmid[:, sl], in0=s_xmid[:, sl],
                        in1=s_f32b[:, 512:1024])
                t_done = cnt["VE"]
            return t_done

        t_ln1 = layernorm_inplace(t_resid)

        w(tv, "VE", t_ln1)
        for dc in range(8):
            vop(tv.tensor_copy, out=s_xT[:, dc * TLOC:(dc + 1) * TLOC],
                in_=s_xmid[:, dc * TLOC:(dc + 1) * TLOC])
        t_xmb = cnt["VE"]

        rowb = s_rowsc[:].bitcast(BF16)
        for tti in range(NTT):
            w(tv, "ST", cnt["ST"])
            for dc in range(8):
                pn = "pS" if dc % 2 == 0 else "pT"
                ps = P[pn][:, 0:128]
                w(tt, "VE", t_ln1)
                trans(pn, ps, s_xmid[:, dc * TLOC + tti * 128:dc * TLOC + (tti + 1) * 128],
                      s_id32[:])
                tk = cnt["MM"]
                w(tv, "MM", tk)
                evict_v(pn, tv.tensor_copy,
                        out=rowb[:, dc * 128:(dc + 1) * 128], in_=ps)
            tke = cnt["VE"]
            w(ty, "VE", tke)
            dma(ty, d_xmid_in[tti * 128:(tti + 1) * 128, :], rowb[:, 0:D], sem="ST")
            _wst(ty)
        t_xmid_st = cnt["ST"]
        t_cc_xm = allgather(d_xmid_in[:, :], d_xmid_out[:, :], t_xmid_st)

        join_all()
        st_wo.close()
        st_osb.close()
        st_ffn1 = ExitStack()
        s_wgub = sbuf("s_wgub", (128, 8 * 1536), BF16, stack=st_ffn1)
        s_wsdb = sbuf("s_wsdb", (128, 6 * 1024), BF16, stack=st_ffn1)
        s_hsh = sbuf("s_hsh", (128, 6 * 512), BF16, stack=st_ffn1)
        st_p4 = ExitStack()
        s_probs = sbuf("s_probs", (128, NTT * E), stack=st_p4)
        s_h = sbuf("s_h", (128, NTT * E), BF16, stack=st_p4)
        s_cprev = sbuf("s_cprev", (128, NTT * E), stack=st_p4)
        s_cem = sbuf("s_cem", (64, 129), stack=st_p4)
        s_gcol = sbuf("s_gcol", (64, NTT), stack=st_p4)

        mark("P3")
        w(tg, "MM", cnt["MM"])
        lgu = dma(tg, s_wgub[:, 0:8 * 1536].rearrange("p (c f) -> p c f", f=1536),
                  wsgu.rearrange("(c p) f -> p c f", p=128))
        lsd = dma(tg, s_wsdb[:, 0:6 * 1024].rearrange("p (c f) -> p c f", f=1024),
                  wsd.rearrange("(c p) f -> p c f", p=128))
        lrw = dma(ty, s_f32a[:, 0:512].rearrange("p (c e) -> p c e", e=E),
                  rwt.rearrange("(c p) e -> p c e", p=128))
        w(tt, "LD", lrw)
        w(tt, "VE", t_ln1)
        for tti in range(NTT):
            pn = "pU" if tti % 2 == 0 else "pV"
            ps = P[pn][:, 0:E]
            for dc in range(8):
                mm(pn, ps, s_xmid[:, dc * TLOC + tti * 128:dc * TLOC + (tti + 1) * 128],
                   s_f32a[:, dc * E:(dc + 1) * E], start=(dc == 0), stop=(dc == 7))
            tk = cnt["MM"]
            w(tv, "MM", tk)
            evict_v(pn, tv.tensor_copy, out=s_probs[:, tti * E:(tti + 1) * E], in_=ps)
        t_logits = cnt["VE"]

        pr3 = s_probs[:].rearrange("p (t e) -> p t e", e=E)
        vop(tv.tensor_reduce, out=s_f32a[:, 0:NTT], in_=pr3, axis=AX, op=ALU.max)
        mx3 = s_f32a[:, 0:NTT].unsqueeze(2).broadcast_to([128, NTT, E])
        vop(tv.tensor_tensor, out=pr3, in0=pr3, in1=mx3, op=ALU.subtract)
        t_sm1 = cnt["VE"]
        w(ts, "VE", t_sm1)
        t_expp = sop(ts.activation, out=s_probs[:], in_=s_probs[:], func=AF.Exp)
        w(tv, "SC", t_expp)
        vop(tv.tensor_reduce, out=s_f32a[:, 0:NTT], in_=pr3, axis=AX, op=ALU.add)
        vop(tv.reciprocal, out=s_f32a[:, 0:NTT], in_=s_f32a[:, 0:NTT])
        vop(tv.tensor_tensor, out=pr3, in0=pr3, in1=mx3, op=ALU.mult)
        t_sm = cnt["VE"]

        hf3 = s_f32b[:, 0:NTT * E].rearrange("p (t e) -> p t e", e=E)
        vop(tv.memset, s_f32b[:], 0.0)
        w83 = s_w8[:].rearrange("p (t r) -> p t r", r=8)
        e83 = s_e8[:].rearrange("p (t r) -> p t r", r=8)
        for r in range(TOPK):
            vop(tv.tensor_reduce, out=w83[:, :, r], in_=pr3, axis=AX, op=ALU.max)
            wb3 = w83[:, :, r:r + 1].broadcast_to([128, NTT, E])
            eq3_ = s_cprev[:].rearrange("p (t e) -> p t e", e=E)
            vop(tv.tensor_tensor, out=eq3_, in0=pr3, in1=wb3, op=ALU.is_ge)
            vop(tv.tensor_tensor, out=hf3, in0=hf3, in1=eq3_, op=ALU.max)
            io3 = s_iotae[:].unsqueeze(1).broadcast_to([128, NTT, E])
            vop(tv.tensor_tensor, out=eq3_, in0=eq3_, in1=io3, op=ALU.mult)
            vop(tv.tensor_reduce, out=e83[:, :, r], in_=eq3_, axis=AX, op=ALU.max)
            vop(tv.tensor_tensor, out=eq3_, in0=pr3, in1=wb3, op=ALU.is_ge)
            vop(tv.scalar_tensor_tensor, out=pr3, in0=eq3_, scalar=-2.0,
                in1=pr3, op0=ALU.mult, op1=ALU.add)
        vop(tv.tensor_copy, out=s_h[:], in_=s_f32b[:, 0:NTT * E])
        t_top8 = cnt["VE"]

        vop(tv.tensor_reduce, out=s_f32b[:, 0:NTT], in_=w83, axis=AX, op=ALU.add)
        vop(tv.reciprocal, out=s_f32b[:, 0:NTT], in_=s_f32b[:, 0:NTT])
        sb3 = s_f32b[:, 0:NTT].unsqueeze(2).broadcast_to([128, NTT, 8])
        wn3 = s_wn8[:].rearrange("p (t r) -> p t r", r=8)
        vop(tv.tensor_tensor, out=wn3, in0=w83, in1=sb3, op=ALU.mult)
        t_wn = cnt["VE"]

        vop(tv.memset, s_f32b[:, 512:513], 0.0)
        t_base0 = cnt["VE"]
        w(tt, "VE", max(t_top8, t_base0))
        t_gtab_st_list = []
        for tti in range(NTT):
            h_t = s_h[:, tti * E:(tti + 1) * E]
            pnc = "pU" if tti % 2 == 0 else "pV"
            psc = P[pnc][0:64, 0:129]
            w(tv, "ST", cnt["ST"])
            mm(pnc, psc, h_t, s_trib[:])
            tk1 = cnt["MM"]
            pnd = "pS" if tti % 2 == 0 else "pT"
            psd = P[pnd][:, 0:E]
            mm(pnd, psd, s_trib[:, 0:128], h_t)
            tk2 = cnt["MM"]
            w(tv, "MM", tk1)
            bb3 = s_f32b[0:64, 512:513].unsqueeze(2).broadcast_to([64, 1, 129])
            cem3 = s_cem[:].rearrange("p (a c) -> p a c", a=1)
            ps3 = psc.rearrange("p (a c) -> p a c", a=1)
            evict_v(pnc, tv.tensor_tensor, out=cem3, in0=ps3, in1=bb3, op=ALU.add)
            w(tv, "MM", tk2)
            evict_v(pnd, tv.tensor_copy, out=s_cprev[:, tti * E:(tti + 1) * E], in_=psd)
            vop(tv.tensor_copy, out=s_gcol[:, tti:tti + 1], in_=s_cem[0:64, 0:1])
            tb = vop(tv.tensor_add, out=s_f32b[0:64, 512:513],
                     in0=s_f32b[0:64, 512:513], in1=psc[0:64, 128:129])
            bank_free[pnc] = ("VE", tb)
            tkc = cnt["VE"]
            w(ty, "VE", tkc)
            dma(ty, d_gtab_in[tti * E:(tti + 1) * E, :], s_cem[0:64, 0:128], sem="ST")
            _wst(ty)
        t_cnts = cnt["VE"]
        t_gtab_st = cnt["ST"]
        w(ty, "VE", t_cnts)
        dma(ty, d_cnt_in[0:1, :].rearrange("a p -> p a"), s_f32b[0:64, 512:513], sem="ST")
        t_cnt_st = cnt["ST"]
        t_cc_cnt = allgather(d_cnt_in[:, :], d_cnt_out[:, :], t_cnt_st)
        t_cc_gt = allgather(d_gtab_in[:, :], d_gtab_out[:, :], t_gtab_st)

        mark("P4")
        w(tt, "GL", lgu)
        w(tt, "VE", t_xmb)
        t_silu_free = 0
        for tca in range(4):
            for ft in range(6):
                png = "pA" if ft % 2 == 0 else "pB"
                psg = P[png][:, 0:512]
                pnu = "pC" if ft % 2 == 0 else "pD"
                psu = P[pnu][:, 0:512]
                for dc in range(8):
                    mm(png, psg, s_wgub[:, dc * 1536 + ft * 128:dc * 1536 + ft * 128 + 128],
                       s_xT[:, dc * TLOC + tca * 512:dc * TLOC + (tca + 1) * 512],
                       start=(dc == 0), stop=(dc == 7))
                tkg = cnt["MM"]
                for dc in range(8):
                    mm(pnu, psu,
                       s_wgub[:, dc * 1536 + 768 + ft * 128:dc * 1536 + 768 + ft * 128 + 128],
                       s_xT[:, dc * TLOC + tca * 512:dc * TLOC + (tca + 1) * 512],
                       start=(dc == 0), stop=(dc == 7))
                tku = cnt["MM"]
                w(ts, "MM", tkg)
                w(ts, "VE", t_silu_free)
                tsg = evict_s(png, ts.activation, out=s_f32a[:, 0:512],
                              in_=psg, func=AF.Silu)
                w(tv, "SC", tsg)
                w(tv, "MM", tku)
                t_silu_free = evict_v(
                    pnu, tv.tensor_tensor,
                    out=s_hsh[:, ft * 512:(ft + 1) * 512],
                    in0=s_f32a[:, 0:512], in1=psu, op=ALU.mult)
            t_hsh = cnt["VE"]
            w(tt, "GL", lsd)
            w(tt, "VE", t_hsh)
            for dt_ in range(8):
                pn = "pA" if dt_ % 2 == 0 else "pB"
                ps = P[pn][:, 0:512]
                for ft in range(6):
                    mm(pn, ps, s_wsdb[:, ft * 1024 + dt_ * 128:ft * 1024 + dt_ * 128 + 128],
                       s_hsh[:, ft * 512:(ft + 1) * 512],
                       start=(ft == 0), stop=(ft == 5))
                tk = cnt["MM"]
                w(tv, "MM", tk)
                sl = slice(dt_ * TLOC + tca * 512, dt_ * TLOC + (tca + 1) * 512)
                evict_v(pn, tv.tensor_tensor, out=s_xmid[:, sl], in0=s_xmid[:, sl],
                        in1=ps, op=ALU.add)
            w(tt, "VE", cnt["VE"])
        t_acc_sh = cnt["VE"]

        mark("P5a")
        w(ty, "CC", t_cc_cnt)
        lcnt = dma(ty, s_cnt_all[:], d_cnt_out[:, :])
        w(tt, "LD", lcnt)
        tmsel = trans("pU", P["pU"][0:8, 0:128], s_msel[:, 0:8], s_id32[:])
        w(tv, "MM", tmsel)
        tmc = evict_v("pU", tv.tensor_copy, out=s_f32a[0:8, 0:128], in_=P["pU"][0:8, 0:128])
        w(tt, "VE", tmc)
        mm("pV", P["pV"][:, 0:E], s_f32a[0:8, 0:128], s_cnt_all[0:8, :])
        tkb = cnt["MM"]
        w(tv, "MM", tkb)
        evict_v("pV", tv.tensor_copy, out=s_base[:], in_=P["pV"][:, 0:E])
        mm("pU", P["pU"][0:64, 0:1], s_cnt_all[0:8, :], s_f32a[0:8, 0:1])
        tkbe = cnt["MM"]
        w(tv, "MM", tkbe)
        evict_v("pU", tv.tensor_copy, out=s_bgl[:], in_=P["pU"][0:64, 0:1])
        t_bases = cnt["VE"]

        vop(tv.memset, s_f32b[:, 0:E], 0.0)
        vop(tv.memset, s_f32a[:, 512:640], 1.0)
        t_ones = cnt["VE"]
        w(tt, "VE", t_ones)
        for tti in range(NTT):
            sl = slice(tti * E, (tti + 1) * E)
            vop(tv.tensor_add, out=s_cprev[:, sl], in0=s_cprev[:, sl], in1=s_f32b[:, 0:E])
            vop(tv.tensor_add, out=s_cprev[:, sl], in0=s_cprev[:, sl], in1=s_base[:])
            pn = "pU" if tti % 2 == 0 else "pV"
            ps = P[pn][:, 0:E]
            mm(pn, ps, s_ones128b[:], s_h[:, sl])
            tk = cnt["MM"]
            w(tv, "MM", tk)
            evict_v(pn, tv.tensor_tensor, out=s_f32b[:, 0:E], in0=s_f32b[:, 0:E],
                    in1=ps, op=ALU.add)
        t_cglob = cnt["VE"]

        for r in range(TOPK):
            e8c3 = e83[:, :, r:r + 1].broadcast_to([128, NTT, E])
            io3 = s_iotae[:].unsqueeze(1).broadcast_to([128, NTT, E])
            eqw = s_f32a[:, 0:NTT * E].rearrange("p (t e) -> p t e", e=E)
            vop(tv.tensor_tensor, out=eqw, in0=io3, in1=e8c3, op=ALU.is_equal)
            gio3 = s_giotae[:].unsqueeze(1).broadcast_to([128, NTT, E])
            gs3 = s_probs[:].rearrange("p (t e) -> p t e", e=E)
            g83 = s_g8[:].rearrange("p (t r) -> p t r", r=8)
            vop(tv.tensor_tensor, out=gs3, in0=eqw, in1=gio3, op=ALU.mult)
            vop(tv.tensor_reduce, out=g83[:, :, r], in_=gs3, axis=AX, op=ALU.max)
            cg3 = s_cprev[:].rearrange("p (t e) -> p t e", e=E)
            vop(tv.tensor_tensor, out=eqw, in0=eqw, in1=cg3, op=ALU.mult)
            p83 = s_pos8[:].rearrange("p (t r) -> p t r", r=8)
            vop(tv.tensor_reduce, out=p83[:, :, r], in_=eqw, axis=AX, op=ALU.add)
        vop(tv.tensor_scalar, out=s_val8[:], in0=s_pos8[:], scalar1=float(CAP),
            scalar2=None, op0=ALU.is_lt)
        vop(tv.tensor_mul, out=s_wn8[:], in0=s_wn8[:], in1=s_val8[:])
        vop(tv.tensor_scalar, out=s_slot8[:], in0=s_e8[:], scalar1=float(CAP),
            scalar2=None, op0=ALU.mult)
        vop(tv.tensor_add, out=s_slot8[:], in0=s_slot8[:], in1=s_pos8[:])
        vop(tv.tensor_mul, out=s_slot8[:], in0=s_slot8[:], in1=s_val8[:])
        v83 = s_val8[:].rearrange("p (t r) -> p t r", r=8)
        vop(tv.tensor_reduce, out=s_ninv[:], in_=v83, axis=AX, op=ALU.add)
        vop(tv.tensor_scalar, out=s_ninv[:], in0=s_ninv[:], scalar1=-1.0,
            scalar2=8.0, op0=ALU.mult, op1=ALU.add)
        vop(tv.tensor_copy, out=s_ninvb[:], in_=s_ninv[:])
        A_ = s_f32a[:, 0:128]
        vop(tv.tensor_scalar, out=A_, in0=s_g8[:], scalar1=float(CAP),
            scalar2=None, op0=ALU.mult)
        vop(tv.tensor_add, out=A_, in0=A_, in1=s_pos8[:])
        vop(tv.tensor_mul, out=A_, in0=A_, in1=s_val8[:])
        vop(tv.tensor_copy, out=s_slot8i[:], in_=A_)
        t_slot = cnt["VE"]

        for tti in range(NTT):
            w(tv, "ST", cnt["ST"])
            vop(tv.memset, s_f32b[:, 64:128], 0.0)
            vop(tv.tensor_copy, out=s_f32b[:, 64:72], in_=s_e8[:, tti * 8:(tti + 1) * 8])
            vop(tv.tensor_copy, out=s_f32b[:, 72:80], in_=s_wn8[:, tti * 8:(tti + 1) * 8])
            tkr = cnt["VE"]
            w(ty, "VE", tkr)
            dma(ty, d_rtab_in[tti * 128:(tti + 1) * 128, :], s_f32b[:, 64:128], sem="ST")
            _wst(ty)
        t_rtab_st = cnt["ST"]
        t_cc_rt = allgather(d_rtab_in[:, :], d_rtab_out[:, :], t_rtab_st)

        w(tv, "VE", t_bases)
        bg3 = s_bgl[:].unsqueeze(2).broadcast_to([64, 1, NTT])
        gc3 = s_gcol[:].unsqueeze(1)
        vop(tv.tensor_tensor, out=gc3, in0=gc3, in1=bg3, op=ALU.add)
        vop(tv.memset, s_f32b[:, 128:192], 0.0)
        vop(tv.tensor_copy, out=s_f32b[0:64, 128:144], in_=s_gcol[:])
        vop(tv.tensor_copy, out=s_f32b[0:64, 144:145], in_=s_f32b[0:64, 512:513])
        t_gst = cnt["VE"]
        w(ty, "VE", t_gst)
        dma(ty, d_gst_in[:, :], s_f32b[0:64, 128:192], sem="ST")
        t_gst_st = cnt["ST"]
        t_cc_gst = allgather(d_gst_in[:, :], d_gst_out[:, :], t_gst_st)

        w(ty, "VE", t_slot)
        dma(ty, d_scri[:, 0:128], s_slot8i[:], sem="ST")
        _wst(ty)
        for g_ in range(8):
            dma(ty, s_yidx[0:16, :].rearrange("q (f gg) -> q f gg", gg=8)[:, :, g_],
                d_scri[16 * g_:16 * g_ + 16, 0:128])
        _wld(ty)
        dma(ty, s_yidx[16:32, :], s_yidx[0:16, :])
        t_yidx = cnt["LD"]
        dma(ty, d_scri[:, 128:144], s_ninvb[:].bitcast(I16), sem="ST")
        _wst(ty)
        dma(ty, s_ninvrow[0:1, :].bitcast(I16).rearrange("a (t p) -> a t p", t=NTT),
            d_scri[:, 128:144].rearrange("(a p) t -> a t p", a=1))
        t_ninvr = cnt["LD"]

        mark("P5b")
        join_all()
        st_p4.close()
        st_ffn1.close()
        st_xT.close()
        st_p6 = ExitStack()
        s_grow = sbuf("s_grow", (128, 32 * 128), stack=st_p6)
        s_rt = sbuf("s_rt", (128, 32 * RTW), stack=st_p6)
        s_gstart = sbuf("s_gstart", (128, 128), stack=st_p6)

        w(tg, "CC", t_cc_gst)
        w(tg, "LD", t_const)
        tgk = gather(s_grow[:, 0:512].rearrange("p (n d) -> p n d", d=64),
                     d_gst_out[:, :], s_gsidx[:], 1024)
        w(tv, "GP", tgk)
        gat3 = s_grow[:, 0:512].rearrange("p (n d) -> p n d", d=64)
        gst2 = s_gstart[:].rearrange("p (n k) -> p n k", k=16)
        vop(tv.tensor_copy, out=gst2, in_=gat3[:, :, 0:16])
        vop(tv.tensor_reduce, out=s_totc[:], in_=gat3[:, :, 16:17].rearrange(
            "p n a -> p (n a)"), axis=AX, op=ALU.add)
        vop(tv.tensor_copy, out=s_f32b[:, 200:208], in_=gat3[:, :, 16])
        t_gstart = cnt["VE"]

        vop(tv.memset, s_kstar[:], 0.0)
        for k in range(128):
            vop(tv.scalar_tensor_tensor, out=s_kstar[:], in0=s_ctens[:],
                scalar=s_gstart[:, k:k + 1], in1=s_kstar[:], op0=ALU.is_ge, op1=ALU.add)
        vop(tv.tensor_scalar, out=s_kstar[:], in0=s_kstar[:], scalar1=-1.0,
            scalar2=None, op0=ALU.add)
        vop(tv.tensor_scalar, out=s_f32a[:, 0:32], in0=s_kstar[:], scalar1=64.0,
            scalar2=None, op0=ALU.mult)
        vop(tv.tensor_scalar, out=s_f32a[:, 0:32], in0=s_f32a[:, 0:32],
            scalar1=s_ecol[:, 0:1], scalar2=None, op0=ALU.add)
        tgidxf = vop(tv.tensor_copy, out=s_kstari[:], in_=s_f32a[:, 0:32])
        w(ty, "VE", tgidxf)
        dma(ty, d_scri[:, 0:32], s_kstari[:], sem="ST")
        _wst(ty)
        for g_ in range(8):
            dma(ty, s_gidx[0:16, 0:256].rearrange("q (b gg) -> q b gg", gg=8)[:, :, g_],
                d_scri[16 * g_:16 * g_ + 16, 0:32])
        _wld(ty)
        dma(ty, s_gidx[16:32, :], s_gidx[0:16, :])
        t_gidx = cnt["LD"]
        w(tg, "CC", t_cc_gt)
        w(tg, "LD", t_gidx)
        w(tg, "VE", cnt["VE"])
        for cq in range(4):
            gather(s_grow[:, cq * 1024:(cq + 1) * 1024].rearrange(
                "p (n d) -> p n d", d=128),
                d_gtab_out[:, :], s_gidx[:, cq * 64:(cq + 1) * 64], 1024)
        t_growg = cnt["GP"]
        w(tv, "GP", t_growg)
        vop(tv.tensor_copy, out=s_f32a[:, 0:32], in_=s_ctens[:])
        for j in range(7):
            vop(tv.tensor_scalar, out=s_f32a[:, 32:64], in0=s_kstar[:],
                scalar1=float(16 * (j + 1)), scalar2=None, op0=ALU.is_ge)
            vop(tv.tensor_scalar, out=s_f32a[:, 32:64], in0=s_f32a[:, 32:64],
                scalar1=s_f32b[:, 200 + j:201 + j], scalar2=None, op0=ALU.mult)
            vop(tv.tensor_sub, out=s_f32a[:, 0:32], in0=s_f32a[:, 0:32],
                in1=s_f32a[:, 32:64])
        gr3 = s_grow[:].rearrange("p (b j) -> p b j", j=128)
        c3 = s_f32a[:, 0:32].unsqueeze(2).broadcast_to([128, 32, 128])
        vop(tv.tensor_tensor, out=gr3, in0=gr3, in1=c3, op=ALU.is_le)
        vop(tv.tensor_reduce, out=s_stok[:], in_=gr3, axis=AX, op=ALU.add)
        vop(tv.tensor_scalar, out=s_f32a[:, 0:32], in0=s_kstar[:], scalar1=128.0,
            scalar2=None, op0=ALU.mult)
        vop(tv.tensor_add, out=s_stok[:], in0=s_stok[:], in1=s_f32a[:, 0:32])
        vop(tv.tensor_scalar, out=s_stok[:], in0=s_stok[:], scalar1=-1.0,
            scalar2=None, op0=ALU.add)
        tstok = vop(tv.tensor_copy, out=s_stoki[:], in_=s_stok[:])
        w(ty, "VE", tstok)
        dma(ty, d_scri[:, 0:32], s_stoki[:], sem="ST")
        _wst(ty)
        for bh in range(2):
            dma(ty, s_sidx[0:16, :].rearrange("q (el cb bb) -> q el cb bb",
                                              el=8, bb=2)[:, :, :, bh],
                d_scri[:, 0:32].rearrange("(el cb) (bb q) -> q el cb bb",
                                          el=8, bb=2)[:, :, :, bh])
        for g_ in range(8):
            dma(ty, s_sidx2[0:16, :].rearrange("q (b gg) -> q b gg", gg=8)[:, :, g_],
                d_scri[16 * g_:16 * g_ + 16, 0:32])
        _wld(ty)
        dma(ty, s_sidx[16:32, :], s_sidx[0:16, :])
        dma(ty, s_sidx2[16:32, :], s_sidx2[0:16, :])
        t_sidx = cnt["LD"]

        w(tg, "CC", t_cc_rt)
        w(tg, "LD", t_sidx)
        for cq in range(4):
            gather(s_rt[:, cq * 512:(cq + 1) * 512].rearrange("p (n d) -> p n d", d=RTW),
                   d_rtab_out[:, :], s_sidx2[:, cq * 64:(cq + 1) * 64], 1024)
        t_rtg = cnt["GP"]
        w(tv, "GP", t_rtg)
        rt3 = s_rt[:].rearrange("p (b d) -> p b d", d=RTW)
        wetmp = s_f32a[:, 0:256].rearrange("p (b r) -> p b r", r=8)
        vop(tv.tensor_scalar, out=wetmp, in0=rt3[:, :, 0:8],
            scalar1=s_ecol[:, 0:1], scalar2=None, op0=ALU.is_equal)
        vop(tv.tensor_tensor, out=wetmp, in0=wetmp, in1=rt3[:, :, 8:16], op=ALU.mult)
        vop(tv.tensor_reduce, out=s_we[:], in_=wetmp, axis=AX, op=ALU.add)
        vop(tv.tensor_scalar, out=s_f32a[:, 0:32], in0=s_ctens[:],
            scalar1=s_totc[:, 0:1], scalar2=None, op0=ALU.is_lt)
        vop(tv.tensor_mul, out=s_we[:], in0=s_we[:], in1=s_f32a[:, 0:32])
        twe = cnt["VE"]
        w(ty, "VE", twe)
        dma(ty, d_scrf[:, 0:32], s_we[:], sem="ST")
        _wst(ty)
        for gg in range(4):
            dma(ty, s_wec[gg * 32:(gg + 1) * 32, :].rearrange("q (el cc) -> q el cc", el=8),
                d_scrf[:, 0:32].rearrange("(el cc g2) q -> g2 q el cc", el=8, cc=4)[gg])
        t_wec = cnt["LD"]

        mark("P6")
        join_all()
        st_p6.close()
        st_p7 = ExitStack()
        s_wex = [sbuf(f"s_wex{i}", (128, 2 * 8 * F + 6 * 1024), BF16, stack=st_p7)
                 for i in range(2)]
        s_xb = sbuf("s_xb", (128, 4 * D), BF16, stack=st_p7)
        s_xbT = sbuf("s_xbT", (128, 8 * 512), BF16, stack=st_p7)
        s_he = sbuf("s_he", (128, 6 * 512), BF16, stack=st_p7)
        s_ye = sbuf("s_ye", (128, 4 * D), BF16, stack=st_p7)

        def load_expert(el, buf):
            dma(tg, buf[:, 0:6144].rearrange("p (c f) -> p c f", f=F),
                weg[el].rearrange("(c p) f -> p c f", p=128))
            dma(tg, buf[:, 6144:12288].rearrange("p (c f) -> p c f", f=F),
                weu[el].rearrange("(c p) f -> p c f", p=128))
            return dma(tg, buf[:, 12288:18432].rearrange("p (c f) -> p c f", f=1024),
                       wed[el].rearrange("(c p) f -> p c f", p=128))

        t_wexl = [0, 0]
        t_silu_free7 = 0
        t_wexl[0] = load_expert(0, s_wex[0])
        t_ye_st = 0
        t_xb_mm = 0
        w(tg, "CC", t_cc_xm)
        w(tg, "LD", t_sidx)
        for el in range(EO):
            buf = s_wex[el % 2]
            w(tg, "MM", t_xb_mm)
            tgx = gather(s_xb[:].rearrange("p (n d) -> p n d", d=D),
                         d_xmid_out[:, :], s_sidx[:, el * 32:(el + 1) * 32], 512)
            w(tt, "GP", tgx)
            for blk in range(4):
                for dc in range(8):
                    pn = "pS" if (blk * 8 + dc) % 2 == 0 else "pT"
                    ps = P[pn][:, 0:64].bitcast(BF16)
                    trans(pn, ps, s_xb[:, blk * D + dc * 128:blk * D + (dc + 1) * 128],
                          s_id16[:])
                    tk = cnt["MM"]
                    w(tv, "MM", tk)
                    evict_v(pn, tv.tensor_copy,
                            out=s_xbT[:, dc * 512 + blk * 128:dc * 512 + (blk + 1) * 128],
                            in_=ps)
            t_xbT = cnt["VE"]
            t_xb_mm = cnt["MM"]
            w(tt, "GL", t_wexl[el % 2])
            w(tt, "VE", t_xbT)
            for ft in range(6):
                png = "pA" if ft % 2 == 0 else "pB"
                psg = P[png][:, 0:512]
                pnu = "pC" if ft % 2 == 0 else "pD"
                psu = P[pnu][:, 0:512]
                for dc in range(8):
                    mm(png, psg, buf[:, dc * F + ft * 128:dc * F + ft * 128 + 128],
                       s_xbT[:, dc * 512:(dc + 1) * 512],
                       start=(dc == 0), stop=(dc == 7))
                tkg = cnt["MM"]
                for dc in range(8):
                    mm(pnu, psu, buf[:, 6144 + dc * F + ft * 128:6144 + dc * F + ft * 128 + 128],
                       s_xbT[:, dc * 512:(dc + 1) * 512],
                       start=(dc == 0), stop=(dc == 7))
                tku = cnt["MM"]
                w(ts, "MM", tkg)
                w(ts, "VE", t_silu_free7)
                tsg = evict_s(png, ts.activation, out=s_f32a[:, 0:512],
                              in_=psg, func=AF.Silu)
                w(tv, "SC", tsg)
                w(tv, "MM", tku)
                t_silu_free7 = evict_v(
                    pnu, tv.tensor_tensor, out=s_he[:, ft * 512:(ft + 1) * 512],
                    in0=s_f32a[:, 0:512], in1=psu, op=ALU.mult)
            t_he = cnt["VE"]
            if el + 1 < EO:
                w(tg, "MM", cnt["MM"])
                t_wexl[(el + 1) % 2] = load_expert(el + 1, s_wex[(el + 1) % 2])
            w(tt, "VE", t_he)
            w(tt, "LD", t_wec)
            for ct in range(4):
                for hfd in range(2):
                    pn = "pU" if (ct * 2 + hfd) % 2 == 0 else "pV"
                    ps = P[pn][:, 0:512]
                    for ft in range(6):
                        mm(pn, ps, s_he[:, ft * 512 + ct * 128:ft * 512 + ct * 128 + 128],
                           buf[:, 12288 + ft * 1024 + hfd * 512:12288 + ft * 1024 + (hfd + 1) * 512],
                           start=(ft == 0), stop=(ft == 5))
                    tk = cnt["MM"]
                    w(tv, "MM", tk)
                    w(tv, "ST", t_ye_st)
                    evict_v(pn, tv.tensor_scalar,
                            out=s_ye[:, ct * D + hfd * 512:ct * D + (hfd + 1) * 512],
                            in0=ps, scalar1=s_wec[:, el * 4 + ct:el * 4 + ct + 1],
                            scalar2=None, op0=ALU.mult)
            t_ye = cnt["VE"]
            w(ty, "VE", t_ye)
            dma(ty, d_ybuf_in[el * CAP:(el + 1) * CAP, :].rearrange(
                "(b p) d -> p b d", p=128),
                s_ye[:].rearrange("p (b d) -> p b d", d=D), sem="ST")
            t_ye_st = cnt["ST"]
            if el % 2 == 1:
                q_ = el // 2
                t_cc_yb = allgather(d_ybuf_in[q_ * 1024:(q_ + 1) * 1024, :],
                                    d_ybuf_out[q_ * 8192:(q_ + 1) * 8192, :],
                                    cnt["ST"])

        mark("P7")
        join_all()
        st_p7.close()
        st_p8 = ExitStack()
        s_yg = sbuf("s_yg", (128, 2 * 8 * D), BF16, stack=st_p8)
        s_acc1 = sbuf("s_acc1", (128, 2 * 4 * D), BF16, stack=st_p8)

        w(ty, "CC", t_cc_yb)
        ly0 = dma(ty, s_y0[0:1, :], d_ybuf_out[0:1, :])
        t_y0 = cnt["LD"]

        t_ygread = [0, 0]
        t_trans = [0, 0]
        for tti in range(NTT):
            b_ = tti % 2
            yg = s_yg[:, b_ * 8 * D:(b_ + 1) * 8 * D]
            ac = s_acc1[:, b_ * 4 * D:(b_ + 1) * 4 * D]
            w(tg, "VE", t_ygread[b_])
            w(tg, "CC", t_cc_yb)
            w(tg, "LD", t_yidx)
            tgy = gather(yg.rearrange("p (n d) -> p n d", d=D),
                         d_ybuf_out[:, :], s_yidx[:, tti * 64:(tti + 1) * 64], 1024)
            w(tv, "GP", tgy)
            w(tv, "MM", t_trans[b_])
            for j in range(4):
                vop(tv.tensor_add, out=ac[:, j * D:(j + 1) * D],
                    in0=yg[:, (2 * j) * D:(2 * j + 1) * D],
                    in1=yg[:, (2 * j + 1) * D:(2 * j + 2) * D])
            t_ygread[b_] = cnt["VE"]
            for j in range(2):
                vop(tv.tensor_add, out=ac[:, j * D:(j + 1) * D],
                    in0=ac[:, (2 * j) * D:(2 * j + 1) * D],
                    in1=ac[:, (2 * j + 1) * D:(2 * j + 2) * D])
            vop(tv.tensor_add, out=ac[:, 0:D], in0=ac[:, 0:D],
                in1=ac[:, D:2 * D])
            t_sum = cnt["VE"]
            w(tt, "LD", max(t_y0, t_ninvr))
            pnc = "pU" if tti % 2 == 0 else "pV"
            mm(pnc, P[pnc][:, 0:512], s_ninvrow[0:1, tti * 128:(tti + 1) * 128],
               s_y0[0:1, 0:512])
            tk1 = cnt["MM"]
            pnd = "pS" if tti % 2 == 0 else "pT"
            mm(pnd, P[pnd][:, 0:512], s_ninvrow[0:1, tti * 128:(tti + 1) * 128],
               s_y0[0:1, 512:1024])
            tk2 = cnt["MM"]
            w(tv, "VE", t_sum)
            w(tv, "MM", tk2)
            evict_v(pnc, tv.tensor_tensor, out=ac[:, 0:512], in0=ac[:, 0:512],
                    in1=P[pnc][:, 0:512], op=ALU.subtract)
            evict_v(pnd, tv.tensor_tensor, out=ac[:, 512:1024], in0=ac[:, 512:1024],
                    in1=P[pnd][:, 0:512], op=ALU.subtract)
            t_rt_tile = cnt["VE"]
            for dc in range(8):
                pn = "pA" if dc % 2 == 0 else "pB"
                ps = P[pn][:, 0:64].bitcast(BF16)
                w(tt, "VE", t_rt_tile)
                trans(pn, ps, ac[:, dc * 128:(dc + 1) * 128], s_id16[:])
                tk = cnt["MM"]
                w(tv, "MM", tk)
                evict_v(pn, tv.tensor_tensor,
                        out=s_xmid[:, dc * TLOC + tti * 128:dc * TLOC + (tti + 1) * 128],
                        in0=s_xmid[:, dc * TLOC + tti * 128:dc * TLOC + (tti + 1) * 128],
                        in1=ps, op=ALU.add)
            t_trans[b_] = tk
        t_routed = cnt["VE"]

        t_ln2 = layernorm_inplace(t_routed)

        for tti in range(NTT):
            hb = (tti % 2) * D
            if tti >= 2:
                w(tv, "ST", cnt["ST"])
            for dc in range(8):
                pn = "pS" if dc % 2 == 0 else "pT"
                ps = P[pn][:, 0:128]
                w(tt, "VE", t_ln2)
                trans(pn, ps, s_xmid[:, dc * TLOC + tti * 128:dc * TLOC + (tti + 1) * 128],
                      s_id32[:])
                tk = cnt["MM"]
                w(tv, "MM", tk)
                evict_v(pn, tv.tensor_copy,
                        out=s_rowsc[:, hb + dc * 128:hb + (dc + 1) * 128], in_=ps)
            tke = cnt["VE"]
            w(ty, "VE", tke)
            dma(ty, out[tti * 128:(tti + 1) * 128, :], s_rowsc[:, hb:hb + D], sem="ST")
        _wst(ty)
        if DBG:
            join_all()
            dma(ty, dbg["xT32"], d_xT32[:, 0:TLOC], sem="ST")
            dma(ty, dbg["st"], d_st_in.rearrange("h a p -> (h a) p"), sem="ST")
            dma(ty, dbg["xmid"], d_xmid_in[0:256, :], sem="ST")
            dma(ty, dbg["cnt"], d_cnt_out[:, :], sem="ST")
            dma(ty, dbg["gtab"], d_gtab_in[0:2 * E, :], sem="ST")
            dma(ty, dbg["gst"], d_gst_in[:, :], sem="ST")
            dma(ty, dbg["rtab"], d_rtab_in[0:128, :], sem="ST")
            dma(ty, dbg["ybuf"], d_ybuf_in[0:128, :], sem="ST")
            dma(ty, dbg["fin"], s_xmid[:, 0:TLOC], sem="ST")
            dma(ty, dbg["slot"], s_slot8[:], sem="ST")
            dma(ty, dbg["stok"], s_stok[:], sem="ST")
            dma(ty, dbg["wec"], s_wec[:], sem="ST")
            dma(ty, dbg["w8"], s_w8[:], sem="ST")
            dma(ty, dbg["val8"], s_val8[:], sem="ST")
            dma(ty, dbg["wn8"], s_wn8[:], sem="ST")
            _wst(ty)
        mark("P8")
        st_p8.close()

        def _replay(rec):
            def f(engine):
                for mname, a, kw, inc, seq in rec.prog:
                    if cutoff is not None and seq > cutoff:
                        continue
                    r = getattr(engine, mname)(*a, **kw)
                    if inc is not None:
                        r.then_inc(*inc)
            return f

        block.sync(_replay(ty))
        block.tensor(_replay(tt))
        block.vector(_replay(tv))
        block.scalar(_replay(ts))
        block.gpsimd(_replay(tg))

    mybir.codegen_inst_isa_subclasses(nc)
    return nc


def kernel(**inputs):
    if "nc" not in _CACHE:
        _CACHE["nc"] = _build()
    nc = _CACHE["nc"]
    in_maps = [_host_inputs(c, inputs) for c in range(NCORES)]
    res = run_bass_kernel_spmd(nc, in_maps, core_ids=list(range(NCORES)))
    outs = [res.results[c]["out"] for c in range(NCORES)]
    full = np.concatenate(outs, axis=0).reshape(2, TGLB // 2, D)
    return full


# version 12 (final)
# speedup vs baseline: 1.0872x; 1.0124x over previous
import numpy as np
import ml_dtypes
import concourse.bass as bass
import concourse.mybir as mybir
from concourse.bass_utils import run_bass_kernel_spmd
from concourse.library_config import mlp as _MLP_LIB

F32 = mybir.dt.float32
BF16 = mybir.dt.bfloat16
I16 = mybir.dt.int16
AF = mybir.ActivationFunctionType
ALU = mybir.AluOpType
AX = mybir.AxisListType.X
BF = ml_dtypes.bfloat16

NCORES = 8
TLOC = 2048
TGLB = NCORES * TLOC
D = 1024
H = 8
CH = 64
NCH = TLOC // CH
NTT = TLOC // 128
E = 64
TOPK = 8
CAP = 512
EO = 8
NSLOT = EO * CAP
F = 768
SF = 768
RTW = 64
EPS_LN = 1e-5

_CACHE = {}
DBG = False
_MARKS = {}


def _host_inputs(core, inp):
    xf = inp["L0_attn_x"].reshape(TGLB, D)
    m = {
        "x_loc": np.ascontiguousarray(xf[core * TLOC:(core + 1) * TLOC]),
        "wq": inp["L0_attn_Wq"], "wk": inp["L0_attn_Wk"], "wv": inp["L0_attn_Wv"],
        "wg": inp["L0_attn_Wg"], "wo": inp["L0_attn_Wo"],
        "rwt": np.ascontiguousarray(inp["L0_ffn_router_w"].T),
        "wsgu": inp["L0_ffn_ws_gate_up"], "wsd": inp["L0_ffn_ws_down"],
        "weg": np.ascontiguousarray(inp["L0_ffn_w_gate"][core * EO:(core + 1) * EO]),
        "weu": np.ascontiguousarray(inp["L0_ffn_w_up"][core * EO:(core + 1) * EO]),
        "wed": np.ascontiguousarray(inp["L0_ffn_w_down"][core * EO:(core + 1) * EO]),
    }
    m["ident32"] = np.eye(128, dtype=np.float32)
    m["ident16"] = np.eye(128, dtype=BF)
    tri = np.zeros((128, 129), np.float32)
    tri[:, :128] = np.triu(np.ones((128, 128), np.float32), 1)
    tri[:, 128] = 1.0
    m["trib"] = tri.astype(BF)
    m["ones128b"] = np.ones((128, 128), dtype=BF)
    m["ones128f"] = np.ones((128, 128), dtype=np.float32)
    ms = np.ones((128, TLOC), np.float32)
    ms[:, ::CH] = 0.0
    m["mask_scan"] = ms.astype(BF)
    mA = np.triu(np.ones((CH, CH), np.float32), 0)
    m["maskA"] = np.concatenate([mA, mA], axis=0).copy()
    m["iota_e"] = np.tile(np.arange(E, dtype=np.float32)[None, :], (128, 1))
    ev = np.arange(E)
    gv = ((ev % 8) // 2) * 16 + (ev // 8) * 2 + (ev % 8) % 2
    m["giota_e"] = np.tile(gv.astype(np.float32)[None, :], (128, 1))
    m["msel"] = np.tile((np.arange(8) < core).astype(np.float32)[None, :], (128, 1))
    cb = (np.arange(128) % 16)[:, None]
    m["ctens"] = (cb * 32 + np.arange(32)[None, :]).astype(np.float32)
    m["ecol"] = (EO * core + np.arange(128) // 16).astype(np.float32)[:, None].copy()
    gs = np.zeros((128, 64), np.int16)
    for i in range(1024):
        blk, p = i // 128, i % 128
        gs[i % 16, i // 16] = blk * 64 + EO * core + p // 16
    gs[16:32] = gs[0:16]
    m["gsidx"] = gs
    return m


def _build(cutoff=None):
    nc = bass.Bass("TRN2", target_bir_lowering=False, num_swdge_queues=1)

    def din(name, shape, dt=F32):
        return nc.dram_tensor(name, list(shape), dt, kind="ExternalInput").ap()

    x_loc = din("x_loc", (TLOC, D))
    wq = din("wq", (D, D)); wk = din("wk", (D, D)); wv = din("wv", (D, D))
    wg = din("wg", (D, D)); wo = din("wo", (D, D))
    rwt = din("rwt", (D, E))
    wsgu = din("wsgu", (D, 2 * SF)); wsd = din("wsd", (SF, D))
    weg = din("weg", (EO, D, F)); weu = din("weu", (EO, D, F)); wed = din("wed", (EO, F, D))
    ident32 = din("ident32", (128, 128))
    ident16 = din("ident16", (128, 128), BF16)
    trib = din("trib", (128, 129), BF16)
    ones128b = din("ones128b", (128, 128), BF16)
    ones128f = din("ones128f", (128, 128))
    mask_scan = din("mask_scan", (128, TLOC), BF16)
    maskA = din("maskA", (128, CH))
    iota_e = din("iota_e", (128, E))
    giota_e = din("giota_e", (128, E))
    msel = din("msel", (128, 8))
    ctens = din("ctens", (128, 32))
    ecol = din("ecol", (128, 1))
    gsidx = din("gsidx", (128, 64), I16)

    out = nc.dram_tensor("out", [TLOC, D], F32, kind="ExternalOutput").ap()

    def dint(name, shape, dt=F32):
        return nc.dram_tensor(name, list(shape), dt).ap()

    d_xT32 = dint("d_xT32", (128, 8 * TLOC))
    d_st_in = dint("d_st_in", (H, 129, 128))
    d_st_out = nc.dram_tensor("d_st_out", [NCORES * H, 129, 128], F32, addr_space="Shared").ap()
    d_cnt_in = dint("d_cnt_in", (1, E))
    d_cnt_out = nc.dram_tensor("d_cnt_out", [NCORES, E], F32, addr_space="Shared").ap()
    d_gst_in = dint("d_gst_in", (E, 64))
    d_gst_out = nc.dram_tensor("d_gst_out", [NCORES * E, 64], F32, addr_space="Shared").ap()
    d_gtab_in = dint("d_gtab_in", (NTT * E, 128))
    d_gtab_out = nc.dram_tensor("d_gtab_out", [128 * E, 128], F32, addr_space="Shared").ap()
    d_xmid_in = dint("d_xmid_in", (TLOC, D), BF16)
    d_xmid_out = nc.dram_tensor("d_xmid_out", [TGLB, D], BF16, addr_space="Shared").ap()
    d_rtab_in = dint("d_rtab_in", (TLOC, RTW))
    d_rtab_out = nc.dram_tensor("d_rtab_out", [TGLB, RTW], F32, addr_space="Shared").ap()
    d_ybuf_in = dint("d_ybuf_in", (NSLOT, D), BF16)
    d_ybuf_out = nc.dram_tensor("d_ybuf_out", [NCORES * NSLOT, D], BF16, addr_space="Shared").ap()
    d_scri = dint("d_scri", (128, 256), I16)
    d_scrf = dint("d_scrf", (128, 64))
    dbg = {}
    if DBG:
        def dout(name, shape, dt=F32):
            return nc.dram_tensor(name, list(shape), dt, kind="ExternalOutput").ap()
        dbg["xT32"] = dout("dbg_xT32", (128, TLOC))
        dbg["st"] = dout("dbg_st", (H * 129, 128))
        dbg["xmid"] = dout("dbg_xmid", (256, D), BF16)
        dbg["cnt"] = dout("dbg_cnt", (NCORES, E))
        dbg["gtab"] = dout("dbg_gtab", (2 * E, 128))
        dbg["gst"] = dout("dbg_gst", (E, 64))
        dbg["rtab"] = dout("dbg_rtab", (128, RTW))
        dbg["ybuf"] = dout("dbg_ybuf", (128, D), BF16)
        dbg["fin"] = dout("dbg_fin", (128, TLOC))
        dbg["slot"] = dout("dbg_slot", (128, NTT * 8))
        dbg["stok"] = dout("dbg_stok", (128, 32))
        dbg["wec"] = dout("dbg_wec", (128, 32))
        dbg["w8"] = dout("dbg_w8", (128, NTT * 8))
        dbg["we"] = dout("dbg_we", (128, 32))
        dbg["rt"] = dout("dbg_rt", (128, 32 * RTW))
        dbg["val8"] = dout("dbg_val8", (128, NTT * 8))
        dbg["wn8"] = dout("dbg_wn8", (128, NTT * 8))

    from contextlib import ExitStack
    ctx = ExitStack()
    with ctx:
        def sbuf(name, shape, dt=F32, stack=None):
            cm = nc.sbuf_tensor(name, list(shape), dt)
            return (stack or ctx).enter_context(cm)

        def psum(name, shape, dt=F32):
            return ctx.enter_context(nc.psum_tensor(name, list(shape), dt))

        s_xmid = sbuf("s_xmid", (128, 8 * TLOC), F32)
        s_id32 = sbuf("s_id32", (128, 128))
        s_id16 = sbuf("s_id16", (128, 128), BF16)
        s_trib = sbuf("s_trib", (128, 129), BF16)
        s_ones128b = sbuf("s_ones128b", (128, 128), BF16)
        s_ones128f = sbuf("s_ones128f", (128, 128))
        s_maskA = sbuf("s_maskA", (128, CH))
        s_iotae = sbuf("s_iotae", (128, E))
        s_giotae = sbuf("s_giotae", (128, E))
        s_g8 = sbuf("s_g8", (128, NTT * 8))
        s_msel = sbuf("s_msel", (128, 8))
        s_ctens = sbuf("s_ctens", (128, 32))
        s_ecol = sbuf("s_ecol", (128, 1))
        s_gsidx = sbuf("s_gsidx", (128, 64), I16)
        s_rowsc = sbuf("s_rowsc", (128, 2 * D))
        s_f32a = sbuf("s_f32a", (128, 1024))
        s_f32b = sbuf("s_f32b", (128, 1024))
        s_w8 = sbuf("s_w8", (128, NTT * 8))
        s_e8 = sbuf("s_e8", (128, NTT * 8))
        s_pos8 = sbuf("s_pos8", (128, NTT * 8))
        s_val8 = sbuf("s_val8", (128, NTT * 8))
        s_wn8 = sbuf("s_wn8", (128, NTT * 8))
        s_slot8 = sbuf("s_slot8", (128, NTT * 8))
        s_slot8i = sbuf("s_slot8i", (128, NTT * 8), I16)
        s_ninv = sbuf("s_ninv", (128, NTT))
        s_ninvb = sbuf("s_ninvb", (128, NTT), BF16)
        s_ninvrow = sbuf("s_ninvrow", (1, TLOC), BF16)
        s_yidx = sbuf("s_yidx", (128, TLOC // 2), I16)
        s_sidx = sbuf("s_sidx", (128, 256), I16)
        s_sidx2 = sbuf("s_sidx2", (128, 256), I16)
        s_gidx = sbuf("s_gidx", (128, 256), I16)
        s_kstar = sbuf("s_kstar", (128, 32))
        s_kstari = sbuf("s_kstari", (128, 32), I16)
        s_stok = sbuf("s_stok", (128, 32))
        s_stoki = sbuf("s_stoki", (128, 32), I16)
        s_we = sbuf("s_we", (128, 32))
        s_wec = sbuf("s_wec", (128, 32))
        s_totc = sbuf("s_totc", (128, 1))
        s_base = sbuf("s_base", (128, E))
        s_cnt_all = sbuf("s_cnt_all", (NCORES, E))
        s_gstc = sbuf("s_gstc", (64, 17))
        s_bgl = sbuf("s_bgl", (64, 1))
        s_y0 = sbuf("s_y0", (1, D), BF16)
        s_colsc = sbuf("s_colsc", (128, 8))
        s_epscol = sbuf("s_epscol", (128, 1))
        s_dcol = sbuf("s_dcol", (128, 8))

        pnames = ["pA", "pB", "pC", "pD", "pS", "pT", "pU", "pV"]
        P = {n: psum(n, (128, 512)) for n in pnames}

        KROT = 3
        semnames = ["MM", "VE", "SC", "CC"]
        for base in ["LD", "ST", "GP", "GL", "GM"]:
            semnames += [f"{base}{i}" for i in range(KROT)]
        sems = {n: ctx.enter_context(nc.semaphore(n)) for n in semnames}
        cnt = {n: 0 for n in sems}
        for base in ["LD", "ST", "GP", "GL", "GM"]:
            cnt[base] = 0
        rot = {"LD": 0, "ST": 0, "GP": 0, "GL": 0, "GM": 0}

        block = ctx.enter_context(nc.Block())

        seqc = [0]

        class _Rec:
            __slots__ = ("prog",)

            def __init__(self):
                self.prog = []

            def __getattr__(self, mname):
                prog = self.prog

                def call(*a, **kw):
                    seqc[0] += 1
                    item = [mname, a, kw, None, seqc[0]]
                    prog.append(item)

                    class _Then:
                        def then_inc(self, sem_, n_):
                            item[3] = (sem_, n_)
                    return _Then()
                return call

        ty, tt, tv, ts, tg = _Rec(), _Rec(), _Rec(), _Rec(), _Rec()

        def mark(name):
            _MARKS[name] = seqc[0]

        def bump(name, n=1):
            cnt[name] += n
            return cnt[name]

        def dma(e_, out_, in_, sem="LD", **kw):
            if e_ is tg and sem == "LD":
                sem = "GL"
            kw.setdefault("allow_slow_non_contiguous", True)
            i = rot[sem]
            rot[sem] = (i + 1) % KROT
            n_ = f"{sem}{i}"
            if cnt[n_] > 0:
                e_.wait_ge(sems[n_], cnt[n_])
            e_.dma_start(out=out_, in_=in_, **kw).then_inc(sems[n_], 16)
            bump(n_, 16)
            bump(sem, 16)
            return cnt[n_]

        bank_free = {}

        def wbank(n):
            if n in bank_free:
                s_, t_ = bank_free[n]
                tt.wait_ge(sems[s_], t_)

        def mm(pn, out_, lhsT, rhs, start=True, stop=True):
            if start:
                wbank(pn)
            tt.matmul(out_, lhsT, rhs, start=start, stop=stop).then_inc(sems["MM"], 1)
            return bump("MM")

        def trans(pn, out_, in_, ident):
            wbank(pn)
            tt.transpose(out_, in_, ident).then_inc(sems["MM"], 1)
            return bump("MM")

        def vop(fn, *a, **kw):
            if cnt["VE"] > 0:
                tv.wait_ge(sems["VE"], cnt["VE"])
            fn(*a, **kw).then_inc(sems["VE"], 1)
            return bump("VE")

        def sop(fn, *a, **kw):
            if cnt["SC"] > 0:
                ts.wait_ge(sems["SC"], cnt["SC"])
            fn(*a, **kw).then_inc(sems["SC"], 1)
            return bump("SC")

        def evict_v(pn, fn, *a, **kw):
            t_ = vop(fn, *a, **kw)
            bank_free[pn] = ("VE", t_)
            return t_

        def evict_s(pn, fn, *a, **kw):
            t_ = sop(fn, *a, **kw)
            bank_free[pn] = ("SC", t_)
            return t_

        DMA_SEMS = ("LD", "ST", "GL", "GP", "GM")

        def w(e_, sem, tick):
            if sem in DMA_SEMS:
                for i in range(KROT):
                    n_ = f"{sem}{i}"
                    if cnt[n_] > 0:
                        e_.wait_ge(sems[n_], cnt[n_])
                return
            if tick > 0:
                e_.wait_ge(sems[sem], tick)

        def _wst(e_):
            w(e_, "ST", 0)

        def _wld(e_):
            w(e_, "LD", 0)

        def gather(out_ap, in_ap, idxs_ap, n_idx):
            i = rot["GP"]
            rot["GP"] = (i + 1) % KROT
            n_ = f"GP{i}"
            if cnt[n_] > 0:
                tg.wait_ge(sems[n_], cnt[n_])
            tg.dma_gather(out_ap=out_ap, in_ap=in_ap, idxs_ap=idxs_ap,
                          num_idxs=n_idx, num_idxs_reg=n_idx,
                          elem_size=in_ap.shape[-1]).then_inc(sems[n_], 16)
            bump(n_, 16)
            bump("GP", 16)
            return cnt[n_]

        def allgather(din_, dout_, tick_st):
            w(tg, "ST", tick_st)
            tg.collective_compute(
                "AllGather", ALU.bypass, replica_groups=[list(range(NCORES))],
                ins=[din_], outs=[dout_]).then_inc(sems["CC"], 1)
            return bump("CC")

        def join_all():
            snap = dict(cnt)
            for e_ in (ty, tt, tv, ts, tg):
                for sname in semnames:
                    if sname != "CC" and snap[sname] > 0:
                        e_.wait_ge(sems[sname], snap[sname])

        st_xT = ExitStack()
        s_xT = sbuf("s_xT", (128, 8 * TLOC), BF16, stack=st_xT)
        st_osb = ExitStack()
        s_osb = sbuf("s_osb", (128, H * TLOC), BF16, stack=st_osb)
        st_attn = ExitStack()
        s_q = sbuf("s_q", (128, TLOC), stack=st_attn)
        s_k = sbuf("s_k", (128, TLOC), stack=st_attn)
        s_lam = sbuf("s_lam", (128, TLOC), stack=st_attn)
        s_eq = sbuf("s_eq", (128, TLOC), stack=st_attn)
        s_ek = sbuf("s_ek", (128, TLOC), stack=st_attn)
        xbv = s_xmid[:].bitcast(BF16)

        def carve16(b0, cols):
            return xbv[:, b0 // 2:b0 // 2 + cols]

        def carve32(b0, cols):
            return s_xmid[:, b0 // 4:b0 // 4 + cols]

        s_qhat = xbv[:, 0:16384]
        s_qd = carve16(32768, TLOC)
        s_kd = carve16(32768 + 4096, TLOC)
        s_khT = carve16(32768 + 8192, TLOC)
        s_v = carve16(32768 + 12288, TLOC)
        s_mscan = carve16(32768 + 16384, TLOC)
        s_wslice = carve16(32768 + 20480, 1024)
        s_sti = carve32(32768 + 22528, 2 * 128)
        s_state = carve32(32768 + 23552, 128)
        s_lend = carve32(32768 + 24064, NCH)
        s_eend = carve32(32768 + 24192, NCH)
        s_linc = carve32(32768 + 24320, NCH)
        s_small = carve32(32768 + 24448, NCH)
        s_sin = carve32(32768 + 24576, 128)
        s_sinb = carve16(32768 + 25088, 128)
        s_am = carve16(32768 + 25344, 16 * CH)
        s_stateb = carve16(32768 + 29696, 128)
        s_wslice2 = carve16(32768 + 29952, 1024)

        for dst, src in [(s_id32[:], ident32), (s_id16[:], ident16),
                         (s_trib[:], trib), (s_ones128b[:], ones128b), (s_ones128f[:], ones128f),
                         (s_mscan[:], mask_scan), (s_maskA[:], maskA),
                         (s_iotae[:], iota_e), (s_giotae[:], giota_e),
                         (s_msel[:], msel),
                         (s_ctens[:], ctens), (s_ecol[:], ecol),
                         (s_gsidx[:], gsidx)]:
            dma(ty, dst, src)
        t_const = cnt["LD"]
        vop(tv.memset, s_epscol[:], EPS_LN)
        vop(tv.memset, s_yidx[:], 0)
        vop(tv.memset, s_sidx[:], 0)
        vop(tv.memset, s_sidx2[:], 0)
        vop(tv.memset, s_gidx[:], 0)
        tg.load_library(_MLP_LIB)

        xbufs = [(s_rowsc[:, 0:D], s_rowsc[:, D:2 * D]),
                 (s_xmid[:, 0:1024], s_xmid[:, 1024:2048])]
        dma(ty, xbufs[0][0], x_loc[0:128, :])
        dma(tg, xbufs[1][0], x_loc[128:256, :])
        for tti in range(NTT):
            b_ = tti % 2
            buf, stg = xbufs[b_]
            cls = "LD" if b_ == 0 else "GL"
            w(tv, "ST", cnt["ST"])
            for dc in range(8):
                pn = "pT" if dc % 2 == 0 else "pU"
                ps = P[pn][:, 0:128]
                w(tt, cls, 0)
                trans(pn, ps, buf[:, dc * 128:(dc + 1) * 128], s_id32[:])
                tk = cnt["MM"]
                w(tv, "MM", tk)
                vop(tv.tensor_copy, out=s_xT[:, dc * TLOC + tti * 128:
                                             dc * TLOC + (tti + 1) * 128], in_=ps)
                evict_v(pn, tv.tensor_copy, out=stg[:, dc * 128:(dc + 1) * 128], in_=ps)
            tke = cnt["VE"]
            w(ty, "VE", tke)
            dma(ty, d_xT32.rearrange("p (c t) -> p c t", t=TLOC)[:, :, tti * 128:(tti + 1) * 128],
                stg.rearrange("p (c q) -> p c q", q=128), sem="ST")
            if tti + 2 < NTT:
                eng = ty if b_ == 0 else tg
                w(eng, "VE", tke)
                dma(eng, buf, x_loc[(tti + 2) * 128:(tti + 3) * 128, :])
        t_xT_all = cnt["VE"]

        mark("P0")
        wseq = []
        for hh in range(H):
            for wt_ in (wg, wq, wk, wv):
                wseq.append((wt_, hh * 128))
        wbufs = [s_wslice, s_wslice2]
        wl = {"i": 0, "c": 0, "use": [0, 0]}

        def wissue():
            i = wl["i"]
            if i >= len(wseq):
                return
            wt_, hof_ = wseq[i]
            par = i % 2
            w(tg, "MM", wl["use"][par])
            dma(tg, wbufs[par][:].rearrange("p (c f) -> p c f", f=128),
                wt_[:, hof_:hof_ + 128].rearrange("(c p) f -> p c f", p=128),
                sem="GL" if par == 0 else "GM")
            wl["i"] = i + 1

        def wopen():
            par = wl["c"] % 2
            w(tt, "GL" if par == 0 else "GM", 0)
            return wbufs[par]

        def wclose():
            par = wl["c"] % 2
            wl["use"][par] = cnt["MM"]
            wl["c"] += 1
            wissue()

        wissue()
        wissue()
        pendC = []

        def drainC(n=None):
            k = len(pendC) if n is None else min(n, len(pendC))
            for _ in range(k):
                pendC.pop(0)()

        for h in range(H):
            hof = h * 128

            def fproj(wt, dst, bks):
                wsl = wopen()
                w(tt, "VE", t_xT_all)
                for tc in range(4):
                    pn = bks[tc % 2]
                    ps = P[pn][:, 0:512]
                    for dc in range(8):
                        mm(pn, ps, wsl[:, dc * 128:(dc + 1) * 128],
                           s_xT[:, dc * TLOC + tc * 512:dc * TLOC + (tc + 1) * 512],
                           start=(dc == 0), stop=(dc == 7))
                    tk = cnt["MM"]
                    w(ts, "MM", tk)
                    evict_s(pn, ts.copy, out=dst[:, tc * 512:(tc + 1) * 512], in_=ps)
                    drainC(3)
                wclose()

            fproj(wg, s_lam, ("pA", "pB"))
            sop(ts.activation, out=s_lam[:], in_=s_lam[:], func=AF.Sigmoid)
            t_sig = sop(ts.activation, out=s_lam[:], in_=s_lam[:], func=AF.Ln)
            fproj(wq, s_q, ("pC", "pD"))
            fproj(wk, s_k, ("pA", "pB"))
            drainC(None)
            tC_last = cnt["MM"]
            t_qkg = cnt["SC"]
            w(tv, "SC", t_sig)
            w(tv, "LD", t_const)
            vop(tv.tensor_tensor_scan, out=s_lam[:], data0=s_mscan[:],
                data1=s_lam[:], initial=0.0, op0=ALU.mult, op1=ALU.add)
            lam3 = s_lam[:].rearrange("p (c k) -> p c k", k=CH)
            vop(tv.tensor_copy, out=s_lend[:], in_=lam3[:, :, CH - 1])
            t_lend = cnt["VE"]
            w(ts, "VE", t_lend)
            t_eend = sop(ts.activation, out=s_eend[:], in_=s_lend[:], func=AF.Exp)
            w(ts, "MM", tC_last)
            sop(ts.activation, out=s_qd[:], in_=s_q[:], func=AF.Square)
            t_q2 = cnt["SC"]
            sop(ts.activation, out=s_kd[:], in_=s_k[:], func=AF.Square)
            t_k2 = cnt["SC"]
            vop(tv.tensor_copy, out=s_linc[:], in_=s_lend[:])
            srcb, dstb = s_linc, s_small
            sh = 1
            while sh < NCH:
                vop(tv.tensor_copy, out=dstb[:, 0:sh], in_=srcb[:, 0:sh])
                vop(tv.tensor_add, out=dstb[:, sh:NCH], in0=srcb[:, sh:NCH],
                    in1=srcb[:, 0:NCH - sh])
                srcb, dstb = dstb, srcb
                sh *= 2
            if srcb is not s_linc:
                vop(tv.tensor_copy, out=s_linc[:], in_=srcb[:])
            vop(tv.tensor_sub, out=s_small[:], in0=s_linc[:], in1=s_lend[:])
            t_lexc = cnt["VE"]
            w(ts, "VE", t_lexc)
            t_elexc = sop(ts.activation, out=s_small[:], in_=s_small[:], func=AF.Exp)
            w(ts, "ST", cnt["ST"])
            sop(ts.activation, out=s_colsc[:, 0:1], in_=s_linc[:, NCH - 1:NCH],
                func=AF.Exp)
            w(ty, "SC", cnt["SC"])
            dma(ty, d_st_in[h, 128:129, :].rearrange("a p -> p a"),
                s_colsc[:, 0:1], sem="ST")

            wsl = wopen()
            for tti in range(NTT):
                pn = "pC" if tti % 2 == 0 else "pD"
                ps = P[pn][:, 0:128]
                for dc in range(8):
                    mm(pn, ps,
                       s_xT[:, dc * TLOC + tti * 128:dc * TLOC + (tti + 1) * 128],
                       wsl[:, dc * 128:(dc + 1) * 128],
                       start=(dc == 0), stop=(dc == 7))
                tk = cnt["MM"]
                w(tv, "MM", tk)
                evict_v(pn, tv.tensor_copy, out=s_v[:, tti * 128:(tti + 1) * 128], in_=ps)
            wclose()

            w(tt, "SC", t_q2)
            for tc in range(4):
                pn = "pA" if tc % 2 == 0 else "pB"
                ps = P[pn][:, 0:512]
                mm(pn, ps, s_ones128b[:], s_qd[:, tc * 512:(tc + 1) * 512])
                tk = cnt["MM"]
                w(ts, "MM", tk)
                evict_s(pn, ts.activation, out=s_eq[:, tc * 512:(tc + 1) * 512],
                        in_=ps, func=AF.Ln)
            w(tt, "SC", t_k2)
            for tc in range(4):
                pn = "pA" if tc % 2 == 0 else "pB"
                ps = P[pn][:, 0:512]
                mm(pn, ps, s_ones128b[:], s_kd[:, tc * 512:(tc + 1) * 512])
                tk = cnt["MM"]
                w(ts, "MM", tk)
                evict_s(pn, ts.activation, out=s_ek[:, tc * 512:(tc + 1) * 512],
                        in_=ps, func=AF.Ln)
            t_lnsk = cnt["SC"]

            w(tv, "SC", t_lnsk)
            t_kd_half = [0, 0]
            t_eqk = 0
            for hf in (0, 1):
                sl = slice(hf * 1024, (hf + 1) * 1024)
                vop(tv.scalar_tensor_tensor, out=s_eq[:, sl], in0=s_eq[:, sl],
                    scalar=-0.5, in1=s_lam[:, sl], op0=ALU.mult, op1=ALU.add)
                vop(tv.scalar_tensor_tensor, out=s_ek[:, sl], in0=s_ek[:, sl],
                    scalar=-0.5, in1=s_lam[:, sl], op0=ALU.mult, op1=ALU.subtract)
                t_lams = cnt["VE"]
                w(ts, "VE", t_lams)
                sop(ts.activation, out=s_eq[:, sl], in_=s_eq[:, sl], func=AF.Exp)
                t_eqk = sop(ts.activation, out=s_ek[:, sl], in_=s_ek[:, sl],
                            func=AF.Exp)
                w(tv, "SC", t_eqk)
                vop(tv.tensor_mul, out=s_qd[:, sl], in0=s_q[:, sl], in1=s_eq[:, sl])
                vop(tv.tensor_mul, out=s_kd[:, sl], in0=s_k[:, sl], in1=s_ek[:, sl])
                t_kd_half[hf] = cnt["VE"]
            eend3 = s_eend[:].unsqueeze(2).broadcast_to([128, NCH, CH])
            ek3 = s_ek[:].rearrange("p (c k) -> p c k", k=CH)
            lam3b = s_lam[:].rearrange("p (c k) -> p c k", k=CH)
            vop(tv.tensor_tensor, out=lam3b, in0=ek3, in1=eend3, op=ALU.mult)
            vop(tv.tensor_mul, out=s_lam[:], in0=s_k[:], in1=s_lam[:])
            t_khf = cnt["VE"]
            w(tv, "SC", t_elexc)
            sm3 = s_small[:].unsqueeze(2).broadcast_to([128, NCH, CH])
            eq3 = s_eq[:].rearrange("p (c k) -> p c k", k=CH)
            vop(tv.tensor_tensor, out=eq3, in0=eq3, in1=sm3, op=ALU.mult)
            vop(tv.tensor_mul, out=s_qhat[:, h * TLOC:(h + 1) * TLOC],
                in0=s_q[:], in1=s_eq[:])
            t_qhat = cnt["VE"]

            w(tt, "VE", t_kd_half[0])
            pnr = ["pA", "pB", "pC", "pD"]
            for c in range(NCH):
                if c == 16:
                    w(tt, "VE", t_kd_half[1])
                cof = c * CH
                base = (c % 2) * 64
                pnA = pnr[c % 4]
                psA = P[pnA][base:base + 64, 0:CH]
                tkA = mm(pnA, psA, s_kd[:, cof:cof + CH], s_qd[:, cof:cof + CH])
                amb = s_am[base:base + 64, (c // 2) * CH:(c // 2 + 1) * CH]
                w(tv, "MM", tkA)
                evict_v(pnA, tv.tensor_tensor, out=amb, in0=psA,
                        in1=s_maskA[base:base + 64, :], op=ALU.mult)
            t_amall = cnt["VE"]
            w(tt, "VE", t_khf)
            for tti in range(NTT):
                pn = "pS" if tti % 2 == 0 else "pT"
                ps = P[pn][:, 0:128]
                trans(pn, ps, s_lam[:, tti * 128:(tti + 1) * 128], s_id32[:])
                tk = cnt["MM"]
                w(tv, "MM", tk)
                evict_v(pn, tv.tensor_copy, out=s_khT[:, tti * 128:(tti + 1) * 128], in_=ps)
            t_khT = cnt["VE"]
            w(tt, "VE", t_amall)
            for c in range(NCH):
                cof = c * CH
                base = (c % 2) * 64
                tti_ = c // 2
                pnO = pnr[c % 4]
                psO = P[pnO][:, 0:CH]
                vsl = s_v[base:base + 64, tti_ * 128:(tti_ + 1) * 128]
                amb = s_am[base:base + 64, tti_ * CH:(tti_ + 1) * CH]
                tkO = mm(pnO, psO, vsl, amb)
                w(tv, "MM", tkO)
                evict_v(pnO, tv.tensor_copy,
                        out=s_osb[:, h * TLOC + cof:h * TLOC + cof + CH], in_=psO)
            t_ointra = cnt["VE"]
            stD = {"prev": 0}

            def _copen(t_oi=t_ointra, D=stD):
                w(tv, "ST", cnt["ST"])
                vop(tv.memset, s_state[:], 0.0)
                vop(tv.memset, s_stateb[:], 0.0)
                w(tt, "VE", t_oi)
                D["prev"] = cnt["VE"]

            pendC.append(_copen)
            for c in range(NCH):
                def _cchunk(c=c, hx=h, D=stD):
                    cof = c * CH
                    base = (c % 2) * 64
                    tti_ = c // 2
                    khsl = s_khT[base:base + 64, tti_ * 128:(tti_ + 1) * 128]
                    vsl = s_v[base:base + 64, tti_ * 128:(tti_ + 1) * 128]
                    if c > 0:
                        pnI = "pU" if c % 2 == 0 else "pV"
                        psI = P[pnI][:, 0:CH]
                        wbank(pnI)
                        w(tt, "VE", D["prev"])
                        tt.matmul(psI, s_stateb[:], s_qd[:, cof:cof + CH],
                                  start=True, stop=True).then_inc(sems["MM"], 1)
                        bump("MM")
                        tkI = cnt["MM"]
                    pnS = "pS" if c % 2 == 0 else "pT"
                    psS = P[pnS][:, 128:256]
                    tkS = mm(pnS, psS, khsl, vsl)
                    w(tv, "MM", tkS)
                    vop(tv.scalar_tensor_tensor, out=s_state[:], in0=s_state[:],
                        scalar=s_eend[:, c:c + 1], in1=psS,
                        op0=ALU.mult, op1=ALU.add)
                    t_st = vop(tv.tensor_copy, out=s_stateb[:], in_=s_state[:])
                    bank_free[pnS] = ("VE", t_st)
                    if c > 0:
                        w(tv, "MM", tkI)
                        evict_v(pnI, tv.tensor_tensor,
                                out=s_osb[:, hx * TLOC + cof:hx * TLOC + cof + CH],
                                in0=s_osb[:, hx * TLOC + cof:hx * TLOC + cof + CH],
                                in1=psI, op=ALU.add)
                    D["prev"] = t_st

                pendC.append(_cchunk)

            def _cstore(hx=h):
                w(ty, "VE", cnt["VE"])
                dma(ty, d_st_in[hx, 0:128, :], s_state[:], sem="ST")

            pendC.append(_cstore)
        drainC(None)
        t_p1_st = cnt["ST"]

        mark("P1")
        t_cc_st = allgather(d_st_in[:, :, :], d_st_out[:, :, :], t_p1_st)
        w(ty, "CC", t_cc_st)
        t_corr_mm = 0
        t_sti_use = 0
        for h in range(H):
            w(ty, "VE", t_sti_use)
            for i in range(7):
                dma(ty, s_dcol[:, i:i + 1],
                    d_st_out[i * H + h, 128:129, :].rearrange("a p -> p a"))
            l0 = dma(ty, s_sti[:, 0:128], d_st_out[0 * H + h, 0:128, :])
            w(tv, "LD", l0)
            w(tv, "MM", t_corr_mm)
            t_half = [0, 0]
            t_half[0] = vop(tv.tensor_scalar, out=s_sin[:], in0=s_sti[:, 0:128],
                            scalar1=s_msel[:, 0:1], scalar2=None, op0=ALU.mult)
            for i in range(1, 7):
                sl_ = s_sti[:, (i % 2) * 128:(i % 2) * 128 + 128]
                w(ty, "VE", t_half[i % 2])
                li = dma(ty, sl_, d_st_out[i * H + h, 0:128, :])
                vop(tv.tensor_scalar, out=s_colsc[:, 7:8], in0=s_dcol[:, i:i + 1],
                    scalar1=-1.0, scalar2=None, op0=ALU.add)
                vop(tv.tensor_scalar, out=s_colsc[:, 7:8], in0=s_colsc[:, 7:8],
                    scalar1=s_msel[:, i:i + 1], scalar2=1.0, op0=ALU.mult, op1=ALU.add)
                vop(tv.tensor_scalar, out=s_sin[:], in0=s_sin[:],
                    scalar1=s_colsc[:, 7:8], scalar2=None, op0=ALU.mult)
                w(tv, "LD", li)
                t_half[i % 2] = vop(tv.scalar_tensor_tensor, out=s_sin[:], in0=sl_,
                                    scalar=s_msel[:, i:i + 1], in1=s_sin[:],
                                    op0=ALU.mult, op1=ALU.add)
                t_sti_use = t_half[i % 2]
            t_sin = vop(tv.tensor_copy, out=s_sinb[:], in_=s_sin[:])
            t_sti_use = t_sin
            w(tt, "VE", t_sin)
            for tc in range(4):
                pn = "pA" if tc % 2 == 0 else "pB"
                ps = P[pn][:, 0:512]
                mm(pn, ps, s_sinb[:],
                   s_qhat[:, h * TLOC + tc * 512:h * TLOC + (tc + 1) * 512])
                tk = cnt["MM"]
                t_corr_mm = tk
                w(tv, "MM", tk)
                evict_v(pn, tv.tensor_tensor,
                        out=s_osb[:, h * TLOC + tc * 512:h * TLOC + (tc + 1) * 512],
                        in0=s_osb[:, h * TLOC + tc * 512:h * TLOC + (tc + 1) * 512],
                        in1=ps, op=ALU.add)
        t_p2 = cnt["VE"]

        mark("P2")
        join_all()
        st_attn.close()
        st_wo = ExitStack()
        s_wob = sbuf("s_wob", (128, 8 * 1024), BF16, stack=st_wo)

        lwo = dma(tg, s_wob[:].rearrange("p (c f) -> p c f", f=1024),
                  wo.rearrange("(c p) f -> p c f", p=128))
        w(tt, "GL", lwo)
        w(tt, "VE", t_p2)
        w(ty, "MM", cnt["MM"])
        w(ty, "VE", cnt["VE"])
        w(ty, "SC", cnt["SC"])
        for dt_ in range(8):
            dma(ty, s_xmid[:, dt_ * TLOC:(dt_ + 1) * TLOC],
                d_xT32[:, dt_ * TLOC:(dt_ + 1) * TLOC])
        pns4 = ["pA", "pB", "pC", "pD"]
        for dt_ in range(8):
            for hv in range(8):
                for tc in range(4):
                    mm(pns4[tc], P[pns4[tc]][:, 0:512],
                       s_wob[:, hv * 1024 + dt_ * 128:hv * 1024 + dt_ * 128 + 128],
                       s_osb[:, hv * TLOC + tc * 512:hv * TLOC + (tc + 1) * 512],
                       start=(hv == 0), stop=(hv == 7))
            tk = cnt["MM"]
            w(tv, "MM", tk)
            w(tv, "LD", 0)
            for tc in range(4):
                xsl = s_xmid[:, dt_ * TLOC + tc * 512:dt_ * TLOC + (tc + 1) * 512]
                t_ev = evict_v(pns4[tc], tv.tensor_tensor, out=xsl, in0=xsl,
                               in1=P[pns4[tc]][:, 0:512], op=ALU.add)
        t_resid = cnt["VE"]

        def layernorm_inplace(t_ready):
            w(ts, "VE", t_ready)
            t_done = 0
            for tc in range(4):
                pnS = "pC" if tc % 2 == 0 else "pD"
                psS = P[pnS][:, 0:512]
                pnQ = "pS" if tc % 2 == 0 else "pT"
                psQ = P[pnQ][:, 0:512]
                for dt_ in range(8):
                    sq_t = sop(ts.activation, out=s_f32b[:, 0:512],
                               in_=s_xmid[:, dt_ * TLOC + tc * 512:dt_ * TLOC + (tc + 1) * 512],
                               func=AF.Square)
                    w(tt, "VE", t_ready)
                    mm(pnS, psS, s_ones128f[:],
                       s_xmid[:, dt_ * TLOC + tc * 512:dt_ * TLOC + (tc + 1) * 512],
                       start=(dt_ == 0), stop=(dt_ == 7))
                    w(tt, "SC", sq_t)
                    mm(pnQ, psQ, s_ones128f[:], s_f32b[:, 0:512],
                       start=(dt_ == 0), stop=(dt_ == 7))
                    ts.wait_ge(sems["MM"], cnt["MM"])
                tkq_ = cnt["MM"]
                w(tv, "MM", tkq_)
                vop(tv.tensor_scalar, out=s_f32a[:, 0:512], in0=psS,
                    scalar1=1.0 / D, scalar2=None, op0=ALU.mult)
                bank_free[pnS] = ("VE", cnt["VE"])
                vop(tv.tensor_scalar, out=s_f32b[:, 0:512], in0=psQ,
                    scalar1=1.0 / D, scalar2=None, op0=ALU.mult)
                bank_free[pnQ] = ("VE", cnt["VE"])
                vop(tv.tensor_tensor, out=s_f32b[:, 512:1024],
                    in0=s_f32a[:, 0:512], in1=s_f32a[:, 0:512], op=ALU.mult)
                vop(tv.tensor_sub, out=s_f32b[:, 512:1024],
                    in0=s_f32b[:, 0:512], in1=s_f32b[:, 512:1024])
                tvar = cnt["VE"]
                w(ts, "VE", tvar)
                sop(ts.activation, out=s_f32b[:, 512:1024],
                    in_=s_f32b[:, 512:1024], func=AF.Ln, bias=s_epscol[:, 0:1])
                trst = sop(ts.activation, out=s_f32b[:, 512:1024],
                           in_=s_f32b[:, 512:1024], func=AF.Exp, scale=-0.5)
                w(tv, "SC", trst)
                for dt_ in range(8):
                    sl = slice(dt_ * TLOC + tc * 512, dt_ * TLOC + (tc + 1) * 512)
                    vop(tv.tensor_sub, out=s_xmid[:, sl], in0=s_xmid[:, sl],
                        in1=s_f32a[:, 0:512])
                    vop(tv.tensor_mul, out=s_xmid[:, sl], in0=s_xmid[:, sl],
                        in1=s_f32b[:, 512:1024])
                t_done = cnt["VE"]
            return t_done

        t_ln1 = layernorm_inplace(t_resid)

        w(tv, "VE", t_ln1)
        for dc in range(8):
            vop(tv.tensor_copy, out=s_xT[:, dc * TLOC:(dc + 1) * TLOC],
                in_=s_xmid[:, dc * TLOC:(dc + 1) * TLOC])
        t_xmb = cnt["VE"]

        rowb = s_rowsc[:].bitcast(BF16)
        for tti in range(NTT):
            w(tv, "ST", cnt["ST"])
            for dc in range(8):
                pn = "pS" if dc % 2 == 0 else "pT"
                ps = P[pn][:, 0:128]
                w(tt, "VE", t_ln1)
                trans(pn, ps, s_xmid[:, dc * TLOC + tti * 128:dc * TLOC + (tti + 1) * 128],
                      s_id32[:])
                tk = cnt["MM"]
                w(tv, "MM", tk)
                evict_v(pn, tv.tensor_copy,
                        out=rowb[:, dc * 128:(dc + 1) * 128], in_=ps)
            tke = cnt["VE"]
            w(ty, "VE", tke)
            dma(ty, d_xmid_in[tti * 128:(tti + 1) * 128, :], rowb[:, 0:D], sem="ST")
            _wst(ty)
        t_xmid_st = cnt["ST"]
        t_cc_xm = allgather(d_xmid_in[:, :], d_xmid_out[:, :], t_xmid_st)

        join_all()
        st_wo.close()
        st_osb.close()
        st_ffn1 = ExitStack()
        s_wgub = sbuf("s_wgub", (128, 8 * 1536), BF16, stack=st_ffn1)
        s_wsdb = sbuf("s_wsdb", (128, 6 * 1024), BF16, stack=st_ffn1)
        s_hsh = sbuf("s_hsh", (128, 6 * 512), BF16, stack=st_ffn1)
        st_p4 = ExitStack()
        s_probs = sbuf("s_probs", (128, NTT * E), stack=st_p4)
        s_h = sbuf("s_h", (128, NTT * E), BF16, stack=st_p4)
        s_cprev = sbuf("s_cprev", (128, NTT * E), stack=st_p4)
        s_cem = sbuf("s_cem", (64, 129), stack=st_p4)
        s_gcol = sbuf("s_gcol", (64, NTT), stack=st_p4)

        mark("P3")
        w(tg, "MM", cnt["MM"])
        lgu = dma(tg, s_wgub[:, 0:8 * 1536].rearrange("p (c f) -> p c f", f=1536),
                  wsgu.rearrange("(c p) f -> p c f", p=128))
        lsd = dma(tg, s_wsdb[:, 0:6 * 1024].rearrange("p (c f) -> p c f", f=1024),
                  wsd.rearrange("(c p) f -> p c f", p=128))
        lrw = dma(ty, s_f32a[:, 0:512].rearrange("p (c e) -> p c e", e=E),
                  rwt.rearrange("(c p) e -> p c e", p=128))
        w(tt, "LD", lrw)
        w(tt, "VE", t_ln1)
        for tti in range(NTT):
            pn = "pU" if tti % 2 == 0 else "pV"
            ps = P[pn][:, 0:E]
            for dc in range(8):
                mm(pn, ps, s_xmid[:, dc * TLOC + tti * 128:dc * TLOC + (tti + 1) * 128],
                   s_f32a[:, dc * E:(dc + 1) * E], start=(dc == 0), stop=(dc == 7))
            tk = cnt["MM"]
            w(tv, "MM", tk)
            evict_v(pn, tv.tensor_copy, out=s_probs[:, tti * E:(tti + 1) * E], in_=ps)
        t_logits = cnt["VE"]

        pr3 = s_probs[:].rearrange("p (t e) -> p t e", e=E)
        vop(tv.tensor_reduce, out=s_f32a[:, 0:NTT], in_=pr3, axis=AX, op=ALU.max)
        mx3 = s_f32a[:, 0:NTT].unsqueeze(2).broadcast_to([128, NTT, E])
        vop(tv.tensor_tensor, out=pr3, in0=pr3, in1=mx3, op=ALU.subtract)
        t_sm1 = cnt["VE"]
        w(ts, "VE", t_sm1)
        t_expp = sop(ts.activation, out=s_probs[:], in_=s_probs[:], func=AF.Exp)
        w(tv, "SC", t_expp)
        vop(tv.tensor_reduce, out=s_f32a[:, 0:NTT], in_=pr3, axis=AX, op=ALU.add)
        vop(tv.reciprocal, out=s_f32a[:, 0:NTT], in_=s_f32a[:, 0:NTT])
        vop(tv.tensor_tensor, out=pr3, in0=pr3, in1=mx3, op=ALU.mult)
        t_sm = cnt["VE"]

        hf3 = s_f32b[:, 0:NTT * E].rearrange("p (t e) -> p t e", e=E)
        vop(tv.memset, s_f32b[:], 0.0)
        w83 = s_w8[:].rearrange("p (t r) -> p t r", r=8)
        e83 = s_e8[:].rearrange("p (t r) -> p t r", r=8)
        for r in range(TOPK):
            vop(tv.tensor_reduce, out=w83[:, :, r], in_=pr3, axis=AX, op=ALU.max)
            wb3 = w83[:, :, r:r + 1].broadcast_to([128, NTT, E])
            eq3_ = s_cprev[:].rearrange("p (t e) -> p t e", e=E)
            vop(tv.tensor_tensor, out=eq3_, in0=pr3, in1=wb3, op=ALU.is_ge)
            vop(tv.tensor_tensor, out=hf3, in0=hf3, in1=eq3_, op=ALU.max)
            io3 = s_iotae[:].unsqueeze(1).broadcast_to([128, NTT, E])
            vop(tv.tensor_tensor, out=eq3_, in0=eq3_, in1=io3, op=ALU.mult)
            vop(tv.tensor_reduce, out=e83[:, :, r], in_=eq3_, axis=AX, op=ALU.max)
            vop(tv.tensor_tensor, out=eq3_, in0=pr3, in1=wb3, op=ALU.is_ge)
            vop(tv.scalar_tensor_tensor, out=pr3, in0=eq3_, scalar=-2.0,
                in1=pr3, op0=ALU.mult, op1=ALU.add)
        vop(tv.tensor_copy, out=s_h[:], in_=s_f32b[:, 0:NTT * E])
        t_top8 = cnt["VE"]

        vop(tv.tensor_reduce, out=s_f32b[:, 0:NTT], in_=w83, axis=AX, op=ALU.add)
        vop(tv.reciprocal, out=s_f32b[:, 0:NTT], in_=s_f32b[:, 0:NTT])
        sb3 = s_f32b[:, 0:NTT].unsqueeze(2).broadcast_to([128, NTT, 8])
        wn3 = s_wn8[:].rearrange("p (t r) -> p t r", r=8)
        vop(tv.tensor_tensor, out=wn3, in0=w83, in1=sb3, op=ALU.mult)
        t_wn = cnt["VE"]

        vop(tv.memset, s_f32b[:, 512:513], 0.0)
        t_base0 = cnt["VE"]
        w(tt, "VE", max(t_top8, t_base0))
        t_gtab_st_list = []
        for tti in range(NTT):
            h_t = s_h[:, tti * E:(tti + 1) * E]
            pnc = "pU" if tti % 2 == 0 else "pV"
            psc = P[pnc][0:64, 0:129]
            w(tv, "ST", cnt["ST"])
            mm(pnc, psc, h_t, s_trib[:])
            tk1 = cnt["MM"]
            pnd = "pS" if tti % 2 == 0 else "pT"
            psd = P[pnd][:, 0:E]
            mm(pnd, psd, s_trib[:, 0:128], h_t)
            tk2 = cnt["MM"]
            w(tv, "MM", tk1)
            bb3 = s_f32b[0:64, 512:513].unsqueeze(2).broadcast_to([64, 1, 129])
            cem3 = s_cem[:].rearrange("p (a c) -> p a c", a=1)
            ps3 = psc.rearrange("p (a c) -> p a c", a=1)
            evict_v(pnc, tv.tensor_tensor, out=cem3, in0=ps3, in1=bb3, op=ALU.add)
            w(tv, "MM", tk2)
            evict_v(pnd, tv.tensor_copy, out=s_cprev[:, tti * E:(tti + 1) * E], in_=psd)
            vop(tv.tensor_copy, out=s_gcol[:, tti:tti + 1], in_=s_cem[0:64, 0:1])
            tb = vop(tv.tensor_add, out=s_f32b[0:64, 512:513],
                     in0=s_f32b[0:64, 512:513], in1=psc[0:64, 128:129])
            bank_free[pnc] = ("VE", tb)
            tkc = cnt["VE"]
            w(ty, "VE", tkc)
            dma(ty, d_gtab_in[tti * E:(tti + 1) * E, :], s_cem[0:64, 0:128], sem="ST")
            _wst(ty)
        t_cnts = cnt["VE"]
        t_gtab_st = cnt["ST"]
        w(ty, "VE", t_cnts)
        dma(ty, d_cnt_in[0:1, :].rearrange("a p -> p a"), s_f32b[0:64, 512:513], sem="ST")
        t_cnt_st = cnt["ST"]
        t_cc_cnt = allgather(d_cnt_in[:, :], d_cnt_out[:, :], t_cnt_st)
        t_cc_gt = allgather(d_gtab_in[:, :], d_gtab_out[:, :], t_gtab_st)

        mark("P4")
        w(tt, "GL", lgu)
        w(tt, "VE", t_xmb)
        t_silu_free = 0
        for tca in range(4):
            for ft in range(6):
                png = "pA" if ft % 2 == 0 else "pB"
                psg = P[png][:, 0:512]
                pnu = "pC" if ft % 2 == 0 else "pD"
                psu = P[pnu][:, 0:512]
                for dc in range(8):
                    mm(png, psg, s_wgub[:, dc * 1536 + ft * 128:dc * 1536 + ft * 128 + 128],
                       s_xT[:, dc * TLOC + tca * 512:dc * TLOC + (tca + 1) * 512],
                       start=(dc == 0), stop=(dc == 7))
                tkg = cnt["MM"]
                for dc in range(8):
                    mm(pnu, psu,
                       s_wgub[:, dc * 1536 + 768 + ft * 128:dc * 1536 + 768 + ft * 128 + 128],
                       s_xT[:, dc * TLOC + tca * 512:dc * TLOC + (tca + 1) * 512],
                       start=(dc == 0), stop=(dc == 7))
                tku = cnt["MM"]
                w(ts, "MM", tkg)
                w(ts, "VE", t_silu_free)
                tsg = evict_s(png, ts.activation, out=s_f32a[:, 0:512],
                              in_=psg, func=AF.Silu)
                w(tv, "SC", tsg)
                w(tv, "MM", tku)
                t_silu_free = evict_v(
                    pnu, tv.tensor_tensor,
                    out=s_hsh[:, ft * 512:(ft + 1) * 512],
                    in0=s_f32a[:, 0:512], in1=psu, op=ALU.mult)
            t_hsh = cnt["VE"]
            w(tt, "GL", lsd)
            w(tt, "VE", t_hsh)
            for dt_ in range(8):
                pn = "pA" if dt_ % 2 == 0 else "pB"
                ps = P[pn][:, 0:512]
                for ft in range(6):
                    mm(pn, ps, s_wsdb[:, ft * 1024 + dt_ * 128:ft * 1024 + dt_ * 128 + 128],
                       s_hsh[:, ft * 512:(ft + 1) * 512],
                       start=(ft == 0), stop=(ft == 5))
                tk = cnt["MM"]
                w(tv, "MM", tk)
                sl = slice(dt_ * TLOC + tca * 512, dt_ * TLOC + (tca + 1) * 512)
                evict_v(pn, tv.tensor_tensor, out=s_xmid[:, sl], in0=s_xmid[:, sl],
                        in1=ps, op=ALU.add)
            w(tt, "VE", cnt["VE"])
        t_acc_sh = cnt["VE"]

        mark("P5a")
        w(ty, "CC", t_cc_cnt)
        lcnt = dma(ty, s_cnt_all[:], d_cnt_out[:, :])
        w(tt, "LD", lcnt)
        tmsel = trans("pU", P["pU"][0:8, 0:128], s_msel[:, 0:8], s_id32[:])
        w(tv, "MM", tmsel)
        tmc = evict_v("pU", tv.tensor_copy, out=s_f32a[0:8, 0:128], in_=P["pU"][0:8, 0:128])
        w(tt, "VE", tmc)
        mm("pV", P["pV"][:, 0:E], s_f32a[0:8, 0:128], s_cnt_all[0:8, :])
        tkb = cnt["MM"]
        w(tv, "MM", tkb)
        evict_v("pV", tv.tensor_copy, out=s_base[:], in_=P["pV"][:, 0:E])
        mm("pU", P["pU"][0:64, 0:1], s_cnt_all[0:8, :], s_f32a[0:8, 0:1])
        tkbe = cnt["MM"]
        w(tv, "MM", tkbe)
        evict_v("pU", tv.tensor_copy, out=s_bgl[:], in_=P["pU"][0:64, 0:1])
        t_bases = cnt["VE"]

        vop(tv.memset, s_f32b[:, 0:E], 0.0)
        vop(tv.memset, s_f32a[:, 512:640], 1.0)
        t_ones = cnt["VE"]
        w(tt, "VE", t_ones)
        for tti in range(NTT):
            sl = slice(tti * E, (tti + 1) * E)
            vop(tv.tensor_add, out=s_cprev[:, sl], in0=s_cprev[:, sl], in1=s_f32b[:, 0:E])
            vop(tv.tensor_add, out=s_cprev[:, sl], in0=s_cprev[:, sl], in1=s_base[:])
            pn = "pU" if tti % 2 == 0 else "pV"
            ps = P[pn][:, 0:E]
            mm(pn, ps, s_ones128b[:], s_h[:, sl])
            tk = cnt["MM"]
            w(tv, "MM", tk)
            evict_v(pn, tv.tensor_tensor, out=s_f32b[:, 0:E], in0=s_f32b[:, 0:E],
                    in1=ps, op=ALU.add)
        t_cglob = cnt["VE"]

        for r in range(TOPK):
            e8c3 = e83[:, :, r:r + 1].broadcast_to([128, NTT, E])
            io3 = s_iotae[:].unsqueeze(1).broadcast_to([128, NTT, E])
            eqw = s_f32a[:, 0:NTT * E].rearrange("p (t e) -> p t e", e=E)
            vop(tv.tensor_tensor, out=eqw, in0=io3, in1=e8c3, op=ALU.is_equal)
            gio3 = s_giotae[:].unsqueeze(1).broadcast_to([128, NTT, E])
            gs3 = s_probs[:].rearrange("p (t e) -> p t e", e=E)
            g83 = s_g8[:].rearrange("p (t r) -> p t r", r=8)
            vop(tv.tensor_tensor, out=gs3, in0=eqw, in1=gio3, op=ALU.mult)
            vop(tv.tensor_reduce, out=g83[:, :, r], in_=gs3, axis=AX, op=ALU.max)
            cg3 = s_cprev[:].rearrange("p (t e) -> p t e", e=E)
            vop(tv.tensor_tensor, out=eqw, in0=eqw, in1=cg3, op=ALU.mult)
            p83 = s_pos8[:].rearrange("p (t r) -> p t r", r=8)
            vop(tv.tensor_reduce, out=p83[:, :, r], in_=eqw, axis=AX, op=ALU.add)
        vop(tv.tensor_scalar, out=s_val8[:], in0=s_pos8[:], scalar1=float(CAP),
            scalar2=None, op0=ALU.is_lt)
        vop(tv.tensor_mul, out=s_wn8[:], in0=s_wn8[:], in1=s_val8[:])
        vop(tv.tensor_scalar, out=s_slot8[:], in0=s_e8[:], scalar1=float(CAP),
            scalar2=None, op0=ALU.mult)
        vop(tv.tensor_add, out=s_slot8[:], in0=s_slot8[:], in1=s_pos8[:])
        vop(tv.tensor_mul, out=s_slot8[:], in0=s_slot8[:], in1=s_val8[:])
        v83 = s_val8[:].rearrange("p (t r) -> p t r", r=8)
        vop(tv.tensor_reduce, out=s_ninv[:], in_=v83, axis=AX, op=ALU.add)
        vop(tv.tensor_scalar, out=s_ninv[:], in0=s_ninv[:], scalar1=-1.0,
            scalar2=8.0, op0=ALU.mult, op1=ALU.add)
        vop(tv.tensor_copy, out=s_ninvb[:], in_=s_ninv[:])
        A_ = s_f32a[:, 0:128]
        vop(tv.tensor_scalar, out=A_, in0=s_g8[:], scalar1=float(CAP),
            scalar2=None, op0=ALU.mult)
        vop(tv.tensor_add, out=A_, in0=A_, in1=s_pos8[:])
        vop(tv.tensor_mul, out=A_, in0=A_, in1=s_val8[:])
        vop(tv.tensor_copy, out=s_slot8i[:], in_=A_)
        t_slot = cnt["VE"]

        for tti in range(NTT):
            w(tv, "ST", cnt["ST"])
            vop(tv.memset, s_f32b[:, 64:128], 0.0)
            vop(tv.tensor_copy, out=s_f32b[:, 64:72], in_=s_e8[:, tti * 8:(tti + 1) * 8])
            vop(tv.tensor_copy, out=s_f32b[:, 72:80], in_=s_wn8[:, tti * 8:(tti + 1) * 8])
            tkr = cnt["VE"]
            w(ty, "VE", tkr)
            dma(ty, d_rtab_in[tti * 128:(tti + 1) * 128, :], s_f32b[:, 64:128], sem="ST")
            _wst(ty)
        t_rtab_st = cnt["ST"]
        t_cc_rt = allgather(d_rtab_in[:, :], d_rtab_out[:, :], t_rtab_st)

        w(tv, "VE", t_bases)
        bg3 = s_bgl[:].unsqueeze(2).broadcast_to([64, 1, NTT])
        gc3 = s_gcol[:].unsqueeze(1)
        vop(tv.tensor_tensor, out=gc3, in0=gc3, in1=bg3, op=ALU.add)
        vop(tv.memset, s_f32b[:, 128:192], 0.0)
        vop(tv.tensor_copy, out=s_f32b[0:64, 128:144], in_=s_gcol[:])
        vop(tv.tensor_copy, out=s_f32b[0:64, 144:145], in_=s_f32b[0:64, 512:513])
        t_gst = cnt["VE"]
        w(ty, "VE", t_gst)
        dma(ty, d_gst_in[:, :], s_f32b[0:64, 128:192], sem="ST")
        t_gst_st = cnt["ST"]
        t_cc_gst = allgather(d_gst_in[:, :], d_gst_out[:, :], t_gst_st)

        w(ty, "VE", t_slot)
        dma(ty, d_scri[:, 0:128], s_slot8i[:], sem="ST")
        _wst(ty)
        for g_ in range(8):
            dma(ty, s_yidx[0:16, :].rearrange("q (f gg) -> q f gg", gg=8)[:, :, g_],
                d_scri[16 * g_:16 * g_ + 16, 0:128])
        _wld(ty)
        dma(ty, s_yidx[16:32, :], s_yidx[0:16, :])
        t_yidx = cnt["LD"]
        dma(ty, d_scri[:, 128:144], s_ninvb[:].bitcast(I16), sem="ST")
        _wst(ty)
        dma(ty, s_ninvrow[0:1, :].bitcast(I16).rearrange("a (t p) -> a t p", t=NTT),
            d_scri[:, 128:144].rearrange("(a p) t -> a t p", a=1))
        t_ninvr = cnt["LD"]

        mark("P5b")
        join_all()
        st_p4.close()
        st_ffn1.close()
        st_xT.close()
        st_p6 = ExitStack()
        s_grow = sbuf("s_grow", (128, 32 * 128), stack=st_p6)
        s_rt = sbuf("s_rt", (128, 32 * RTW), stack=st_p6)
        s_gstart = sbuf("s_gstart", (128, 128), stack=st_p6)

        w(tg, "CC", t_cc_gst)
        w(tg, "LD", t_const)
        tgk = gather(s_grow[:, 0:512].rearrange("p (n d) -> p n d", d=64),
                     d_gst_out[:, :], s_gsidx[:], 1024)
        w(tv, "GP", tgk)
        gat3 = s_grow[:, 0:512].rearrange("p (n d) -> p n d", d=64)
        gst2 = s_gstart[:].rearrange("p (n k) -> p n k", k=16)
        vop(tv.tensor_copy, out=gst2, in_=gat3[:, :, 0:16])
        vop(tv.tensor_reduce, out=s_totc[:], in_=gat3[:, :, 16:17].rearrange(
            "p n a -> p (n a)"), axis=AX, op=ALU.add)
        vop(tv.tensor_copy, out=s_f32b[:, 200:208], in_=gat3[:, :, 16])
        t_gstart = cnt["VE"]

        vop(tv.memset, s_kstar[:], 0.0)
        for k in range(128):
            vop(tv.scalar_tensor_tensor, out=s_kstar[:], in0=s_ctens[:],
                scalar=s_gstart[:, k:k + 1], in1=s_kstar[:], op0=ALU.is_ge, op1=ALU.add)
        vop(tv.tensor_scalar, out=s_kstar[:], in0=s_kstar[:], scalar1=-1.0,
            scalar2=None, op0=ALU.add)
        vop(tv.tensor_scalar, out=s_f32a[:, 0:32], in0=s_kstar[:], scalar1=64.0,
            scalar2=None, op0=ALU.mult)
        vop(tv.tensor_scalar, out=s_f32a[:, 0:32], in0=s_f32a[:, 0:32],
            scalar1=s_ecol[:, 0:1], scalar2=None, op0=ALU.add)
        tgidxf = vop(tv.tensor_copy, out=s_kstari[:], in_=s_f32a[:, 0:32])
        w(ty, "VE", tgidxf)
        dma(ty, d_scri[:, 0:32], s_kstari[:], sem="ST")
        _wst(ty)
        for g_ in range(8):
            dma(ty, s_gidx[0:16, 0:256].rearrange("q (b gg) -> q b gg", gg=8)[:, :, g_],
                d_scri[16 * g_:16 * g_ + 16, 0:32])
        _wld(ty)
        dma(ty, s_gidx[16:32, :], s_gidx[0:16, :])
        t_gidx = cnt["LD"]
        w(tg, "CC", t_cc_gt)
        w(tg, "LD", t_gidx)
        w(tg, "VE", cnt["VE"])
        for cq in range(4):
            gather(s_grow[:, cq * 1024:(cq + 1) * 1024].rearrange(
                "p (n d) -> p n d", d=128),
                d_gtab_out[:, :], s_gidx[:, cq * 64:(cq + 1) * 64], 1024)
        t_growg = cnt["GP"]
        w(tv, "GP", t_growg)
        vop(tv.tensor_copy, out=s_f32a[:, 0:32], in_=s_ctens[:])
        for j in range(7):
            vop(tv.tensor_scalar, out=s_f32a[:, 32:64], in0=s_kstar[:],
                scalar1=float(16 * (j + 1)), scalar2=None, op0=ALU.is_ge)
            vop(tv.tensor_scalar, out=s_f32a[:, 32:64], in0=s_f32a[:, 32:64],
                scalar1=s_f32b[:, 200 + j:201 + j], scalar2=None, op0=ALU.mult)
            vop(tv.tensor_sub, out=s_f32a[:, 0:32], in0=s_f32a[:, 0:32],
                in1=s_f32a[:, 32:64])
        gr3 = s_grow[:].rearrange("p (b j) -> p b j", j=128)
        c3 = s_f32a[:, 0:32].unsqueeze(2).broadcast_to([128, 32, 128])
        vop(tv.tensor_tensor, out=gr3, in0=gr3, in1=c3, op=ALU.is_le)
        vop(tv.tensor_reduce, out=s_stok[:], in_=gr3, axis=AX, op=ALU.add)
        vop(tv.tensor_scalar, out=s_f32a[:, 0:32], in0=s_kstar[:], scalar1=128.0,
            scalar2=None, op0=ALU.mult)
        vop(tv.tensor_add, out=s_stok[:], in0=s_stok[:], in1=s_f32a[:, 0:32])
        vop(tv.tensor_scalar, out=s_stok[:], in0=s_stok[:], scalar1=-1.0,
            scalar2=None, op0=ALU.add)
        tstok = vop(tv.tensor_copy, out=s_stoki[:], in_=s_stok[:])
        w(ty, "VE", tstok)
        dma(ty, d_scri[:, 0:32], s_stoki[:], sem="ST")
        _wst(ty)
        for bh in range(2):
            dma(ty, s_sidx[0:16, :].rearrange("q (el cb bb) -> q el cb bb",
                                              el=8, bb=2)[:, :, :, bh],
                d_scri[:, 0:32].rearrange("(el cb) (bb q) -> q el cb bb",
                                          el=8, bb=2)[:, :, :, bh])
        for g_ in range(8):
            dma(ty, s_sidx2[0:16, :].rearrange("q (b gg) -> q b gg", gg=8)[:, :, g_],
                d_scri[16 * g_:16 * g_ + 16, 0:32])
        _wld(ty)
        dma(ty, s_sidx[16:32, :], s_sidx[0:16, :])
        dma(ty, s_sidx2[16:32, :], s_sidx2[0:16, :])
        t_sidx = cnt["LD"]

        w(tg, "CC", t_cc_rt)
        w(tg, "LD", t_sidx)
        for cq in range(4):
            gather(s_rt[:, cq * 512:(cq + 1) * 512].rearrange("p (n d) -> p n d", d=RTW),
                   d_rtab_out[:, :], s_sidx2[:, cq * 64:(cq + 1) * 64], 1024)
        t_rtg = cnt["GP"]
        w(tv, "GP", t_rtg)
        rt3 = s_rt[:].rearrange("p (b d) -> p b d", d=RTW)
        wetmp = s_f32a[:, 0:256].rearrange("p (b r) -> p b r", r=8)
        vop(tv.tensor_scalar, out=wetmp, in0=rt3[:, :, 0:8],
            scalar1=s_ecol[:, 0:1], scalar2=None, op0=ALU.is_equal)
        vop(tv.tensor_tensor, out=wetmp, in0=wetmp, in1=rt3[:, :, 8:16], op=ALU.mult)
        vop(tv.tensor_reduce, out=s_we[:], in_=wetmp, axis=AX, op=ALU.add)
        vop(tv.tensor_scalar, out=s_f32a[:, 0:32], in0=s_ctens[:],
            scalar1=s_totc[:, 0:1], scalar2=None, op0=ALU.is_lt)
        vop(tv.tensor_mul, out=s_we[:], in0=s_we[:], in1=s_f32a[:, 0:32])
        twe = cnt["VE"]
        w(ty, "VE", twe)
        dma(ty, d_scrf[:, 0:32], s_we[:], sem="ST")
        _wst(ty)
        for gg in range(4):
            dma(ty, s_wec[gg * 32:(gg + 1) * 32, :].rearrange("q (el cc) -> q el cc", el=8),
                d_scrf[:, 0:32].rearrange("(el cc g2) q -> g2 q el cc", el=8, cc=4)[gg])
        t_wec = cnt["LD"]

        mark("P6")
        join_all()
        st_p6.close()
        st_p7 = ExitStack()
        s_wex = [sbuf(f"s_wex{i}", (128, 2 * 8 * F + 6 * 1024), BF16, stack=st_p7)
                 for i in range(2)]
        s_xb = sbuf("s_xb", (128, 4 * D), BF16, stack=st_p7)
        s_xbT = sbuf("s_xbT", (128, 8 * 512), BF16, stack=st_p7)
        s_he = sbuf("s_he", (128, 6 * 512), BF16, stack=st_p7)
        s_ye = sbuf("s_ye", (128, 4 * D), BF16, stack=st_p7)

        def load_expert(el, buf):
            dma(tg, buf[:, 0:6144].rearrange("p (c f) -> p c f", f=F),
                weg[el].rearrange("(c p) f -> p c f", p=128))
            dma(tg, buf[:, 6144:12288].rearrange("p (c f) -> p c f", f=F),
                weu[el].rearrange("(c p) f -> p c f", p=128))
            return dma(tg, buf[:, 12288:18432].rearrange("p (c f) -> p c f", f=1024),
                       wed[el].rearrange("(c p) f -> p c f", p=128))

        t_wexl = [0, 0]
        t_silu_free7 = 0
        t_wexl[0] = load_expert(0, s_wex[0])
        t_ye_st = 0
        t_xb_mm = 0
        w(tg, "CC", t_cc_xm)
        w(tg, "LD", t_sidx)
        for el in range(EO):
            buf = s_wex[el % 2]
            w(tg, "MM", t_xb_mm)
            tgx = gather(s_xb[:].rearrange("p (n d) -> p n d", d=D),
                         d_xmid_out[:, :], s_sidx[:, el * 32:(el + 1) * 32], 512)
            w(tt, "GP", tgx)
            for blk in range(4):
                for dc in range(8):
                    pn = "pS" if (blk * 8 + dc) % 2 == 0 else "pT"
                    ps = P[pn][:, 0:64].bitcast(BF16)
                    trans(pn, ps, s_xb[:, blk * D + dc * 128:blk * D + (dc + 1) * 128],
                          s_id16[:])
                    tk = cnt["MM"]
                    w(tv, "MM", tk)
                    evict_v(pn, tv.tensor_copy,
                            out=s_xbT[:, dc * 512 + blk * 128:dc * 512 + (blk + 1) * 128],
                            in_=ps)
            t_xbT = cnt["VE"]
            t_xb_mm = cnt["MM"]
            w(tt, "GL", t_wexl[el % 2])
            w(tt, "VE", t_xbT)
            for ft in range(6):
                png = "pA" if ft % 2 == 0 else "pB"
                psg = P[png][:, 0:512]
                pnu = "pC" if ft % 2 == 0 else "pD"
                psu = P[pnu][:, 0:512]
                for dc in range(8):
                    mm(png, psg, buf[:, dc * F + ft * 128:dc * F + ft * 128 + 128],
                       s_xbT[:, dc * 512:(dc + 1) * 512],
                       start=(dc == 0), stop=(dc == 7))
                tkg = cnt["MM"]
                for dc in range(8):
                    mm(pnu, psu, buf[:, 6144 + dc * F + ft * 128:6144 + dc * F + ft * 128 + 128],
                       s_xbT[:, dc * 512:(dc + 1) * 512],
                       start=(dc == 0), stop=(dc == 7))
                tku = cnt["MM"]
                w(ts, "MM", tkg)
                w(ts, "VE", t_silu_free7)
                tsg = evict_s(png, ts.activation, out=s_f32a[:, 0:512],
                              in_=psg, func=AF.Silu)
                w(tv, "SC", tsg)
                w(tv, "MM", tku)
                t_silu_free7 = evict_v(
                    pnu, tv.tensor_tensor, out=s_he[:, ft * 512:(ft + 1) * 512],
                    in0=s_f32a[:, 0:512], in1=psu, op=ALU.mult)
            t_he = cnt["VE"]
            if el + 1 < EO:
                w(tg, "MM", cnt["MM"])
                t_wexl[(el + 1) % 2] = load_expert(el + 1, s_wex[(el + 1) % 2])
            w(tt, "VE", t_he)
            w(tt, "LD", t_wec)
            for ct in range(4):
                for hfd in range(2):
                    pn = "pU" if (ct * 2 + hfd) % 2 == 0 else "pV"
                    ps = P[pn][:, 0:512]
                    for ft in range(6):
                        mm(pn, ps, s_he[:, ft * 512 + ct * 128:ft * 512 + ct * 128 + 128],
                           buf[:, 12288 + ft * 1024 + hfd * 512:12288 + ft * 1024 + (hfd + 1) * 512],
                           start=(ft == 0), stop=(ft == 5))
                    tk = cnt["MM"]
                    w(tv, "MM", tk)
                    w(tv, "ST", t_ye_st)
                    evict_v(pn, tv.tensor_scalar,
                            out=s_ye[:, ct * D + hfd * 512:ct * D + (hfd + 1) * 512],
                            in0=ps, scalar1=s_wec[:, el * 4 + ct:el * 4 + ct + 1],
                            scalar2=None, op0=ALU.mult)
            t_ye = cnt["VE"]
            w(ty, "VE", t_ye)
            dma(ty, d_ybuf_in[el * CAP:(el + 1) * CAP, :].rearrange(
                "(b p) d -> p b d", p=128),
                s_ye[:].rearrange("p (b d) -> p b d", d=D), sem="ST")
            t_ye_st = cnt["ST"]
            if el % 2 == 1:
                q_ = el // 2
                t_cc_yb = allgather(d_ybuf_in[q_ * 1024:(q_ + 1) * 1024, :],
                                    d_ybuf_out[q_ * 8192:(q_ + 1) * 8192, :],
                                    cnt["ST"])

        mark("P7")
        join_all()
        st_p7.close()
        st_p8 = ExitStack()
        s_yg = sbuf("s_yg", (128, 2 * 8 * D), BF16, stack=st_p8)
        s_acc1 = sbuf("s_acc1", (128, 2 * 4 * D), BF16, stack=st_p8)

        w(ty, "CC", t_cc_yb)
        ly0 = dma(ty, s_y0[0:1, :], d_ybuf_out[0:1, :])
        t_y0 = cnt["LD"]

        t_ygread = [0, 0]
        t_trans = [0, 0]
        for tti in range(NTT):
            b_ = tti % 2
            yg = s_yg[:, b_ * 8 * D:(b_ + 1) * 8 * D]
            ac = s_acc1[:, b_ * 4 * D:(b_ + 1) * 4 * D]
            w(tg, "VE", t_ygread[b_])
            w(tg, "CC", t_cc_yb)
            w(tg, "LD", t_yidx)
            tgy = gather(yg.rearrange("p (n d) -> p n d", d=D),
                         d_ybuf_out[:, :], s_yidx[:, tti * 64:(tti + 1) * 64], 1024)
            w(tv, "GP", tgy)
            w(tv, "MM", t_trans[b_])
            for j in range(4):
                vop(tv.tensor_add, out=ac[:, j * D:(j + 1) * D],
                    in0=yg[:, (2 * j) * D:(2 * j + 1) * D],
                    in1=yg[:, (2 * j + 1) * D:(2 * j + 2) * D])
            t_ygread[b_] = cnt["VE"]
            for j in range(2):
                vop(tv.tensor_add, out=ac[:, j * D:(j + 1) * D],
                    in0=ac[:, (2 * j) * D:(2 * j + 1) * D],
                    in1=ac[:, (2 * j + 1) * D:(2 * j + 2) * D])
            vop(tv.tensor_add, out=ac[:, 0:D], in0=ac[:, 0:D],
                in1=ac[:, D:2 * D])
            t_sum = cnt["VE"]
            w(tt, "LD", max(t_y0, t_ninvr))
            pnc = "pU" if tti % 2 == 0 else "pV"
            mm(pnc, P[pnc][:, 0:512], s_ninvrow[0:1, tti * 128:(tti + 1) * 128],
               s_y0[0:1, 0:512])
            tk1 = cnt["MM"]
            pnd = "pS" if tti % 2 == 0 else "pT"
            mm(pnd, P[pnd][:, 0:512], s_ninvrow[0:1, tti * 128:(tti + 1) * 128],
               s_y0[0:1, 512:1024])
            tk2 = cnt["MM"]
            w(tv, "VE", t_sum)
            w(tv, "MM", tk2)
            evict_v(pnc, tv.tensor_tensor, out=ac[:, 0:512], in0=ac[:, 0:512],
                    in1=P[pnc][:, 0:512], op=ALU.subtract)
            evict_v(pnd, tv.tensor_tensor, out=ac[:, 512:1024], in0=ac[:, 512:1024],
                    in1=P[pnd][:, 0:512], op=ALU.subtract)
            t_rt_tile = cnt["VE"]
            for dc in range(8):
                pn = "pA" if dc % 2 == 0 else "pB"
                ps = P[pn][:, 0:64].bitcast(BF16)
                w(tt, "VE", t_rt_tile)
                trans(pn, ps, ac[:, dc * 128:(dc + 1) * 128], s_id16[:])
                tk = cnt["MM"]
                w(tv, "MM", tk)
                evict_v(pn, tv.tensor_tensor,
                        out=s_xmid[:, dc * TLOC + tti * 128:dc * TLOC + (tti + 1) * 128],
                        in0=s_xmid[:, dc * TLOC + tti * 128:dc * TLOC + (tti + 1) * 128],
                        in1=ps, op=ALU.add)
            t_trans[b_] = tk
        t_routed = cnt["VE"]

        t_ln2 = layernorm_inplace(t_routed)

        for tti in range(NTT):
            hb = (tti % 2) * D
            if tti >= 2:
                w(tv, "ST", cnt["ST"])
            for dc in range(8):
                pn = "pS" if dc % 2 == 0 else "pT"
                ps = P[pn][:, 0:128]
                w(tt, "VE", t_ln2)
                trans(pn, ps, s_xmid[:, dc * TLOC + tti * 128:dc * TLOC + (tti + 1) * 128],
                      s_id32[:])
                tk = cnt["MM"]
                w(tv, "MM", tk)
                evict_v(pn, tv.tensor_copy,
                        out=s_rowsc[:, hb + dc * 128:hb + (dc + 1) * 128], in_=ps)
            tke = cnt["VE"]
            w(ty, "VE", tke)
            dma(ty, out[tti * 128:(tti + 1) * 128, :], s_rowsc[:, hb:hb + D], sem="ST")
        _wst(ty)
        if DBG:
            join_all()
            dma(ty, dbg["xT32"], d_xT32[:, 0:TLOC], sem="ST")
            dma(ty, dbg["st"], d_st_in.rearrange("h a p -> (h a) p"), sem="ST")
            dma(ty, dbg["xmid"], d_xmid_in[0:256, :], sem="ST")
            dma(ty, dbg["cnt"], d_cnt_out[:, :], sem="ST")
            dma(ty, dbg["gtab"], d_gtab_in[0:2 * E, :], sem="ST")
            dma(ty, dbg["gst"], d_gst_in[:, :], sem="ST")
            dma(ty, dbg["rtab"], d_rtab_in[0:128, :], sem="ST")
            dma(ty, dbg["ybuf"], d_ybuf_in[0:128, :], sem="ST")
            dma(ty, dbg["fin"], s_xmid[:, 0:TLOC], sem="ST")
            dma(ty, dbg["slot"], s_slot8[:], sem="ST")
            dma(ty, dbg["stok"], s_stok[:], sem="ST")
            dma(ty, dbg["wec"], s_wec[:], sem="ST")
            dma(ty, dbg["w8"], s_w8[:], sem="ST")
            dma(ty, dbg["val8"], s_val8[:], sem="ST")
            dma(ty, dbg["wn8"], s_wn8[:], sem="ST")
            _wst(ty)
        mark("P8")
        st_p8.close()

        def _replay(rec):
            def f(engine):
                for mname, a, kw, inc, seq in rec.prog:
                    if cutoff is not None and seq > cutoff:
                        continue
                    r = getattr(engine, mname)(*a, **kw)
                    if inc is not None:
                        r.then_inc(*inc)
            return f

        block.sync(_replay(ty))
        block.tensor(_replay(tt))
        block.vector(_replay(tv))
        block.scalar(_replay(ts))
        block.gpsimd(_replay(tg))

    mybir.codegen_inst_isa_subclasses(nc)
    return nc


def kernel(**inputs):
    if "nc" not in _CACHE:
        _CACHE["nc"] = _build()
    nc = _CACHE["nc"]
    in_maps = [_host_inputs(c, inputs) for c in range(NCORES)]
    res = run_bass_kernel_spmd(nc, in_maps, core_ids=list(range(NCORES)))
    outs = [res.results[c]["out"] for c in range(NCORES)]
    full = np.concatenate(outs, axis=0).reshape(2, TGLB // 2, D)
    return full
